# version 16
# baseline (speedup 1.0000x reference)
import numpy as np
import ml_dtypes
import concourse.bass as bass
import concourse.mybir as mybir
import concourse.expressions_rust as er
from concourse.bass_utils import run_bass_kernel_spmd

F32 = mybir.dt.float32
BF16 = mybir.dt.bfloat16
I8 = mybir.dt.int8
U32 = mybir.dt.uint32
AF = mybir.ActivationFunctionType
ALU = mybir.AluOpType
AX = mybir.AxisListType.X
BF = ml_dtypes.bfloat16

B, S, E, H, HD = 4, 4096, 1024, 16, 64
TOWN, FR = 2048, 3072
NDC = E // 128
IFF, ICORE, NIC = 2816, 352, 22
DA = H * (HD + 1)
NTC = FR // 128
USABLE = 208832
RG = [list(range(8))]


class Led:
    def __init__(self):
        self.c = {}
        self.mark = {}

    def inc(self, instr, sem, k=None):
        if k is None:
            k = 16 if sem.name in ("qS", "qG") else 1
        instr.then_inc(sem, k)
        self.c[sem.name] = self.c.get(sem.name, 0) + k
        return self.c[sem.name]

    def at(self, sem):
        return self.c.get(sem.name, 0)

    def set(self, name, sem):
        self.mark[name] = self.at(sem)

    def wait(self, eng, sem, name):
        eng.wait_ge(sem, self.mark[name])

    def chain(self, eng, sem, instr):
        instr.then_inc(sem, 1)
        self.c[sem.name] = self.c.get(sem.name, 0) + 1
        eng.wait_ge(sem, self.c[sem.name])


def build(stage=99, tap=None):
    nc = bass.Bass("TRN2", target_bir_lowering=False)
    L = Led()

    def din(name, shape, dt=F32):
        return nc.dram_tensor(name, shape, dt, kind="ExternalInput").ap()

    xTd = din("xT", [E, FR])
    wqTd = din("wqT", [E, E])
    wkTd = din("wkT", [E, E])
    wvTd = din("wvTa", [E, DA])
    woTd = din("woT", [E, E])
    bq2d = din("bq2", [128, NDC])
    bk2d = din("bk2", [128, NDC])
    bo2d = din("bo2", [128, NDC])
    bvad = din("bva", [1, DA])
    cTd = din("cTd", [128, NDC * B])
    wmodsd = din("wmods", [E, 768])
    clipxd = din("clipx", [1, 1])
    clipdd = din("clipd", [1, 1])
    wgNd = din("wgN", [ICORE, E])
    wgTd = din("wgTs", [E, ICORE])
    wuNd = din("wuN", [ICORE, E])
    wuTd = din("wuTs", [E, ICORE])
    wdNd = din("wdN", [128, IFF])
    padsd = din("padsi", [1, 8], U32)
    padvd = din("padvd", [128, DA], BF16)
    m0d = din("m0d", [128, 2048], BF16)
    m3d = din("m3d", [128, 2048], BF16)
    t1d = din("t1d", [128, 128], BF16)
    t2d = din("t2d", [128, 128], BF16)
    idnBd = din("idnB", [128, 128], BF16)
    idnFd = din("idnF", [128, 128], F32)
    lnod = din("lnod", [128, 128], BF16)
    onesRFd = din("onesRFd", [1, 128], F32)

    out = nc.dram_tensor("out", [E, TOWN], F32, kind="ExternalOutput").ap()
    dbg = None
    if tap is not None:
        dbg = nc.dram_tensor("dbg", list(tap[1]), F32, kind="ExternalOutput").ap()

    dt_ = nc.dram_tensor
    agmod_i = dt_("agmod_i", [B * 768], F32)
    agmod_o = dt_("agmod_o", [8 * B * 768], F32)
    modrow_d = dt_("modrow_d", [6 * E], F32)
    agwg_i = dt_("agwg_i", [E, ICORE], BF16)
    agwg_o = dt_("agwg_o", [8 * E * ICORE], BF16)
    agwu_i = dt_("agwu_i", [E, ICORE], BF16)
    agwu_o = dt_("agwu_o", [8 * E * ICORE], BF16)
    agwd_i = dt_("agwd_i", [IFF, 128], BF16)
    agwd_o = dt_("agwd_o", [8 * IFF * 128], BF16)
    agsc_i = dt_("agsc_i", [832], F32)
    agsc_o = dt_("agsc_o", [8 * 832], F32)
    scflat = dt_("scflat", [2 * IFF + E], F32)
    agm1_i = dt_("agm1_i", [128], F32)
    agm1_o = dt_("agm1_o", [8 * 128], F32)
    agm2_i = dt_("agm2_i", [128], F32)
    agm2_o = dt_("agm2_o", [8 * 128], F32)
    xnew_d = dt_("xnew_d", [E, TOWN], F32)
    atm_d = dt_("atm_d", [16, 128, E], BF16)
    qhd = dt_("qhd", [E, TOWN], BF16)
    qld = dt_("qld", [E, TOWN], BF16)
    khd = dt_("khd", [E, FR], BF16)
    kld = dt_("kld", [E, FR], BF16)

    backing = nc.alloc_sbuf_tensor("arena", [128, USABLE], mybir.dt.uint8)
    base_addr = nc.lookup_mloc(backing).addr

    def place(name, shape, dtype, off):
        nbytes = int(np.prod(shape[1:])) * mybir.dt.size(dtype)
        assert off % 32 == 0 and off + nbytes <= USABLE, (name, off, nbytes)
        return nc.alloc_sbuf_tensor_at(name, list(shape), dtype,
                                       offset=base_addr + off)

    o = [0]

    def cplace(name, shape, dtype):
        nbytes = int(np.prod(shape[1:])) * mybir.dt.size(dtype)
        h = place(name, shape, dtype, o[0])
        o[0] = (o[0] + nbytes + 31) // 32 * 32
        assert o[0] <= 11456, (name, o[0])
        return h

    c_bq = cplace("c_bq", [128, NDC], F32)
    c_bk = cplace("c_bk", [128, NDC], F32)
    c_bo = cplace("c_bo", [128, NDC], F32)
    c_eps = cplace("c_eps", [128, 1], F32)
    c_idB = cplace("c_idB", [128, 128], BF16)
    c_idF = cplace("c_idF", [128, 128], F32)
    c_ln1 = cplace("c_ln1", [128, 128], BF16)
    c_1rf = cplace("c_1rf", [1, 128], F32)
    c_1rb = cplace("c_1rb", [1, 128], BF16)
    c_t1 = cplace("c_t1", [128, 128], BF16)
    c_t2 = cplace("c_t2", [128, 128], BF16)
    c_bva = cplace("c_bva", [1, DA], BF16)
    c_pads = cplace("c_pads", [1, 8], U32)
    c_clip = cplace("c_clip", [1, 2], F32)
    c_sg = cplace("c_sg", [128, NIC], F32)
    c_su = cplace("c_su", [128, NIC], F32)
    c_sd = cplace("c_sd", [128, NDC], F32)
    c_sgc = cplace("c_sgc", [128, NIC], F32)
    c_suc = cplace("c_suc", [128, NIC], F32)
    c_sdc = cplace("c_sdc", [128, NDC], F32)
    c_inv1 = cplace("c_inv1", [128, 1], F32)
    c_inv2 = cplace("c_inv2", [128, 1], F32)
    c_s1r = cplace("c_s1r", [128, 1], F32)
    c_s2r = cplace("c_s2r", [128, 1], F32)
    c_sc = cplace("c_sc", [1, 16], F32)
    c_mod = cplace("c_mod", [128, 48], F32)
    c_ga = cplace("c_ga", [128, 8], F32)
    c_gb = cplace("c_gb", [128, 8], F32)
    c_g1 = cplace("c_g1", [128, 8], F32)
    c_g2 = cplace("c_g2", [128, 8], F32)
    c_csl = cplace("c_csl", [128, NDC * B], F32)
    c_csi = cplace("c_csi", [128, NDC * B], F32)
    c_mx = cplace("c_mx", [128, 1], F32)
    c_mxt = cplace("c_mxt", [128, 1], F32)
    c_zr = cplace("c_zr", [128, 1], F32)
    c_zr4 = cplace("c_zr4", [128, 4], F32)
    c_zcp = cplace("c_zcp", [128, 4], F32)
    c_negC = cplace("c_negC", [128, 1], F32)
    c_q25 = cplace("c_q25", [128, 1], F32)
    c_padv = cplace("c_padv", [128, DA], BF16)
    c_km1 = cplace("c_km1", [128, 512], BF16)
    c_k0 = cplace("c_k0", [128, 512], BF16)
    c_127 = cplace("c_127", [128, 1], F32)
    c_r127 = cplace("c_r127", [128, 1], F32)
    c_sgn = cplace("c_sgn", [128, 8], F32)
    c_mag = cplace("c_mag", [128, 8], F32)
    c_mag2 = cplace("c_mag2", [128, 8], F32)
    c_mag3 = cplace("c_mag3", [128, 8], F32)

    R1, R2, R3, R4, R5 = 11456, 44224, 76992, 109760, 158912
    xt = [place("xt0", [128, NDC * 512], F32, R1),
          place("xt1", [128, NDC * 512], F32, R1 + 16384)]
    kt2 = [place(f"kt2_{i}", [128, 2048], BF16, R1 + 4096 * i) for i in range(2)]
    qt2 = [place(f"qt2_{i}", [128, 1024], BF16, R1 + 8192 + 2048 * i) for i in range(2)]
    pstf = [place(f"pstf{i}", [128, 512], F32, R1 + 12288 + 2048 * i) for i in range(2)]
    psth = [place(f"psth{i}", [128, 512], BF16, R1 + 16384 + 1024 * i) for i in range(2)]
    pstl = [place(f"pstl{i}", [128, 512], BF16, R1 + 18432 + 1024 * i) for i in range(2)]
    vtmp = place("vtmp", [128, 2048], F32, R1)
    q8t = place("q8t", [128, 2048], I8, R1 + 8192)
    wgt = [place(f"wgt{i}", [128, NDC * 128], BF16, R2 + 2048 * i) for i in range(2)]
    wut = [place(f"wut{i}", [128, NDC * 128], BF16, R2 + 4096 + 2048 * i) for i in range(2)]
    wdt = [place(f"wdt{i}", [128, NIC * 128], BF16, R1 + 12288 + 5632 * i) for i in range(2)]
    sc22 = place("sc22", [22, 128], F32, R1 + 28672)
    sc8r = place("sc8r", [8, 128], F32, R1 + 29248)

    xb_t = place("xb_t", [128, NDC * 512], BF16, R2)
    xq_t = place("xq_t", [128, NDC * 512], BF16, R2 + 8192)
    muS = place("muS", [128, 512], F32, R2 + 16384)
    mu2 = place("mu2", [128, 512], F32, R2 + 18432)
    varS = place("varS", [128, 512], F32, R2 + 20480)
    lnvS = place("lnvS", [128, 512], F32, R2 + 22528)
    rstdS = place("rstdS", [128, 512], F32, R2 + 24576)
    mrtS = place("mrtS", [128, 512], F32, R2 + 26624)
    tmpA = place("tmpA", [128, 512], F32, R2 + 28672)
    tmpB = place("tmpB", [128, 512], F32, R2 + 30720)
    wtl = [place(f"wtl{i}", [128, 1024], F32, R2 + 4096 * i) for i in range(2)]
    wqbs = [place(f"wqbs{i}", [128, NDC * 128], BF16, R2 + 8192 + 2048 * i) for i in range(2)]
    silq = place("silq", [128, 512], F32, R2 + 16384)
    silq2 = place("silq2", [128, 512], F32, R2 + 18432)
    xno = [place(f"xno{i}", [128, 512], F32, R2 + 20480 + 2048 * i) for i in range(2)]
    outt = [place(f"outt{i}", [128, 512], F32, R2 + 24576 + 2048 * i) for i in range(2)]
    evt = place("evt", [128, 512], F32, R2 + 28672)
    tld = [place(f"tld{i}", [128, 128], BF16, R2 + 4096 * i) for i in range(2)]

    sc48 = place("sc48", [48, 128], F32, R3)
    wvb = place("wvb", [128, NDC * DA], BF16, R3)
    wob = place("wob", [128, NDC * 1024], BF16, R3)
    a_cx = place("a_cx", [128, NDC * TOWN], BF16, R3)

    a_h1 = place("a_h1", [128, NDC * FR], BF16, R4)
    aE = [place("aE0", [128, 12 * 512], BF16, R4),
          place("aE1", [128, 12 * 512], BF16, R4 + 12288)]
    aEm = place("aEm", [128, 3072], BF16, R4 + 24576)
    c_mm = place("c_mm", [128, 2048], BF16, R4 + 30720)
    tstg = [place(f"tstg{i}", [128, 128], BF16, R4 + 34816 + 256 * i) for i in range(4)]
    a_afm = place("a_afm", [128, NDC * TOWN], BF16, R4)
    a_h2 = place("a_h2", [128, NDC * TOWN], F32, R4)
    a_hr = place("a_hr", [128, NIC * TOWN], BF16, R4)
    wpf = place("wpf", [128, 3072], F32, R4)
    wps = place("wps", [128, IFF], F32, R4 + 12288)
    wp8 = place("wp8", [128, IFF], I8, R4 + 23552)
    wpq = place("wpq", [128, IFF], BF16, R4 + 26368)
    wrep = place("wrep", [128, ICORE], F32, R4 + 32000)
    wsc = place("wsc", [128, 8], F32, R4 + 33408)
    wrow = place("wrow", [1, ICORE], F32, R4 + 33440)
    winv = place("winv", [1, ICORE], F32, R4 + 36288)
    wstg = [place(f"wstg{i}", [128, 128], BF16, R4 + 37696 + 256 * i) for i in range(2)]

    wmf = place("wmf", [128, NDC * 768], F32, R5)
    modp = place("modp", [4, 768], F32, R5 + 36864)
    c_bvaf = place("c_bvaf", [1, DA], F32, R5 + 40960)
    c_mrow = place("c_mrow", [1, 1024], F32, R5 + 45056)
    a_v = place("a_v", [128, NTC * DA], BF16, R5)

    ps = [nc.alloc_psum_tensor(f"pb{i}", [128, 1024], F32) for i in range(4)]

    with (
        nc.semaphore("qS") as qS,
        nc.semaphore("qG") as qG,
        nc.semaphore("tE") as tE,
        nc.semaphore("vE") as vE,
        nc.semaphore("sE") as sE,
        nc.semaphore("cC") as cC,
        nc.Block() as blk,
    ):
        @blk.sync
        def _(sync):
            for t, d in ((c_bq, bq2d), (c_bk, bk2d), (c_bo, bo2d), (c_idB, idnBd),
                         (c_idF, idnFd), (c_ln1, lnod), (c_1rf, onesRFd),
                         (c_t1, t1d), (c_t2, t2d), (c_padv, padvd),
                         (c_pads, padsd), (c_bvaf, bvad)):
                i = sync.dma_start(out=t[:, :], in_=d)
                L.inc(i, qS)
            i = sync.dma_start(out=c_clip[0:1, 0:1], in_=clipxd)
            L.inc(i, qS)
            i = sync.dma_start(out=c_clip[0:1, 1:2], in_=clipdd)
            L.inc(i, qS)
            L.set("p0_small", qS)
            i = sync.dma_start(out=c_csl[:, :], in_=cTd)
            L.inc(i, qS)
            L.set("p0_c", qS)
            for dc in range(NDC):
                i = sync.dma_start(
                    out=wmf[:, dc * 768:(dc + 1) * 768],
                    in_=wmodsd.rearrange("(c p) j -> c p j", p=128)[dc])
                L.inc(i, qS)
            L.set("p0_wmod", qS)

        @blk.vector
        def _(vector):
            vector.memset(c_eps[:, :], 1e-5)
            vector.memset(c_127[:, :], 127.0)
            vector.memset(c_negC[:, :], -30.0)
            vector.memset(c_q25[:, :], 0.25)
            vector.memset(c_km1[:, :], -1.0)
            vector.memset(c_k0[:, :], 0.0)
            vector.reciprocal(c_r127[:, :], c_127[:, :])
            vector.memset(c_mx[:, :], 0.0)
            L.wait(vector, qS, "p0_small")
            vector.tensor_copy(c_1rb[:, :], c_1rf[:, :])
            vector.tensor_copy(c_bva[:, :], c_bvaf[:, :])
            L.set("v_wmb", vE)

        PRE6 = ()

        def pre6(eng, which=0):
            if which not in PRE6:
                return
            eng.add_instruction(mybir.InstLoadActFuncSet(
                name=nc.get_next_instruction_name(), act_func_set_id=6,
                ins=[], outs=[]))

        @blk.scalar
        def _(scalar):
            L.wait(scalar, qS, "p0_c")
            i = scalar.activation(c_csi[:, :], c_csl[:, :], AF.Silu)
            L.inc(i, sE)
            L.set("s_csi", sE)

        @blk.tensor
        def _(tensor):
            L.wait(tensor, qS, "p0_wmod")
            L.wait(tensor, sE, "s_csi")
            for nt, (n0, n1) in enumerate(((0, 512), (512, 768))):
                for dc in range(NDC):
                    i = tensor.matmul(ps[2 + nt].ap()[0:4, 0:n1 - n0],
                                      c_csi[:, dc * 4:(dc + 1) * 4],
                                      wmf[:, dc * 768 + n0:dc * 768 + n1],
                                      start=(dc == 0), stop=(dc == NDC - 1))
            L.inc(i, tE)
            L.set("t_modp", tE)

        @blk.scalar
        def _(scalar):
            L.wait(scalar, tE, "t_modp")
            scalar.activation(modp[:, 0:512], ps[2].ap()[0:4, 0:512], AF.Identity)
            i = scalar.activation(modp[:, 512:768], ps[3].ap()[0:4, 0:256],
                                  AF.Identity)
            L.inc(i, sE)
            L.set("s_modp", sE)
            pre6(scalar, 1)

        @blk.gpsimd
        def _(g):
            L.wait(g, sE, "s_modp")
            i = g.dma_start(out=agmod_i.rearrange("(b j) -> b j", b=B),
                            in_=modp[:, :])
            L.inc(i, qG)
            g.wait_ge(qG, L.at(qG))
            i = g.collective_compute("AllGather", ALU.bypass, replica_groups=RG,
                                     ins=[agmod_i.ap().opt()],
                                     outs=[agmod_o.ap().opt()])
            L.inc(i, cC)
            L.wait(g, cC, None) if False else g.wait_ge(cC, L.at(cC))
            rb = g.alloc_register("rb")
            g.reg_load(rb, c_pads[0:1, 2:3])
            rbv = er.make_scalar_value(rb, min_val=0, max_val=3)
            i = g.dma_start(
                out=modrow_d.rearrange("(r j) -> r j", r=8),
                in_=agmod_o.rearrange("(r b j) -> b r j", r=8, b=B,
                                      j=768)[bass.ds(rbv, 1)])
            L.inc(i, qG)
            L.set("g_modrow", qG)

        @blk.sync
        def _(sync):
            L.wait(sync, qG, "g_modrow")
            i = sync.dma_start(out=sc48[:, :],
                               in_=modrow_d.rearrange("(q p) -> q p", q=48, p=128))
            L.inc(i, qS)
            L.set("p0_sc48", qS)

        @blk.tensor
        def _(tensor):
            L.wait(tensor, qS, "p0_sc48")
            i = tensor.transpose(ps[2].ap()[:, 0:48], sc48[:, :], c_idF[0:48, 0:48])
            L.inc(i, tE)
            L.set("t_mod48", tE)

        @blk.vector
        def _(vector):
            L.wait(vector, tE, "t_mod48")
            i = vector.tensor_copy(c_mod[:, :], ps[2].ap()[:, 0:48])
            L.inc(i, vE)
            L.set("v_cmod", vE)

        @blk.vector
        def _(vector):
            L.wait(vector, vE, "v_cmod")
            for src0, dstg in ((16, c_ga), (40, c_gb)):
                L.chain(vector, vE, vector.tensor_single_scalar(
                    c_sgn[:, :], c_mod[:, src0:src0 + 8], -1.0, ALU.mult))
                L.chain(vector, vE, vector.tensor_max(
                    c_mag[:, 0:8], c_mod[:, src0:src0 + 8], c_sgn[:, :]))
                L.chain(vector, vE, vector.tensor_scalar(
                    c_mag2[:, :], c_mag[:, 0:8], c_q25[:, 0:1], None, ALU.max))
                L.chain(vector, vE, vector.tensor_single_scalar(
                    c_mag3[:, :], c_mag2[:, :], -1.0, ALU.mult))
                L.chain(vector, vE, vector.tensor_single_scalar(
                    c_sgn[:, :], c_mod[:, src0:src0 + 8], 1e20, ALU.mult))
                L.chain(vector, vE, vector.tensor_tensor(
                    c_mag3[:, :], c_sgn[:, :], c_mag3[:, :], ALU.max))
                L.chain(vector, vE, vector.tensor_tensor(
                    dstg[:, :], c_mag3[:, :], c_mag2[:, :], ALU.min))
            vector.tensor_single_scalar(c_g1[:, :], c_mod[:, 8:16], 1.0, ALU.add)
            i = vector.tensor_single_scalar(c_g2[:, :], c_mod[:, 32:40], 1.0,
                                            ALU.add)
            L.inc(i, vE)
            L.set("v_mod", vE)

        if tap is not None and tap[0] == "gax":
            @blk.gpsimd
            def _(g):
                L.wait(g, vE, "v_mod")
                i = g.dma_start(out=dbg.rearrange("(a p) t -> a p t", a=1)[0][:, 0:8], in_=c_ga[:, :])
                L.inc(i, qG)
                i = g.dma_start(out=dbg.rearrange("(a p) t -> a p t", a=1)[0][:, 8:16], in_=c_gb[:, :])
                L.inc(i, qG)
                i = g.dma_start(out=dbg.rearrange("(a p) t -> a p t", a=1)[0][:, 16:24], in_=c_g1[:, :])
                L.inc(i, qG)
                i = g.dma_start(out=dbg.rearrange("(a p) t -> a p t", a=1)[0][:, 24:32], in_=c_g2[:, :])
                L.inc(i, qG)
                i = g.dma_start(out=dbg.rearrange("(a p) t -> a p t", a=1)[0][:, 32:40], in_=c_sgn[:, :])
                L.inc(i, qG)
                i = g.dma_start(out=dbg.rearrange("(a p) t -> a p t", a=1)[0][:, 40:48], in_=c_mag[:, 0:8])
                L.inc(i, qG)
                i = g.dma_start(out=dbg.rearrange("(a p) t -> a p t", a=1)[0][:, 48:56], in_=c_mag2[:, :])
                L.inc(i, qG)
                i = g.dma_start(out=dbg.rearrange("(a p) t -> a p t", a=1)[0][:, 56:64], in_=c_mod[:, 16:24])
                L.inc(i, qG)
                i = g.dma_start(out=dbg.rearrange("(a p) t -> a p t", a=1)[0][:, 64:72], in_=c_mag3[:, :])
                L.inc(i, qG)
                g.wait_ge(qG, L.at(qG))

        if tap is not None and tap[0] == "mod":
            @blk.gpsimd
            def _(g):
                L.wait(g, vE, "v_mod")
                i = g.dma_start(out=dbg, in_=c_mod[:, :])
                L.inc(i, qG)
                g.wait_ge(qG, L.at(qG))
        if stage < 2:
            _finish(nc, L, blk, qG, out, backing)
            return nc

        @blk.sync
        def _(sync):
            i = sync.dma_start(out=wpf[:, 0:IFF], in_=wdNd)
            L.inc(i, qS)
            L.set("p0_wd", qS)

        @blk.vector
        def _(vector):
            L.wait(vector, qS, "p0_wd")
            L.chain(vector, vE, vector.reduce_max(
                wsc[:, 0:1], wpf[:, 0:IFF], axis=AX,
                apply_absolute_value=True))
            i = vector.tensor_tensor(wsc[:, 1:2], wsc[:, 0:1], c_r127[:, 0:1],
                                     ALU.mult)
            L.inc(i, vE)
            L.set("v_wdsc", vE)

        @blk.scalar
        def _(scalar):
            L.wait(scalar, vE, "v_wdsc")
            L.chain(scalar, sE, scalar.activation(wsc[:, 3:4], wsc[:, 1:2],
                                                  AF.Ln))
            i = scalar.activation(wsc[:, 2:3], wsc[:, 3:4], AF.Exp, scale=-1.0)
            L.inc(i, sE)
            L.set("s_wdinv", sE)

        @blk.vector
        def _(vector):
            L.wait(vector, sE, "s_wdinv")
            vector.tensor_scalar(wps[:, 0:IFF], wpf[:, 0:IFF], wsc[:, 2:3], None,
                                 ALU.mult)
            vector.tensor_copy(wp8[:, 0:IFF], wps[:, 0:IFF])
            i = vector.tensor_copy(wpq[:, 0:IFF], wp8[:, 0:IFF])
            L.inc(i, vE)
            L.set("v_wdq", vE)

        @blk.gpsimd
        def _(g):
            L.wait(g, vE, "v_wdq")
            i = g.dma_start(out=agsc_i.rearrange("(a b) -> a b", b=1)[704:832],
                            in_=wsc[:, 1:2])
            L.inc(i, qG)
            L.set("g_sd", qG)

        for ic in range(NIC):
            @blk.tensor
            def _(tensor, ic=ic):
                if ic == 0:
                    L.wait(tensor, vE, "v_wdq")
                if ic >= 2:
                    L.wait(tensor, vE, f"v_wdT{ic - 2}")
                i = tensor.transpose(
                    ps[2 + ic % 2].ap().bitcast(BF16)[:, 0:128],
                    wpq[:, ic * 128:(ic + 1) * 128], c_idB[:, :])
                L.inc(i, tE)
                L.set(f"t_wdT{ic}", tE)

            @blk.vector
            def _(vector, ic=ic):
                L.wait(vector, tE, f"t_wdT{ic}")
                if ic >= 2:
                    L.wait(vector, qG, f"g_wdT{ic - 2}")
                i = vector.tensor_copy(
                    wstg[ic % 2][:, :],
                    ps[2 + ic % 2].ap().bitcast(BF16)[:, 0:128])
                L.inc(i, vE)
                L.set(f"v_wdT{ic}", vE)

            @blk.gpsimd
            def _(g, ic=ic):
                L.wait(g, vE, f"v_wdT{ic}")
                i = g.dma_start(out=agwd_i.ap()[ic * 128:(ic + 1) * 128, :],
                                in_=wstg[ic % 2][:, :])
                L.inc(i, qG)
                L.set(f"g_wdT{ic}", qG)
        L.set("g_wd_all", qG)

        for wname, wN, wT, agi, sc_off in (("g", wgNd, wgTd, agwg_i, 0),
                                           ("u", wuNd, wuTd, agwu_i, 352)):
            @blk.sync
            def _(sync, wN=wN, wname=wname):
                L.wait(sync, vE, "v_wdq" if wname == "g" else "v_wgq7")
                for k, rows in ((0, 128), (1, 128), (2, 96)):
                    i = sync.dma_start(out=wpf[0:rows, k * 1024:(k + 1) * 1024],
                                       in_=wN[k * 128:k * 128 + rows, :])
                    L.inc(i, qS)
                L.set(f"p_w{wname}n", qS)

            @blk.vector
            def _(vector, wname=wname):
                L.wait(vector, qS, f"p_w{wname}n")
                for k, rows in ((0, 128), (1, 128), (2, 96)):
                    L.chain(vector, vE, vector.reduce_max(
                        wsc[0:rows, k:k + 1],
                        wpf[0:rows, k * 1024:(k + 1) * 1024],
                        axis=AX, apply_absolute_value=True))
                i = vector.tensor_scalar(wsc[:, 4:7], wsc[:, 0:3],
                                         c_r127[:, 0:1], None, ALU.mult)
                L.inc(i, vE)
                L.set(f"v_w{wname}sc", vE)

            @blk.tensor
            def _(tensor, wname=wname):
                L.wait(tensor, vE, f"v_w{wname}sc")
                for k, rows in ((0, 128), (1, 128), (2, 96)):
                    i = tensor.transpose(ps[2].ap()[0:1, k * 128:k * 128 + rows],
                                         wsc[0:rows, 4 + k:5 + k],
                                         c_idF[0:rows, 0:rows])
                L.inc(i, tE)
                L.set(f"t_w{wname}r", tE)

            @blk.scalar
            def _(scalar, wname=wname):
                L.wait(scalar, tE, f"t_w{wname}r")
                i = scalar.activation(wrow[0:1, :], ps[2].ap()[0:1, 0:ICORE],
                                      AF.Identity)
                L.inc(i, sE)
                L.set(f"s_w{wname}row", sE)

            @blk.gpsimd
            def _(g, wname=wname, sc_off=sc_off):
                L.wait(g, sE, f"s_w{wname}row")
                i = g.dma_start(
                    out=agsc_i.rearrange("(a b) -> a b",
                                         a=1)[0:1, sc_off:sc_off + ICORE],
                    in_=wrow[0:1, :])
                L.inc(i, qG)
                L.set(f"g_w{wname}sc", qG)

            @blk.scalar
            def _(scalar, wname=wname):
                L.wait(scalar, sE, f"s_w{wname}row")
                L.wait(scalar, qG, f"g_w{wname}sc")
                L.chain(scalar, sE, scalar.activation(
                    wrep[0:1, 0:ICORE], wrow[0:1, :], AF.Ln))
                i = scalar.activation(winv[0:1, :], wrep[0:1, 0:ICORE],
                                      AF.Exp, scale=-1.0)
                L.inc(i, sE)
                L.set(f"s_w{wname}inv", sE)

            @blk.tensor
            def _(tensor, wname=wname):
                L.wait(tensor, sE, f"s_w{wname}inv")
                i = tensor.matmul(ps[3].ap()[:, 0:ICORE], c_1rf[0:1, :],
                                  winv[0:1, :], start=True, stop=True)
                L.inc(i, tE)
                L.set(f"t_w{wname}rep", tE)

            @blk.vector
            def _(vector, wname=wname):
                L.wait(vector, tE, f"t_w{wname}rep")
                i = vector.tensor_copy(wrep[:, :], ps[3].ap()[:, 0:ICORE])
                L.inc(i, vE)
                L.set(f"v_w{wname}rep", vE)

            for ec in range(NDC):
                @blk.sync
                def _(sync, wT=wT, ec=ec, wname=wname):
                    if ec == 0:
                        L.wait(sync, vE, f"v_w{wname}sc")
                    else:
                        L.wait(sync, vE, f"v_w{wname}q{ec - 1}")
                    i = sync.dma_start(out=wpf[:, 0:ICORE],
                                       in_=wT[ec * 128:(ec + 1) * 128, :])
                    L.inc(i, qS)
                    L.set(f"p_w{wname}t{ec}", qS)

                @blk.vector
                def _(vector, ec=ec, wname=wname):
                    L.wait(vector, qS, f"p_w{wname}t{ec}")
                    if ec == 0:
                        L.wait(vector, vE, f"v_w{wname}rep")
                    if ec >= 1:
                        L.wait(vector, qG, f"g_w{wname}q{ec - 1}")
                    L.chain(vector, vE, vector.tensor_tensor(
                        wps[:, 0:ICORE], wpf[:, 0:ICORE], wrep[:, :], ALU.mult))
                    L.chain(vector, vE, vector.tensor_copy(
                        wp8[:, 0:ICORE], wps[:, 0:ICORE]))
                    i = vector.tensor_copy(wpq[:, 0:ICORE], wp8[:, 0:ICORE])
                    L.inc(i, vE)
                    L.set(f"v_w{wname}q{ec}", vE)

                @blk.gpsimd
                def _(g, agi=agi, ec=ec, wname=wname):
                    L.wait(g, vE, f"v_w{wname}q{ec}")
                    i = g.dma_start(out=agi.ap()[ec * 128:(ec + 1) * 128, :],
                                    in_=wpq[:, 0:ICORE])
                    L.inc(i, qG)
                    L.set(f"g_w{wname}q{ec}", qG)
            L.mark[f"g_w{wname}_all"] = L.at(qG)

        @blk.gpsimd
        def _(g):
            g.wait_ge(qG, L.mark["g_wu_all"])
            g.wait_ge(qG, L.mark["g_wd_all"])
            g.wait_ge(qG, L.mark["g_sd"])
            for ins_, outs_, nm in ((agwg_i, agwg_o, "ag_wg"),
                                    (agwu_i, agwu_o, "ag_wu"),
                                    (agwd_i, agwd_o, "ag_wd"),
                                    (agsc_i, agsc_o, "ag_sc")):
                i = g.collective_compute("AllGather", ALU.bypass, replica_groups=RG,
                                         ins=[ins_.ap().opt()],
                                         outs=[outs_.ap().opt()])
                L.inc(i, cC)
                L.set(nm, cC)
        L.mark["p0b_v_done"] = L.at(vE)

        if tap is not None and tap[0] == "wgq":
            @blk.sync
            def _(sync):
                L.wait(sync, cC, "ag_wg")
                i = sync.dma_start(
                    out=sc22[:, :] if False else wpf[:, 0:IFF],
                    in_=agwg_o.rearrange("(a b) -> a b", b=IFF)[0:128])
                L.inc(i, qS)
                L.set("tap_wgq", qS)

            @blk.vector
            def _(vector):
                L.wait(vector, qS, "tap_wgq")
                i = vector.tensor_copy(wps[:, 0:IFF], wpf[:, 0:IFF].bitcast(BF16)[:, 0:IFF])
                L.inc(i, vE)
                L.set("tap_wgq_v", vE)

            @blk.gpsimd
            def _(g):
                L.wait(g, vE, "tap_wgq_v")
                i = g.dma_start(out=dbg, in_=wps[:, 0:IFF])
                L.inc(i, qG)
                g.wait_ge(qG, L.at(qG))
        if stage < 3:
            _finish(nc, L, blk, qG, out, backing)
            return nc

        def emit_ln(tag, src_dram, n_tok, h_out, h_stride, g_t, s_t, s_off,
                    first_wait=None):
            for tt in range(n_tok // 512):
                r = tt % 2

                @blk.sync
                def _(sync, tt=tt, r=r):
                    if tt == 0 and first_wait is not None:
                        sync.wait_ge(qG, L.mark[first_wait])
                    if tt >= 2:
                        L.wait(sync, vE, f"{tag}_ap{tt - 2}")
                    i = sync.dma_start(
                        out=xt[r].rearrange("p (c t) -> p c t", c=NDC),
                        in_=src_dram.rearrange(
                            "(c p) t -> p c t",
                            p=128)[:, :, tt * 512:(tt + 1) * 512])
                    L.inc(i, qS)
                    L.set(f"{tag}_x{tt}", qS)

                @blk.vector
                def _(vector, tt=tt, r=r):
                    L.wait(vector, qS, f"{tag}_x{tt}")
                    if tt >= 1:
                        L.wait(vector, tE, f"{tag}_st{tt - 1}")
                    for dc in range(NDC):
                        i = vector.tensor_copy(xb_t[:, dc * 512:(dc + 1) * 512],
                                               xt[r][:, dc * 512:(dc + 1) * 512])
                    L.inc(i, vE)
                    L.set(f"{tag}_xb{tt}", vE)

                @blk.scalar
                def _(scalar, tt=tt, r=r):
                    L.wait(scalar, qS, f"{tag}_x{tt}")
                    if tt >= 1:
                        L.wait(scalar, tE, f"{tag}_st{tt - 1}")
                    for dc in range(NDC):
                        i = scalar.activation(xq_t[:, dc * 512:(dc + 1) * 512],
                                              xt[r][:, dc * 512:(dc + 1) * 512],
                                              AF.Square)
                    L.inc(i, sE)
                    L.set(f"{tag}_xq{tt}", sE)

                @blk.tensor
                def _(tensor, tt=tt):
                    L.wait(tensor, vE, f"{tag}_xb{tt}")
                    L.wait(tensor, sE, f"{tag}_xq{tt}")
                    if tt >= 1:
                        L.wait(tensor, vE, f"{tag}_mrt{tt - 1}")
                        L.wait(tensor, sE, f"{tag}_mu{tt - 1}")
                    for dc in range(NDC):
                        tensor.matmul(ps[0].ap()[:, 0:512], c_ln1[:, :],
                                      xb_t[:, dc * 512:(dc + 1) * 512],
                                      start=(dc == 0), stop=(dc == NDC - 1))
                    for dc in range(NDC):
                        i = tensor.matmul(ps[0].ap()[:, 512:1024], c_ln1[:, :],
                                          xq_t[:, dc * 512:(dc + 1) * 512],
                                          start=(dc == 0), stop=(dc == NDC - 1))
                    L.inc(i, tE)
                    L.set(f"{tag}_st{tt}", tE)

                @blk.scalar
                def _(scalar, tt=tt):
                    L.wait(scalar, tE, f"{tag}_st{tt}")
                    scalar.activation(muS[:, :], ps[0].ap()[:, 0:512], AF.Identity)
                    i = scalar.activation(mu2[:, :], ps[0].ap()[:, 0:512], AF.Square)
                    L.inc(i, sE)
                    L.set(f"{tag}_mu{tt}", sE)

                @blk.vector
                def _(vector, tt=tt):
                    L.wait(vector, sE, f"{tag}_mu{tt}")
                    i = vector.tensor_sub(varS[:, :], ps[0].ap()[:, 512:1024],
                                          mu2[:, :])
                    L.inc(i, vE)
                    L.set(f"{tag}_var{tt}", vE)

                @blk.scalar
                def _(scalar, tt=tt):
                    L.wait(scalar, vE, f"{tag}_var{tt}")
                    scalar.activation(lnvS[:, :], varS[:, :], AF.Ln,
                                      bias=c_eps[:, 0:1])
                    i = scalar.activation(rstdS[:, :], lnvS[:, :], AF.Exp,
                                          scale=-0.5)
                    L.inc(i, sE)
                    L.set(f"{tag}_rstd{tt}", sE)

                @blk.vector
                def _(vector, tt=tt, r=r):
                    L.wait(vector, sE, f"{tag}_rstd{tt}")
                    if tag == "L1" and tt == 0:
                        L.wait(vector, vE, "v_mod")
                    i = vector.tensor_mul(mrtS[:, :], muS[:, :], rstdS[:, :])
                    L.inc(i, vE)
                    L.set(f"{tag}_mrt{tt}", vE)
                    for dc in range(NDC):
                        vector.tensor_mul(tmpA[:, :],
                                          xt[r][:, dc * 512:(dc + 1) * 512],
                                          rstdS[:, :])
                        vector.tensor_sub(tmpB[:, :], tmpA[:, :], mrtS[:, :])
                        i = vector.tensor_scalar(
                            h_out[:, dc * h_stride + tt * 512:
                                  dc * h_stride + tt * 512 + 512],
                            tmpB[:, :], g_t[:, dc:dc + 1],
                            s_t[:, s_off + dc:s_off + dc + 1],
                            ALU.mult, ALU.add)
                    L.inc(i, vE)
                    L.set(f"{tag}_ap{tt}", vE)
            L.mark[f"{tag}_done_v"] = L.at(vE)

        emit_ln("L1", xTd, FR, a_h1, FR, c_g1, c_mod, 0)

        if tap is not None and tap[0] == "h1":
            @blk.gpsimd
            def _(g):
                g.wait_ge(vE, L.mark["L1_done_v"])
                for dc in range(NDC):
                    i = g.dma_start(
                        out=dbg.rearrange("(c p) t -> c p t", p=128)[dc],
                        in_=a_h1[:, dc * FR:(dc + 1) * FR])
                    L.inc(i, qG)
                g.wait_ge(qG, L.at(qG))
        if stage < 4:
            _finish(nc, L, blk, qG, out, backing)
            return nc


        for ec in range(NDC):
            @blk.sync
            def _(sync, ec=ec):
                if ec == 0:
                    L.wait(sync, vE, "L1_done_v")
                if ec >= 1:
                    L.wait(sync, vE, f"v_wc{ec - 1}")
                i = sync.dma_start(
                    out=wtl[ec % 2][:, 0:DA] if False else wtl[0][:, 0:1024],
                    in_=wvTd.rearrange("(c p) d -> c p d", p=128)[ec][:, 0:1024])
                L.inc(i, qS)
                i = sync.dma_start(
                    out=wtl[1][:, 0:16],
                    in_=wvTd.rearrange("(c p) d -> c p d", p=128)[ec][:, 1024:1040])
                L.inc(i, qS)
                L.set(f"v_wl{ec}", qS)

            @blk.vector
            def _(vector, ec=ec):
                L.wait(vector, qS, f"v_wl{ec}")
                vector.tensor_copy(wvb[:, ec * DA:ec * DA + 1024], wtl[0][:, 0:1024])
                i = vector.tensor_copy(wvb[:, ec * DA + 1024:ec * DA + 1040],
                                       wtl[1][:, 0:16])
                L.inc(i, vE)
                L.set(f"v_wc{ec}", vE)

        for tc in range(NTC):
            @blk.tensor
            def _(tensor, tc=tc):
                if tc == 0:
                    L.wait(tensor, vE, f"v_wc{NDC - 1}")
                if tc >= 1:
                    L.wait(tensor, vE, f"v_evv{tc - 1}")
                for part, (n0, n1) in enumerate(((0, 512), (512, 1024),
                                                 (1024, 1040))):
                    pv = ps[2].ap()[:, n0:n1] if part < 2 else ps[3].ap()[:, 0:16]
                    for ec in range(NDC):
                        tensor.matmul(
                            pv,
                            a_h1[:, ec * FR + tc * 128:ec * FR + tc * 128 + 128],
                            wvb[:, ec * DA + n0:ec * DA + n1],
                            start=(ec == 0), stop=False)
                    i = tensor.matmul(pv, c_1rb[0:1, :],
                                      c_bva[0:1, n0:n1], start=False, stop=True)
                L.inc(i, tE)
                L.set(f"v_mmv{tc}", tE)

            @blk.vector
            def _(vector, tc=tc):
                L.wait(vector, tE, f"v_mmv{tc}")
                vector.tensor_copy(a_v[:, tc * DA:tc * DA + 1024],
                                   ps[2].ap()[:, 0:1024])
                i = vector.tensor_copy(a_v[:, tc * DA + 1024:tc * DA + 1040],
                                       ps[3].ap()[:, 0:16])
                L.inc(i, vE)
                L.set(f"v_evv{tc}", vE)
        L.mark["v_done"] = L.at(vE)

        for nm, wdram, bias_t, ntt, toff in (("q", wqTd, c_bq, 4, 512),
                                             ("k", wkTd, c_bk, 6, 0)):
            for dc in range(NDC):
                r = dc % 2

                @blk.sync
                def _(sync, dc=dc, r=r, nm=nm, wdram=wdram):
                    if nm == "q" and dc == 0:
                        L.wait(sync, vE, f"v_wc{NDC - 1}")
                    if nm == "k" and dc < 2:
                        L.wait(sync, vE, "q_wc7")
                    if dc >= 2:
                        L.wait(sync, vE, f"{nm}_wc{dc - 2}")
                    for ec in range(NDC):
                        i = sync.dma_start(
                            out=wtl[r][:, ec * 128:ec * 128 + 128],
                            in_=wdram.rearrange("(c p) d -> c p d",
                                                p=128)[ec][:, dc * 128:dc * 128 + 128])
                        L.inc(i, qS)
                    L.set(f"{nm}_wl{dc}", qS)

                @blk.vector
                def _(vector, dc=dc, r=r, nm=nm, ntt=ntt):
                    L.wait(vector, qS, f"{nm}_wl{dc}")
                    if nm == "k" and dc < 2:
                        L.wait(vector, tE, "q_mm31")
                    if dc >= 2:
                        L.wait(vector, tE, f"{nm}_mm{(dc - 2) * ntt + ntt - 1}")
                    i = vector.tensor_copy(wqbs[r][:, :], wtl[r][:, :])
                    L.inc(i, vE)
                    L.set(f"{nm}_wc{dc}", vE)

                def emit_pmm(tensor, dc, tt, idx, r, nm, ntt, toff):
                    if tt == 0:
                        L.wait(tensor, vE, f"{nm}_wc{dc}")
                    pr = dc * ntt + tt - 2
                    if pr >= 0:
                        L.wait(tensor, sE, f"{nm}_ev{pr}")
                    if nm == "k" and dc == 0 and tt == 0:
                        L.wait(tensor, vE, "v_done")
                    for ec in range(NDC):
                        i = tensor.matmul(
                            ps[idx].ap()[:, 0:512],
                            wqbs[r][:, ec * 128:ec * 128 + 128],
                            a_h1[:, ec * FR + toff + tt * 512:
                                 ec * FR + toff + tt * 512 + 512],
                            start=(ec == 0), stop=(ec == NDC - 1))
                    L.inc(i, tE)
                    L.set(f"{nm}_mm{dc * ntt + tt}", tE)

                for th in range(ntt // 2):
                    tts = (2 * th, 2 * th + 1)
                    for tt in tts:
                        @blk.tensor
                        def _(tensor, dc=dc, tt=tt, r=r, nm=nm, ntt=ntt,
                              toff=toff):
                            emit_pmm(tensor, dc, tt, (dc * ntt + tt) % 2,
                                     r, nm, ntt, toff)

                    if nm == "q":
                        @blk.scalar
                        def _(scalar, dc=dc, tts=tts, ntt=ntt, bias_t=bias_t):
                            for tt in tts:
                                it = dc * ntt + tt
                                L.wait(scalar, tE, f"q_mm{it}")
                                if it >= 2:
                                    L.wait(scalar, qG, f"q_sp{it - 2}")
                                i = scalar.activation(psth[it % 2][:, :],
                                                      ps[it % 2].ap()[:, 0:512],
                                                      AF.Identity,
                                                      bias=bias_t[:, dc:dc + 1])
                                L.inc(i, sE)
                                L.set(f"q_ev{it}", sE)

                        for tt in tts:
                            @blk.gpsimd
                            def _(g, dc=dc, tt=tt, ntt=ntt):
                                it = dc * ntt + tt
                                L.wait(g, sE, f"q_ev{it}")
                                i = g.dma_start(
                                    out=qhd.rearrange("(c p) t -> c p t",
                                                      p=128)[dc][
                                        :, tt * 512:tt * 512 + 512],
                                    in_=psth[it % 2][:, :])
                                L.inc(i, qG)
                                L.set(f"q_sp{it}", qG)
                    else:
                        @blk.scalar
                        def _(scalar, dc=dc, tts=tts, ntt=ntt, bias_t=bias_t):
                            for tt in tts:
                                it = dc * ntt + tt
                                L.wait(scalar, tE, f"k_mm{it}")
                                if it >= 2:
                                    L.wait(scalar, vE, f"k_lo{it - 2}")
                                L.chain(scalar, sE, scalar.activation(
                                    pstf[it % 2][:, :],
                                    ps[it % 2].ap()[:, 0:512],
                                    AF.Identity, bias=bias_t[:, dc:dc + 1]))
                                i = scalar.activation(psth[it % 2][:, :],
                                                      pstf[it % 2][:, :],
                                                      AF.Identity)
                                L.inc(i, sE)
                                L.set(f"k_ev{it}", sE)

                        for tt in tts:
                            @blk.vector
                            def _(vector, dc=dc, tt=tt, ntt=ntt):
                                it = dc * ntt + tt
                                L.wait(vector, sE, f"k_ev{it}")
                                if it >= 2:
                                    L.wait(vector, qG, f"k_sp{it - 2}")
                                i = vector.tensor_sub(pstl[it % 2][:, :],
                                                      pstf[it % 2][:, :],
                                                      psth[it % 2][:, :])
                                L.inc(i, vE)
                                L.set(f"k_lo{it}", vE)

                            @blk.gpsimd
                            def _(g, dc=dc, tt=tt, ntt=ntt):
                                it = dc * ntt + tt
                                L.wait(g, vE, f"k_lo{it}")
                                i = g.dma_start(
                                    out=khd.rearrange("(c p) t -> c p t",
                                                      p=128)[dc][
                                        :, tt * 512:tt * 512 + 512],
                                    in_=psth[it % 2][:, :])
                                L.inc(i, qG)
                                i = g.dma_start(
                                    out=kld.rearrange("(c p) t -> c p t",
                                                      p=128)[dc][
                                        :, tt * 512:tt * 512 + 512],
                                    in_=pstl[it % 2][:, :])
                                L.inc(i, qG)
                                L.set(f"k_sp{it}", qG)
            L.mark[f"{nm}_done"] = L.at(qG)

        if tap is not None and tap[0] == "qlo":
            @blk.gpsimd
            def _(g):
                g.wait_ge(qG, L.mark["q_done"])
                i = g.dma_start(out=dbg.rearrange("(a p) t -> a p t", a=2)[0],
                                in_=qhd.rearrange("(c p) t -> c p t", p=128)[0])
                L.inc(i, qG)
                i = g.dma_start(out=dbg.rearrange("(a p) t -> a p t", a=2)[1],
                                in_=qld.rearrange("(c p) t -> c p t", p=128)[0])
                L.inc(i, qG)
                g.wait_ge(qG, L.at(qG))

        @blk.gpsimd
        def _(g):
            g.wait_ge(qG, L.mark["k_done"])
            rk = g.alloc_register("rkp")
            g.reg_load(rk, c_pads[0:1, 0:1])
            rkv = er.make_scalar_value(rk, min_val=0, max_val=2560,
                                       guaranteed_mod_val=512)
            for dc in range(NDC):
                i = g.dma_start(
                    out=khd.rearrange("(c p) t -> c p t", p=128)[dc][
                        :, bass.ds(rkv, 512)],
                    in_=c_km1[:, :])
                L.inc(i, qG)
                i = g.dma_start(
                    out=kld.rearrange("(c p) t -> c p t", p=128)[dc][
                        :, bass.ds(rkv, 512)],
                    in_=c_k0[:, :])
                L.inc(i, qG)
            L.set("kpad_done", qG)

        @blk.vector
        def _(vector):
            vector.wait_ge(vE, L.mark["v_done"])
            rv = vector.alloc_register("rv")
            vector.reg_load(rv, c_pads[0:1, 1:2])
            rvv = er.make_scalar_value(rv, min_val=0, max_val=20800)
            for k in range(4):
                i = vector.tensor_copy(a_v[:, bass.ds(rvv + k * DA, DA)],
                                       c_padv[:, :])
            L.inc(i, vE)
            L.set("pads_done", vE)

        if tap is not None and tap[0] == "av":
            @blk.gpsimd
            def _(g):
                L.wait(g, vE, "pads_done")
                i = g.dma_start(out=dbg, in_=a_v[:, :])
                L.inc(i, qG)
                g.wait_ge(qG, L.at(qG))
        if stage < 5:
            _finish(nc, L, blk, qG, out, backing)
            return nc


        def sspan(slot, jc):
            if slot == 0:
                return (0, 512) if jc <= 7 else (128 * (jc - 8), 512)
            if slot == 3:
                return (0, 128 * (jc + 1)) if jc <= 3 else (0, 512)
            if jc <= 3:
                return (0, 128 * (jc + 1))
            if jc <= 7:
                return (0, 512)
            return (128 * (jc - 8), 512)

        def vjcs(slot, ib):
            if slot == 0:
                return list(range(0, ib + 9))
            if slot == 3:
                return list(range(ib, 12))
            return list(range(ib, ib + 9))

        def esrc(slot, jc, ib, Eb):
            if slot == 0:
                if jc <= 3:
                    return aEm[:, 512 * jc + 128 * ib:512 * jc + 128 * ib + 128]
                if jc == ib + 8:
                    return aEm[:, 2560 + 128 * ib:2560 + 128 * ib + 128]
            elif slot == 3:
                if jc >= 8:
                    return aEm[:, 512 * (jc - 8) + 128 * ib:
                               512 * (jc - 8) + 128 * ib + 128]
                if jc == ib:
                    return aEm[:, 2048 + 128 * ib:2048 + 128 * ib + 128]
            else:
                if jc == ib:
                    return aEm[:, 2048 + 128 * ib:2048 + 128 * ib + 128]
                if jc == ib + 8:
                    return aEm[:, 2560 + 128 * ib:2560 + 128 * ib + 128]
            return Eb[:, 512 * jc + 128 * ib:512 * jc + 128 * ib + 128]

        @blk.sync
        def _(sync):
            L.wait(sync, vE, "pads_done")
            i = sync.dma_start(out=c_mm[:, :], in_=m0d)
            L.inc(i, qS)
            L.set("mm0_ready", qS)

        def emit_pair_load(p):
            w, h = p // 16, p % 16
            r = p % 2

            @blk.sync
            def _(sync, p=p, w=w, h=h, r=r):
                if p == 0:
                    sync.wait_ge(qG, L.mark["q_done"])
                    sync.wait_ge(qG, L.mark["kpad_done"])
                if p >= 2:
                    L.wait(sync, tE, f"sc{p - 2}_5")
                i = sync.dma_start(
                    out=kt2[r][0:64, 0:1536],
                    in_=khd[64 * h:64 * h + 64, 512 * w:512 * w + 1536])
                L.inc(i, qS)
                i = sync.dma_start(
                    out=kt2[r][64:128, 0:1536],
                    in_=kld[64 * h:64 * h + 64, 512 * w:512 * w + 1536])
                L.inc(i, qS)
                i = sync.dma_start(
                    out=qt2[r][0:64, 0:512],
                    in_=qhd[64 * h:64 * h + 64, 512 * w:512 * w + 512])
                L.inc(i, qS)
                i = sync.dma_start(
                    out=qt2[r][64:128, 0:512],
                    in_=qhd[64 * h:64 * h + 64, 512 * w:512 * w + 512])
                L.inc(i, qS)
                L.set(f"ptl{p}", qS)



        emit_pair_load(0)
        emit_pair_load(1)

        for p in range(64):
            w, h = p // 16, p % 16
            slot = w
            dc = h // 2
            r0 = 64 * (h % 2)
            Eb = aE[p % 2]
            qc0 = dc * TOWN + 512 * w

            def emit_sc(tensor, p, slot, g):
                r = p % 2
                if g == 0:
                    L.wait(tensor, qS, f"ptl{p}")
                if p == 0 and g == 0:
                    L.wait(tensor, vE, "pads_done")
                if g >= 3:
                    L.wait(tensor, sE, f"exp{p}_{g - 3}")
                elif p >= 1:
                    L.wait(tensor, sE, f"exp{p - 1}_{g + 3}")
                for jc in (2 * g, 2 * g + 1):
                    n0, n1 = sspan(slot, jc)
                    po = ps[g % 3].ap()[:, (jc % 2) * 512 + n0:
                                        (jc % 2) * 512 + n1]
                    i = tensor.matmul(po,
                                      kt2[r][:, 128 * jc:128 * jc + 128],
                                      qt2[r][:, n0:n1],
                                      start=True, stop=True)
                L.inc(i, tE)
                L.set(f"sc{p}_{g}", tE)

            for half in range(2):
                @blk.tensor
                def _(tensor, p=p, slot=slot, half=half):
                    for g in (3 * half, 3 * half + 1, 3 * half + 2):
                        emit_sc(tensor, p, slot, g)

                @blk.scalar
                def _(scalar, p=p, Eb=Eb, slot=slot, half=half):
                    for g in (3 * half, 3 * half + 1, 3 * half + 2):
                        L.wait(scalar, tE, f"sc{p}_{g}")
                        if p >= 2 and g == 0:
                            L.wait(scalar, tE, f"vmm{p - 2}")
                            L.wait(scalar, vE, f"msk{p - 2}")
                        e0, _ = sspan(slot, 2 * g)
                        _, e1 = sspan(slot, 2 * g + 1)
                        i = scalar.activation(
                            Eb[:, 1024 * g + e0:1024 * g + 512 + e1],
                            ps[g % 3].ap()[:, e0:512 + e1], AF.Exp)
                        L.inc(i, sE)
                        L.set(f"exp{p}_{g}", sE)

            if p + 2 < 64:
                emit_pair_load(p + 2)

            @blk.vector
            def _(vector, p=p, slot=slot, Eb=Eb):
                L.wait(vector, sE, f"exp{p}_5")
                if p >= 1:
                    L.wait(vector, tE, f"vmm{p - 1}")
                if slot == 0:
                    if p == 0:
                        L.wait(vector, qS, "mm0_ready")
                    vector.tensor_tensor(aEm[:, 0:2048], Eb[:, 0:2048],
                                         c_mm[:, :], ALU.mult)
                    for ib in range(4):
                        i = vector.tensor_tensor(
                            aEm[:, 2560 + 128 * ib:2688 + 128 * ib],
                            Eb[:, 4096 + 640 * ib:4224 + 640 * ib],
                            c_t2[:, :], ALU.mult)
                elif slot == 3:
                    if p == 48:
                        L.wait(vector, qS, "mm3_ready")
                    vector.tensor_tensor(aEm[:, 0:2048], Eb[:, 4096:6144],
                                         c_mm[:, :], ALU.mult)
                    for ib in range(4):
                        i = vector.tensor_tensor(
                            aEm[:, 2048 + 128 * ib:2176 + 128 * ib],
                            Eb[:, 640 * ib:640 * ib + 128],
                            c_t1[:, :], ALU.mult)
                else:
                    for ib in range(4):
                        vector.tensor_tensor(
                            aEm[:, 2048 + 128 * ib:2176 + 128 * ib],
                            Eb[:, 640 * ib:640 * ib + 128],
                            c_t1[:, :], ALU.mult)
                        i = vector.tensor_tensor(
                            aEm[:, 2560 + 128 * ib:2688 + 128 * ib],
                            Eb[:, 4096 + 640 * ib:4224 + 640 * ib],
                            c_t2[:, :], ALU.mult)
                L.inc(i, vE)
                L.set(f"msk{p}", vE)

            if tap is not None and tap[0] == f"e{p}":
                @blk.gpsimd
                def _(g, p=p, Eb=Eb):
                    L.wait(g, vE, f"msk{p}")
                    i = g.dma_start(out=dbg.rearrange("(a p) t -> a p t", a=1)[0][:, 0:6144],
                                    in_=Eb[:, :])
                    L.inc(i, qG)
                    i = g.dma_start(out=dbg.rearrange("(a p) t -> a p t", a=1)[0][:, 6144:9216],
                                    in_=aEm[:, :])
                    L.inc(i, qG)
                    g.wait_ge(qG, L.at(qG))

            if p == 15:
                @blk.sync
                def _(sync):
                    L.wait(sync, vE, "msk15")
                    i = sync.dma_start(out=c_mm[:, :], in_=m3d)
                    L.inc(i, qS)
                    L.set("mm3_ready", qS)

            @blk.tensor
            def _(tensor, p=p, w=w, h=h, slot=slot, Eb=Eb):
                L.wait(tensor, vE, f"msk{p}")
                if p >= 1:
                    L.wait(tensor, vE, f"evac{p - 1}")
                for ib in range(4):
                    pv = ps[3].ap()[:, 256 * ib:256 * ib + 65]
                    jcs = vjcs(slot, ib)
                    for ji, jc in enumerate(jcs):
                        i = tensor.matmul(
                            pv, esrc(slot, jc, ib, Eb),
                            a_v[:, (4 * w + jc) * DA + 65 * h:
                                (4 * w + jc) * DA + 65 * h + 65],
                            start=(ji == 0), stop=(ji == len(jcs) - 1))
                L.inc(i, tE)
                L.set(f"vmm{p}", tE)

            @blk.vector
            def _(vector, p=p):
                L.wait(vector, tE, f"vmm{p}")
                for ib in range(4):
                    L.chain(vector, vE, vector.tensor_copy(
                        c_zcp[:, ib:ib + 1],
                        ps[3].ap()[:, 256 * ib + 64:256 * ib + 65]))
                L.chain(vector, vE, vector.reciprocal(c_zr4[:, :],
                                                      c_zcp[:, :]))
                L.set(f"zinv{p}", vE)

            @blk.vector
            def _(vector, p=p, w=w, h=h):
                L.wait(vector, vE, f"zinv{p}")
                if p >= 1:
                    L.wait(vector, qG, f"gatm{p - 1}")
                for ib in range(4):
                    i = vector.tensor_scalar(
                        tstg[ib][:, 0:64],
                        ps[3].ap()[:, 256 * ib:256 * ib + 64],
                        c_zr4[:, ib:ib + 1], None, ALU.mult)
                L.inc(i, vE)
                L.set(f"evac{p}", vE)

            if tap is not None and tap[0].startswith("z") and tap[0][1:].isdigit() and p == int(tap[0][1:]):
                @blk.vector
                def _(vector, p=p):
                    L.wait(vector, tE, f"vmm{p}")
                    for ib in range(4):
                        i = vector.tensor_copy(
                            evt[:, 65 * ib:65 * ib + 65],
                            ps[2 + ib // 2].ap()[:, 512 * (ib % 2):
                                                 512 * (ib % 2) + 65])
                    L.inc(i, vE)
                    L.set("z0v", vE)

                @blk.gpsimd
                def _(g):
                    L.wait(g, vE, "z0v")
                    i = g.dma_start(
                        out=dbg.rearrange("(a p) t -> a p t", a=1)[0][:, 0:260],
                        in_=evt[:, 0:260])
                    L.inc(i, qG)
                    g.wait_ge(qG, L.at(qG))

            if tap is not None and tap[0] == f"zz{p}":
                @blk.gpsimd
                def _(g, p=p):
                    L.wait(g, vE, f"evac{p}")
                    i = g.dma_start(out=dbg.rearrange("(a p) t -> a p t", a=1)[0][:, 0:4], in_=c_zcp[:, :])
                    L.inc(i, qG)
                    i = g.dma_start(out=dbg.rearrange("(a p) t -> a p t", a=1)[0][:, 4:8], in_=c_zr4[:, :])
                    L.inc(i, qG)
                    for ib in range(4):
                        i = g.dma_start(out=dbg.rearrange("(a p) t -> a p t", a=1)[0][:, 8 + 64 * ib:72 + 64 * ib], in_=tstg[ib][:, 0:64])
                        L.inc(i, qG)
                    g.wait_ge(qG, L.at(qG))

            @blk.gpsimd
            def _(g, p=p, w=w, h=h):
                L.wait(g, vE, f"evac{p}")
                for ib in range(4):
                    i = g.dma_start(
                        out=atm_d.ap()[4 * w + ib][:, 64 * h:64 * h + 64],
                        in_=tstg[ib][:, 0:64])
                    L.inc(i, qG)
                L.set(f"gatm{p}", qG)

        if tap is not None and tap[0] == "av2":
            @blk.gpsimd
            def _(g):
                L.wait(g, qG, "gatm63")
                i = g.dma_start(out=dbg, in_=a_v[:, :])
                L.inc(i, qG)
                g.wait_ge(qG, L.at(qG))

        if tap is not None and tap[0] == "atm":
            @blk.gpsimd
            def _(g):
                L.wait(g, qG, "gatm63")
                for tc in range(16):
                    i = g.dma_start(
                        out=dbg.rearrange("(tc p) d -> tc p d", p=128)[tc],
                        in_=atm_d.ap()[tc])
                    L.inc(i, qG)
                g.wait_ge(qG, L.at(qG))

        for tc in range(16):
            for dc2 in range(NDC):
                it = tc * NDC + dc2
                r = it % 2

                @blk.sync
                def _(sync, tc=tc, dc2=dc2, it=it, r=r):
                    if it == 0:
                        L.wait(sync, qG, "gatm63")
                    if it >= 2:
                        L.wait(sync, tE, f"ttp{it - 2}")
                    i = sync.dma_start(out=tld[r][:, :],
                                       in_=atm_d.ap()[tc][:, 128 * dc2:
                                                          128 * dc2 + 128])
                    L.inc(i, qS)
                    L.set(f"tl{it}", qS)

                @blk.tensor
                def _(tensor, it=it, r=r):
                    L.wait(tensor, qS, f"tl{it}")
                    if it >= 2:
                        L.wait(tensor, vE, f"tcp{it - 2}")
                    i = tensor.transpose(
                        ps[2 + r].ap().bitcast(BF16)[:, 0:128],
                        tld[r][:, :], c_idB[:, :])
                    L.inc(i, tE)
                    L.set(f"ttp{it}", tE)

                @blk.vector
                def _(vector, tc=tc, dc2=dc2, it=it, r=r):
                    L.wait(vector, tE, f"ttp{it}")
                    i = vector.tensor_copy(
                        a_afm[:, dc2 * TOWN + 128 * tc:dc2 * TOWN + 128 * tc + 128],
                        ps[2 + r].ap().bitcast(BF16)[:, 0:128])
                    L.inc(i, vE)
                    L.set(f"tcp{it}", vE)
        L.mark["afm_done"] = L.at(vE)

        if tap is not None and tap[0] == "afm":
            @blk.gpsimd
            def _(g):
                g.wait_ge(vE, L.mark["afm_done"])
                for dc2 in range(NDC):
                    i = g.dma_start(
                        out=dbg.rearrange("(c p) t -> c p t", p=128)[dc2],
                        in_=a_afm[:, dc2 * TOWN:(dc2 + 1) * TOWN])
                    L.inc(i, qG)
                g.wait_ge(qG, L.at(qG))
        if stage < 6:
            _finish(nc, L, blk, qG, out, backing)
            return nc

        for ec in range(NDC):
            @blk.sync
            def _(sync, ec=ec):
                if ec == 0:
                    L.wait(sync, vE, "afm_done")
                if ec >= 2:
                    L.wait(sync, vE, f"wo_wc{ec - 2}")
                i = sync.dma_start(
                    out=wtl[ec % 2][:, :],
                    in_=woTd.rearrange("(c p) d -> c p d", p=128)[ec])
                L.inc(i, qS)
                L.set(f"wo_wl{ec}", qS)

            @blk.vector
            def _(vector, ec=ec):
                L.wait(vector, qS, f"wo_wl{ec}")
                i = vector.tensor_copy(wob[:, ec * 1024:(ec + 1) * 1024],
                                       wtl[ec % 2][:, :])
                L.inc(i, vE)
                L.set(f"wo_wc{ec}", vE)

        for ec in range(NDC):
            for th in range(2):
                tts = (2 * th, 2 * th + 1)
                for tt in tts:
                    @blk.tensor
                    def _(tensor, ec=ec, tt=tt):
                        it = ec * 4 + tt
                        if it == 0:
                            L.wait(tensor, vE, f"wo_wc{NDC - 1}")
                        if it >= 2:
                            L.wait(tensor, sE, f"oev{it - 2}")
                        for dc2 in range(NDC):
                            i = tensor.matmul(
                                ps[it % 2].ap()[:, 0:512],
                                wob[:, dc2 * 1024 + 128 * ec:
                                    dc2 * 1024 + 128 * ec + 128],
                                a_afm[:, dc2 * TOWN + 512 * tt:
                                      dc2 * TOWN + 512 * tt + 512],
                                start=(dc2 == 0), stop=(dc2 == NDC - 1))
                        L.inc(i, tE)
                        L.set(f"omm{it}", tE)

                @blk.scalar
                def _(scalar, ec=ec, tts=tts):
                    for tt in tts:
                        it = ec * 4 + tt
                        L.wait(scalar, tE, f"omm{it}")
                        if it >= 2:
                            L.wait(scalar, vE, f"ost{it - 2}")
                        dst = evt[:, :] if it % 2 == 0 else silq2[:, :]
                        i = scalar.activation(dst, ps[it % 2].ap()[:, 0:512],
                                              AF.Identity,
                                              bias=c_bo[:, ec:ec + 1])
                        L.inc(i, sE)
                        L.set(f"oev{it}", sE)

                for tt in tts:
                    @blk.sync
                    def _(sync, ec=ec, tt=tt):
                        it = ec * 4 + tt
                        if it >= 2:
                            L.wait(sync, vE, f"ost{it - 2}")
                        i = sync.dma_start(
                            out=xno[it % 2][:, :],
                            in_=xTd.rearrange("(c p) t -> c p t", p=128)[ec][
                                :, 512 + 512 * tt:1024 + 512 * tt])
                        L.inc(i, qS)
                        L.set(f"oxl{it}", qS)

                    @blk.vector
                    def _(vector, ec=ec, tt=tt):
                        it = ec * 4 + tt
                        L.wait(vector, sE, f"oev{it}")
                        L.wait(vector, qS, f"oxl{it}")
                        if it >= 2:
                            L.wait(vector, qG, f"oxs{it - 2}")
                        src = evt[:, :] if it % 2 == 0 else silq2[:, :]
                        i = vector.scalar_tensor_tensor(
                            outt[it % 2][:, :], src, c_ga[:, ec:ec + 1],
                            xno[it % 2][:, :], ALU.mult, ALU.add)
                        L.inc(i, vE)
                        L.set(f"ost{it}", vE)

                    @blk.gpsimd
                    def _(g, ec=ec, tt=tt):
                        it = ec * 4 + tt
                        L.wait(g, vE, f"ost{it}")
                        i = g.dma_start(
                            out=xnew_d.rearrange("(c p) t -> c p t", p=128)[ec][
                                :, 512 * tt:512 * tt + 512],
                            in_=outt[it % 2][:, :])
                        L.inc(i, qG)
                        L.set(f"oxs{it}", qG)
        L.mark["xnew_done"] = L.at(qG)

        if tap is not None and tap[0] == "xnew":
            @blk.gpsimd
            def _(g):
                g.wait_ge(qG, L.mark["xnew_done"])
                for dc in range(NDC):
                    i = g.dma_start(
                        out=dbg.rearrange("(c p) t -> c p t", p=128)[dc],
                        in_=xnew_d.rearrange("(c p) t -> c p t", p=128)[dc])
                    L.inc(i, qG)
                g.wait_ge(qG, L.at(qG))
        if stage < 7:
            _finish(nc, L, blk, qG, out, backing)
            return nc

        emit_ln("L2", xnew_d, TOWN, a_h2, TOWN, c_g2, c_mod, 24,
                first_wait="xnew_done")

        @blk.vector
        def _(vector):
            vector.wait_ge(vE, L.mark["L2_done_v"])
            L.chain(vector, vE, vector.memset(c_mx[:, :], 0.0))
            for dc in range(NDC):
                L.chain(vector, vE, vector.reduce_max(
                    c_mxt[:, :], a_h2[:, dc * TOWN:(dc + 1) * TOWN], axis=AX,
                    apply_absolute_value=True))
                L.chain(vector, vE, vector.tensor_max(c_mx[:, :], c_mxt[:, :],
                                                      c_mx[:, :]))
            L.set("m1_local", vE)

        @blk.gpsimd
        def _(g):
            L.wait(g, vE, "m1_local")
            i = g.dma_start(out=agm1_i.rearrange("(p x) -> p x", x=1),
                            in_=c_mx[:, :])
            L.inc(i, qG)
            L.set("m1_sent", qG)
            g.wait_ge(qG, L.at(qG))
            i = g.collective_compute("AllGather", ALU.bypass, replica_groups=RG,
                                     ins=[agm1_i.ap().opt()],
                                     outs=[agm1_o.ap().opt()])
            L.inc(i, cC)
            L.set("ag_m1", cC)

        @blk.sync
        def _(sync):
            L.wait(sync, cC, "ag_m1")
            i = sync.dma_start(out=c_mrow[0:1, :],
                               in_=agm1_o.rearrange("(a t) -> a t", a=1))
            L.inc(i, qS)
            L.set("m1_row", qS)

        @blk.vector
        def _(vector):
            L.wait(vector, qS, "m1_row")
            L.chain(vector, vE, vector.reduce_max(c_sc[0:1, 0:1], c_mrow[0:1, :],
                                                  axis=AX))
            L.set("m1_gmax", vE)

        @blk.scalar
        def _(scalar):
            L.wait(scalar, qS, "p0_small")
            L.chain(scalar, sE, scalar.activation(c_sc[0:1, 6:7],
                                                  c_clip[0:1, 0:1], AF.Exp,
                                                  scale=-1.0))
            L.chain(scalar, sE, scalar.activation(c_sc[0:1, 14:15],
                                                  c_clip[0:1, 1:2], AF.Exp,
                                                  scale=-1.0))
            L.set("sigE", sE)

        @blk.vector
        def _(vector):
            L.wait(vector, sE, "sigE")
            L.chain(vector, vE, vector.tensor_single_scalar(
                c_sc[0:1, 7:8], c_sc[0:1, 6:7], 1.0, ALU.add))
            L.chain(vector, vE, vector.reciprocal(c_sc[0:1, 1:2],
                                                  c_sc[0:1, 7:8]))
            L.chain(vector, vE, vector.tensor_single_scalar(
                c_sc[0:1, 15:16], c_sc[0:1, 14:15], 1.0, ALU.add))
            L.chain(vector, vE, vector.reciprocal(c_sc[0:1, 9:10],
                                                  c_sc[0:1, 15:16]))
            L.set("sig_done", vE)

        @blk.vector
        def _(vector):
            L.wait(vector, vE, "m1_gmax")
            L.wait(vector, vE, "sig_done")
            L.chain(vector, vE, vector.tensor_mul(c_sc[0:1, 2:3], c_sc[0:1, 0:1],
                                                  c_sc[0:1, 1:2]))
            L.chain(vector, vE, vector.tensor_mul(c_sc[0:1, 3:4], c_sc[0:1, 2:3],
                                                  c_r127[0:1, 0:1]))
            L.set("s1_done", vE)

        @blk.scalar
        def _(scalar):
            L.wait(scalar, vE, "s1_done")
            L.chain(scalar, sE, scalar.activation(c_sc[0:1, 4:5], c_sc[0:1, 3:4],
                                                  AF.Ln))
            L.chain(scalar, sE, scalar.activation(c_sc[0:1, 5:6], c_sc[0:1, 4:5],
                                                  AF.Exp, scale=-1.0))
            L.set("inv1_done", sE)

        @blk.tensor
        def _(tensor):
            L.wait(tensor, sE, "inv1_done")
            tensor.matmul(ps[0].ap()[:, 0:1], c_1rf[0:1, :], c_sc[0:1, 5:6],
                          start=True, stop=True)
            i = tensor.matmul(ps[0].ap()[:, 1:2], c_1rf[0:1, :], c_sc[0:1, 3:4],
                              start=True, stop=True)
            L.inc(i, tE)
            L.set("rep1", tE)

        @blk.vector
        def _(vector):
            L.wait(vector, tE, "rep1")
            L.chain(vector, vE, vector.tensor_copy(c_inv1[:, :],
                                                   ps[0].ap()[:, 0:1]))
            L.chain(vector, vE, vector.tensor_copy(c_s1r[:, :],
                                                   ps[0].ap()[:, 1:2]))
            L.set("rep1v", vE)

        for dc in range(NDC):
            @blk.vector
            def _(vector, dc=dc):
                if dc == 0:
                    L.wait(vector, vE, "rep1v")
                L.chain(vector, vE, vector.tensor_scalar(
                    vtmp[:, :], a_h2[:, dc * TOWN:(dc + 1) * TOWN],
                    c_inv1[:, 0:1], None, ALU.mult))
                L.chain(vector, vE, vector.tensor_copy(q8t[:, :], vtmp[:, :]))
                i = vector.tensor_copy(a_cx[:, dc * TOWN:(dc + 1) * TOWN],
                                       q8t[:, :])
                L.inc(i, vE)
                L.set(f"cx{dc}", vE)
        L.mark["cx_done"] = L.at(vE)

        if tap is not None and tap[0] == "cx":
            @blk.gpsimd
            def _(g):
                g.wait_ge(vE, L.mark["cx_done"])
                for dc in range(NDC):
                    i = g.dma_start(
                        out=dbg.rearrange("(c p) t -> c p t", p=128)[dc],
                        in_=a_cx[:, dc * TOWN:(dc + 1) * TOWN])
                    L.inc(i, qG)
                g.wait_ge(qG, L.at(qG))
        if stage < 8:
            _finish(nc, L, blk, qG, out, backing)
            return nc

        @blk.gpsimd
        def _(g):
            L.wait(g, cC, "ag_sc")
            sf = scflat.rearrange("(a t) -> a t", a=1)
            so = agsc_o.rearrange("(a t) -> a t", a=1)
            for r_ in range(8):
                i = g.dma_start(out=sf[0:1, r_ * ICORE:(r_ + 1) * ICORE],
                                in_=so[0:1, r_ * 832:r_ * 832 + ICORE])
                L.inc(i, qG)
                i = g.dma_start(out=sf[0:1, IFF + r_ * ICORE:IFF + (r_ + 1) * ICORE],
                                in_=so[0:1, r_ * 832 + ICORE:r_ * 832 + 2 * ICORE])
                L.inc(i, qG)
                i = g.dma_start(out=sf[0:1, 2 * IFF + r_ * 128:2 * IFF + (r_ + 1) * 128],
                                in_=so[0:1, r_ * 832 + 704:r_ * 832 + 832])
                L.inc(i, qG)
            L.set("scflat_done", qG)

        @blk.sync
        def _(sync):
            L.wait(sync, qG, "scflat_done")
            L.wait(sync, vE, "L2_done_v")
            i = sync.dma_start(out=sc22[:, :],
                               in_=scflat.rearrange("(q p) -> q p", q=52,
                                                    p=128)[0:22])
            L.inc(i, qS)
            L.set("sg22", qS)

        @blk.tensor
        def _(tensor):
            L.wait(tensor, qS, "sg22")
            i = tensor.transpose(ps[1].ap()[:, 0:22], sc22[:, :],
                                 c_idF[0:22, 0:22])
            L.inc(i, tE)
            L.set("sgT", tE)

        @blk.vector
        def _(vector):
            L.wait(vector, tE, "sgT")
            L.chain(vector, vE, vector.tensor_copy(c_sg[:, :],
                                                   ps[1].ap()[:, 0:22]))
            L.set("sgv", vE)

        @blk.sync
        def _(sync):
            L.wait(sync, vE, "sgv")
            i = sync.dma_start(
                out=sc22[:, :],
                in_=scflat.rearrange("(q p) -> q p", q=52, p=128)[22:44])
            L.inc(i, qS)
            L.set("su22", qS)

        @blk.tensor
        def _(tensor):
            L.wait(tensor, qS, "su22")
            i = tensor.transpose(ps[1].ap()[:, 0:22], sc22[:, :],
                                 c_idF[0:22, 0:22])
            L.inc(i, tE)
            L.set("suT", tE)

        @blk.vector
        def _(vector):
            L.wait(vector, tE, "suT")
            L.chain(vector, vE, vector.tensor_copy(c_su[:, :],
                                                   ps[1].ap()[:, 0:22]))
            L.set("suv", vE)

        @blk.sync
        def _(sync):
            L.wait(sync, vE, "suv")
            i = sync.dma_start(
                out=sc8r[:, :],
                in_=scflat.rearrange("(q p) -> q p", q=52, p=128)[44:52])
            L.inc(i, qS)
            L.set("sd8", qS)

        @blk.tensor
        def _(tensor):
            L.wait(tensor, qS, "sd8")
            i = tensor.transpose(ps[1].ap()[:, 0:8], sc8r[:, :], c_idF[0:8, 0:8])
            L.inc(i, tE)
            L.set("sdT", tE)

        @blk.vector
        def _(vector):
            L.wait(vector, tE, "sdT")
            L.chain(vector, vE, vector.tensor_copy(c_sd[:, :],
                                                   ps[1].ap()[:, 0:8]))
            L.chain(vector, vE, vector.tensor_scalar(c_sgc[:, :], c_sg[:, :],
                                                     c_s1r[:, 0:1], None,
                                                     ALU.mult))
            L.chain(vector, vE, vector.tensor_scalar(c_suc[:, :], c_su[:, :],
                                                     c_s1r[:, 0:1], None,
                                                     ALU.mult))
            L.wait(vector, qG, "m1_sent")
            L.chain(vector, vE, vector.memset(c_mx[:, :], 0.0))
            L.set("ffnsc_done", vE)

        def wtile_dmas(sync, agout, r, ic, which):
            g0 = 128 * ic
            runs = []
            while g0 < 128 * (ic + 1):
                s_ = g0 // ICORE
                l0 = g0 % ICORE
                w_ = min(ICORE - l0, 128 * (ic + 1) - g0)
                runs.append((s_, l0, g0 - 128 * ic, w_))
                g0 += w_
            for s_, l0, off, w_ in runs:
                i = sync.dma_start(
                    out=(wgt if which == "g" else wut)[r].rearrange(
                        "p (c i) -> p c i", c=NDC)[:, :, off:off + w_],
                    in_=agout.rearrange("(s c p i) -> s p c i", s=8, c=NDC,
                                        p=128)[s_][:, :, l0:l0 + w_])
                L.inc(i, qS)

        for ic in range(NIC):
            r = ic % 2

            @blk.sync
            def _(sync, ic=ic, r=r):
                if ic == 0:
                    sync.wait_ge(cC, L.mark["ag_wg"])
                    sync.wait_ge(cC, L.mark["ag_wu"])
                    L.wait(sync, vE, "L2_done_v")
                if ic >= 2:
                    L.wait(sync, tE, f"gu_mm{ic - 2}")
                wtile_dmas(sync, agwg_o, r, ic, "g")
                wtile_dmas(sync, agwu_o, r, ic, "u")
                L.set(f"gu_wl{ic}", qS)

            for tt in range(4):
                @blk.tensor
                def _(tensor, ic=ic, r=r, tt=tt):
                    if tt == 0:
                        L.wait(tensor, qS, f"gu_wl{ic}")
                        if ic == 0:
                            tensor.wait_ge(vE, L.mark["cx_done"])
                            tensor.wait_ge(vE, L.mark["ffnsc_done"])
                    if ic >= 1:
                        L.wait(tensor, vE, f"hrt{ic - 1}_{tt}")
                    for ec in range(NDC):
                        tensor.matmul(
                            ps[tt].ap()[:, 0:512],
                            wgt[r][:, ec * 128:ec * 128 + 128],
                            a_cx[:, ec * TOWN + 512 * tt:
                                 ec * TOWN + 512 * tt + 512],
                            start=(ec == 0), stop=(ec == NDC - 1))
                    for ec in range(NDC):
                        i = tensor.matmul(
                            ps[tt].ap()[:, 512:1024],
                            wut[r][:, ec * 128:ec * 128 + 128],
                            a_cx[:, ec * TOWN + 512 * tt:
                                 ec * TOWN + 512 * tt + 512],
                            start=(ec == 0), stop=(ec == NDC - 1))
                    L.inc(i, tE)
                    L.set(f"gu_mmt{ic}_{tt}", tE)
                    if tt == 3:
                        L.mark[f"gu_mm{ic}"] = L.at(tE)

            for hv in range(2):
                @blk.scalar
                def _(scalar, ic=ic, hv=hv):
                    for tt in (2 * hv, 2 * hv + 1):
                        L.wait(scalar, tE, f"gu_mmt{ic}_{tt}")
                        if tt >= 2:
                            L.wait(scalar, vE, f"hrt{ic}_{tt - 2}")
                        i = scalar.activation(
                            silq[:, :] if tt % 2 == 0 else silq2[:, :],
                            ps[tt].ap()[:, 0:512], AF.Silu,
                            scale=c_sgc[:, ic:ic + 1])
                        L.inc(i, sE)
                        L.set(f"sil{ic}_{tt}", sE)

                @blk.vector
                def _(vector, ic=ic, hv=hv):
                    for tt in (2 * hv, 2 * hv + 1):
                        L.wait(vector, sE, f"sil{ic}_{tt}")
                        i = vector.scalar_tensor_tensor(
                            a_hr[:, ic * TOWN + 512 * tt:
                                 ic * TOWN + 512 * tt + 512],
                            ps[tt].ap()[:, 512:1024], c_suc[:, ic:ic + 1],
                            silq[:, :] if tt % 2 == 0 else silq2[:, :],
                            ALU.mult, ALU.mult)
                        L.inc(i, vE)
                        L.set(f"hrt{ic}_{tt}", vE)

            @blk.vector
            def _(vector, ic=ic):
                L.chain(vector, vE, vector.reduce_max(
                    c_mxt[:, :], a_hr[:, ic * TOWN:(ic + 1) * TOWN], axis=AX,
                    apply_absolute_value=True))
                L.chain(vector, vE, vector.tensor_max(c_mx[:, :], c_mxt[:, :],
                                                      c_mx[:, :]))
                L.set(f"hr{ic}", vE)
        L.mark["hr_done"] = L.at(vE)

        @blk.scalar
        def _(scalar):
            L.wait(scalar, vE, f"hrt{NIC - 1}_3")
            pre6(scalar, 2)

        @blk.gpsimd
        def _(g):
            g.wait_ge(vE, L.mark["hr_done"])
            i = g.dma_start(out=agm2_i.rearrange("(p x) -> p x", x=1),
                            in_=c_mx[:, :])
            L.inc(i, qG)
            g.wait_ge(qG, L.at(qG))
            i = g.collective_compute("AllGather", ALU.bypass, replica_groups=RG,
                                     ins=[agm2_i.ap().opt()],
                                     outs=[agm2_o.ap().opt()])
            L.inc(i, cC)
            L.set("ag_m2", cC)

        @blk.sync
        def _(sync):
            L.wait(sync, cC, "ag_m2")
            i = sync.dma_start(out=c_mrow[0:1, :],
                               in_=agm2_o.rearrange("(a t) -> a t", a=1))
            L.inc(i, qS)
            L.set("m2_row", qS)

        @blk.vector
        def _(vector):
            L.wait(vector, qS, "m2_row")
            L.wait(vector, vE, "sig_done")
            L.chain(vector, vE, vector.reduce_max(c_sc[0:1, 8:9],
                                                  c_mrow[0:1, :], axis=AX))
            L.chain(vector, vE, vector.tensor_mul(c_sc[0:1, 10:11],
                                                  c_sc[0:1, 8:9],
                                                  c_sc[0:1, 9:10]))
            L.chain(vector, vE, vector.tensor_mul(c_sc[0:1, 11:12],
                                                  c_sc[0:1, 10:11],
                                                  c_r127[0:1, 0:1]))
            L.set("s2_done", vE)

        @blk.scalar
        def _(scalar):
            L.wait(scalar, vE, "s2_done")
            L.chain(scalar, sE, scalar.activation(c_sc[0:1, 12:13],
                                                  c_sc[0:1, 11:12], AF.Ln))
            L.chain(scalar, sE, scalar.activation(c_sc[0:1, 13:14],
                                                  c_sc[0:1, 12:13], AF.Exp,
                                                  scale=-1.0))
            L.set("inv2_done", sE)

        @blk.tensor
        def _(tensor):
            L.wait(tensor, sE, "inv2_done")
            tensor.matmul(ps[0].ap()[:, 0:1], c_1rf[0:1, :], c_sc[0:1, 13:14],
                          start=True, stop=True)
            i = tensor.matmul(ps[0].ap()[:, 1:2], c_1rf[0:1, :], c_sc[0:1, 11:12],
                              start=True, stop=True)
            L.inc(i, tE)
            L.set("rep2", tE)

        @blk.vector
        def _(vector):
            L.wait(vector, tE, "rep2")
            L.chain(vector, vE, vector.tensor_copy(c_inv2[:, :],
                                                   ps[0].ap()[:, 0:1]))
            L.chain(vector, vE, vector.tensor_copy(c_s2r[:, :],
                                                   ps[0].ap()[:, 1:2]))
            L.chain(vector, vE, vector.tensor_scalar(c_sdc[:, :], c_sd[:, :],
                                                     c_s2r[:, 0:1], None,
                                                     ALU.mult))
            L.set("rep2v", vE)

        for ic in range(NIC):
            @blk.vector
            def _(vector, ic=ic):
                if ic == 0:
                    L.wait(vector, vE, "rep2v")
                L.chain(vector, vE, vector.tensor_scalar(
                    vtmp[:, :], a_hr[:, ic * TOWN:(ic + 1) * TOWN],
                    c_inv2[:, 0:1], None, ALU.mult))
                L.chain(vector, vE, vector.tensor_copy(q8t[:, :], vtmp[:, :]))
                i = vector.tensor_copy(a_hr[:, ic * TOWN:(ic + 1) * TOWN],
                                       q8t[:, :])
                L.inc(i, vE)
                L.set(f"ch{ic}", vE)
        L.mark["ch_done"] = L.at(vE)
        if stage < 9:
            _finish(nc, L, blk, qG, out, backing)
            return nc

        for ec in range(NDC):
            r = ec % 2

            @blk.sync
            def _(sync, ec=ec, r=r):
                if ec == 0:
                    sync.wait_ge(cC, L.mark["ag_wd"])
                    L.wait(sync, vE, "L2_done_v")
                if ec >= 2:
                    L.wait(sync, tE, f"d_mm{ec - 2}_3")
                i = sync.dma_start(
                    out=wdt[r].rearrange("p (ic e) -> p ic e", ic=NIC),
                    in_=agwd_o.rearrange("(s ic p e) -> s p ic e", s=8, ic=NIC,
                                         p=128)[ec])
                L.inc(i, qS)
                L.set(f"d_wl{ec}", qS)

            for th in range(2):
                tts = (2 * th, 2 * th + 1)
                for tt in tts:
                    @blk.tensor
                    def _(tensor, ec=ec, tt=tt, r=r):
                        it = ec * 4 + tt
                        if tt == 0:
                            L.wait(tensor, qS, f"d_wl{ec}")
                        if it == 0:
                            tensor.wait_ge(vE, L.mark["ch_done"])
                        if it >= 2:
                            L.wait(tensor, sE, f"d_ev{it - 2}")
                        for ic in range(NIC):
                            i = tensor.matmul(
                                ps[it % 2].ap()[:, 0:512],
                                wdt[r][:, ic * 128:ic * 128 + 128],
                                a_hr[:, ic * TOWN + 512 * tt:
                                     ic * TOWN + 512 * tt + 512],
                                start=(ic == 0), stop=(ic == NIC - 1))
                        L.inc(i, tE)
                        L.set(f"d_mm{ec}_{tt}", tE)

                @blk.scalar
                def _(scalar, ec=ec, tts=tts):
                    for tt in tts:
                        it = ec * 4 + tt
                        L.wait(scalar, tE, f"d_mm{ec}_{tt}")
                        if it >= 2:
                            L.wait(scalar, vE, f"d_st{it - 2}")
                        dst = evt[:, :] if it % 2 == 0 else silq2[:, :]
                        i = scalar.activation(dst, ps[it % 2].ap()[:, 0:512],
                                              AF.Identity,
                                              scale=c_sdc[:, ec:ec + 1])
                        L.inc(i, sE)
                        L.set(f"d_ev{it}", sE)

                for tt in tts:
                    @blk.sync
                    def _(sync, ec=ec, tt=tt):
                        it = ec * 4 + tt
                        if it >= 2:
                            L.wait(sync, vE, f"d_st{it - 2}")
                        i = sync.dma_start(
                            out=xno[it % 2][:, :],
                            in_=xnew_d.rearrange("(c p) t -> c p t", p=128)[ec][
                                :, 512 * tt:512 * tt + 512])
                        L.inc(i, qS)
                        L.set(f"d_xl{it}", qS)

                    @blk.vector
                    def _(vector, ec=ec, tt=tt):
                        it = ec * 4 + tt
                        L.wait(vector, sE, f"d_ev{it}")
                        L.wait(vector, qS, f"d_xl{it}")
                        if it >= 2:
                            L.wait(vector, qG, f"d_os{it - 2}")
                        src_ = evt[:, :] if it % 2 == 0 else silq2[:, :]
                        i = vector.scalar_tensor_tensor(
                            outt[it % 2][:, :], src_, c_gb[:, ec:ec + 1],
                            xno[it % 2][:, :], ALU.mult, ALU.add)
                        L.inc(i, vE)
                        L.set(f"d_st{it}", vE)

                    @blk.gpsimd
                    def _(g, ec=ec, tt=tt):
                        it = ec * 4 + tt
                        L.wait(g, vE, f"d_st{it}")
                        i = g.dma_start(
                            out=out.rearrange("(c p) t -> c p t", p=128)[ec][
                                :, 512 * tt:512 * tt + 512],
                            in_=outt[it % 2][:, :])
                        L.inc(i, qG)
                        L.set(f"d_os{it}", qG)

        @blk.gpsimd
        def _(g):
            g.wait_ge(qG, L.at(qG))

        return nc



def _finish(nc, L, blk, qG, out, backing):
    @blk.gpsimd
    def _(g):
        av = backing.ap().bitcast(F32)
        for dc in range(NDC):
            i = g.dma_start(out=out.rearrange("(c p) t -> c p t", p=128)[dc],
                            in_=av[:, 0:TOWN])
            L.inc(i, qG)
        g.wait_ge(qG, L.at(qG))


def _stage_host(inputs, core):
    b, s = core // 2, core % 2
    x = np.asarray(inputs["L0_attn_x"])
    g0 = 2048 * s
    fr = np.zeros((FR, E), np.float32)
    lo, hi = g0 - 512, g0 + 2560
    cl, ch = max(lo, 0), min(hi, S)
    fr[cl - lo:ch - lo] = x[b, cl:ch]
    d = {
        "xT": np.ascontiguousarray(fr.T),
        "wqT": np.ascontiguousarray(np.asarray(inputs["L0_attn_wq"]).T),
        "wkT": np.ascontiguousarray(np.asarray(inputs["L0_attn_wk"]).T),
        "woT": np.ascontiguousarray(np.asarray(inputs["L0_attn_wo"]).T),
        "cTd": np.ascontiguousarray(
            np.asarray(inputs["L0_c"]).T.reshape(NDC, 128, B)
            .transpose(1, 0, 2).reshape(128, NDC * B)),
        "clipx": np.asarray(inputs["L0_ffn_clip_x"], np.float32).reshape(1, 1),
        "clipd": np.asarray(inputs["L0_ffn_clip_down"], np.float32).reshape(1, 1),
        "wmods": np.ascontiguousarray(
            np.asarray(inputs["L0_wmod"])[:, 768 * core:768 * (core + 1)]),
    }
    wv = np.asarray(inputs["L0_attn_wv"])
    wvT = np.zeros((E, DA), np.float32)
    bvA = np.zeros((1, DA), np.float32)
    bv = np.asarray(inputs["L0_attn_bv"])
    for h in range(H):
        wvT[:, 65 * h:65 * h + 64] = wv[64 * h:64 * h + 64, :].T
        bvA[0, 65 * h:65 * h + 64] = bv[64 * h:64 * h + 64]
        bvA[0, 65 * h + 64] = 1.0
    d["wvTa"] = wvT
    d["bva"] = bvA
    pv = np.full((128, DA), -1.0, np.float32)
    pv[:, 64::65] = 1.0
    d["padvd"] = pv.astype(BF)
    for nm, src in (("bq2", "L0_attn_bq"), ("bk2", "L0_attn_bk"),
                    ("bo2", "L0_attn_bo")):
        d[nm] = np.ascontiguousarray(
            np.asarray(inputs[src]).reshape(NDC, 128).T)
    wg = np.asarray(inputs["L0_ffn_w_gate"])
    wu = np.asarray(inputs["L0_ffn_w_up"])
    wd = np.asarray(inputs["L0_ffn_w_down"])
    d["wgN"] = np.ascontiguousarray(wg[ICORE * core:ICORE * (core + 1)])
    d["wuN"] = np.ascontiguousarray(wu[ICORE * core:ICORE * (core + 1)])
    d["wgTs"] = np.ascontiguousarray(wg.T[:, ICORE * core:ICORE * (core + 1)])
    d["wuTs"] = np.ascontiguousarray(wu.T[:, ICORE * core:ICORE * (core + 1)])
    d["wdN"] = np.ascontiguousarray(wd[128 * core:128 * (core + 1)])
    pk = 0 if s == 0 else 2560
    ptc = 0 if s == 0 else 20
    woff = [1536, 1536, 1536, 1536]
    if s == 0:
        woff[0] = 0
    else:
        woff[3] = 1024
    d["padsi"] = np.array([[pk, ptc * DA, b, ptc] + woff], np.uint32)
    p = np.arange(128)[:, None]
    i_ = np.arange(512)[None, :]
    m0 = np.ones((128, 2048), np.float32)
    m3 = np.ones((128, 2048), np.float32)
    if s == 1:
        for jc in range(4):
            m0[:, 512 * jc:512 * jc + 512] = (i_ <= 128 * jc + p)
    if s == 0:
        for k in range(4):
            m3[:, 512 * k:512 * k + 512] = (i_ >= 128 * k + p)
    d["m0d"] = m0.astype(BF)
    d["m3d"] = m3.astype(BF)
    q = np.arange(128)[None, :]
    d["t1d"] = (q <= p).astype(BF)
    d["t2d"] = (q >= p).astype(BF)
    d["idnB"] = np.eye(128).astype(BF)
    d["idnF"] = np.eye(128).astype(np.float32)
    d["lnod"] = np.full((128, 128), 1.0 / 1024.0).astype(BF)
    d["onesRFd"] = np.ones((1, 128), np.float32)
    return d


def kernel(**inputs):
    nc = build()
    in_maps = [_stage_host(inputs, c) for c in range(8)]
    res = run_bass_kernel_spmd(nc, in_maps, core_ids=list(range(8)))
    outp = np.zeros((B, S, E), np.float32)
    for c in range(8):
        b, s = c // 2, c % 2
        outp[b, 2048 * s:2048 * (s + 1), :] = res.results[c]["out"].T
    return outp


# version 19
# speedup vs baseline: 1.1872x; 1.1872x over previous
import numpy as np
import ml_dtypes
import concourse.bass as bass
import concourse.mybir as mybir
import concourse.expressions_rust as er
from concourse.bass_utils import run_bass_kernel_spmd

F32 = mybir.dt.float32
BF16 = mybir.dt.bfloat16
I8 = mybir.dt.int8
U32 = mybir.dt.uint32
AF = mybir.ActivationFunctionType
ALU = mybir.AluOpType
AX = mybir.AxisListType.X
BF = ml_dtypes.bfloat16

B, S, E, H, HD = 4, 4096, 1024, 16, 64
TOWN, FR = 2048, 3072
NDC = E // 128
IFF, ICORE, NIC = 2816, 352, 22
DA = H * (HD + 1)
NTC = FR // 128
USABLE = 208832
RG = [list(range(8))]


class Led:
    def __init__(self):
        self.c = {}
        self.mark = {}

    def inc(self, instr, sem, k=None):
        if k is None:
            k = 16 if sem.name in ("qS", "qG") else 1
        instr.then_inc(sem, k)
        self.c[sem.name] = self.c.get(sem.name, 0) + k
        return self.c[sem.name]

    def at(self, sem):
        return self.c.get(sem.name, 0)

    def set(self, name, sem):
        self.mark[name] = self.at(sem)

    def wait(self, eng, sem, name):
        eng.wait_ge(sem, self.mark[name])

    def chain(self, eng, sem, instr):
        instr.then_inc(sem, 1)
        self.c[sem.name] = self.c.get(sem.name, 0) + 1
        eng.wait_ge(sem, self.c[sem.name])


def build(stage=99, tap=None):
    nc = bass.Bass("TRN2", target_bir_lowering=False)
    L = Led()

    def din(name, shape, dt=F32):
        return nc.dram_tensor(name, shape, dt, kind="ExternalInput").ap()

    xTd = din("xT", [E, FR])
    wqTd = din("wqT", [E, E])
    wkTd = din("wkT", [E, E])
    wvTd = din("wvTa", [E, DA])
    woTd = din("woT", [E, E])
    bq2d = din("bq2", [128, NDC])
    bk2d = din("bk2", [128, NDC])
    bo2d = din("bo2", [128, NDC])
    bvad = din("bva", [1, DA])
    cTd = din("cTd", [128, NDC * B])
    wmodsd = din("wmods", [E, 768])
    clipxd = din("clipx", [1, 1])
    clipdd = din("clipd", [1, 1])
    wgNd = din("wgN", [ICORE, E])
    wgTd = din("wgTs", [E, ICORE])
    wuNd = din("wuN", [ICORE, E])
    wuTd = din("wuTs", [E, ICORE])
    wdNd = din("wdN", [128, IFF])
    padsd = din("padsi", [1, 8], U32)
    padvd = din("padvd", [128, DA], BF16)
    m0d = din("m0d", [128, 2048], BF16)
    m3d = din("m3d", [128, 2048], BF16)
    t1d = din("t1d", [128, 128], BF16)
    t2d = din("t2d", [128, 128], BF16)
    idnBd = din("idnB", [128, 128], BF16)
    idnFd = din("idnF", [128, 128], F32)
    lnod = din("lnod", [128, 128], BF16)
    onesRFd = din("onesRFd", [1, 128], F32)

    out = nc.dram_tensor("out", [E, TOWN], F32, kind="ExternalOutput").ap()
    dbg = None
    if tap is not None:
        dbg = nc.dram_tensor("dbg", list(tap[1]), F32, kind="ExternalOutput").ap()

    dt_ = nc.dram_tensor
    agmod_i = dt_("agmod_i", [B * 768], F32)
    agmod_o = dt_("agmod_o", [8 * B * 768], F32)
    modrow_d = dt_("modrow_d", [6 * E], F32)
    agwg_i = dt_("agwg_i", [E, ICORE], BF16)
    agwg_o = dt_("agwg_o", [8 * E * ICORE], BF16)
    agwu_i = dt_("agwu_i", [E, ICORE], BF16)
    agwu_o = dt_("agwu_o", [8 * E * ICORE], BF16)
    agwd_i = dt_("agwd_i", [IFF, 128], BF16)
    agwd_o = dt_("agwd_o", [8 * IFF * 128], BF16)
    agsc_i = dt_("agsc_i", [832], F32)
    agsc_o = dt_("agsc_o", [8 * 832], F32)
    scflat = dt_("scflat", [2 * IFF + E], F32)
    agm1_i = dt_("agm1_i", [128], F32)
    agm1_o = dt_("agm1_o", [8 * 128], F32)
    agm2_i = dt_("agm2_i", [128], F32)
    agm2_o = dt_("agm2_o", [8 * 128], F32)
    xnew_d = dt_("xnew_d", [E, TOWN], F32)
    atm_d = dt_("atm_d", [16, 128, E], BF16)
    qhd = dt_("qhd", [E, TOWN], BF16)
    qld = dt_("qld", [E, TOWN], BF16)
    khd = dt_("khd", [E, FR], BF16)
    kld = dt_("kld", [E, FR], BF16)

    backing = nc.alloc_sbuf_tensor("arena", [128, USABLE], mybir.dt.uint8)
    base_addr = nc.lookup_mloc(backing).addr

    def place(name, shape, dtype, off):
        nbytes = int(np.prod(shape[1:])) * mybir.dt.size(dtype)
        assert off % 32 == 0 and off + nbytes <= USABLE, (name, off, nbytes)
        return nc.alloc_sbuf_tensor_at(name, list(shape), dtype,
                                       offset=base_addr + off)

    o = [0]

    def cplace(name, shape, dtype):
        nbytes = int(np.prod(shape[1:])) * mybir.dt.size(dtype)
        h = place(name, shape, dtype, o[0])
        o[0] = (o[0] + nbytes + 31) // 32 * 32
        assert o[0] <= 11456, (name, o[0])
        return h

    c_bq = cplace("c_bq", [128, NDC], F32)
    c_bk = cplace("c_bk", [128, NDC], F32)
    c_bo = cplace("c_bo", [128, NDC], F32)
    c_eps = cplace("c_eps", [128, 1], F32)
    c_idB = cplace("c_idB", [128, 128], BF16)
    c_idF = cplace("c_idF", [128, 128], F32)
    c_ln1 = cplace("c_ln1", [128, 128], BF16)
    c_1rf = cplace("c_1rf", [1, 128], F32)
    c_1rb = cplace("c_1rb", [1, 128], BF16)
    c_t1 = cplace("c_t1", [128, 128], BF16)
    c_t2 = cplace("c_t2", [128, 128], BF16)
    c_bva = cplace("c_bva", [1, DA], BF16)
    c_pads = cplace("c_pads", [1, 8], U32)
    c_clip = cplace("c_clip", [1, 2], F32)
    c_sg = cplace("c_sg", [128, NIC], F32)
    c_su = cplace("c_su", [128, NIC], F32)
    c_sd = cplace("c_sd", [128, NDC], F32)
    c_sgc = cplace("c_sgc", [128, NIC], F32)
    c_suc = cplace("c_suc", [128, NIC], F32)
    c_sdc = cplace("c_sdc", [128, NDC], F32)
    c_inv1 = cplace("c_inv1", [128, 1], F32)
    c_inv2 = cplace("c_inv2", [128, 1], F32)
    c_s1r = cplace("c_s1r", [128, 1], F32)
    c_s2r = cplace("c_s2r", [128, 1], F32)
    c_sc = cplace("c_sc", [1, 16], F32)
    c_mod = cplace("c_mod", [128, 48], F32)
    c_ga = cplace("c_ga", [128, 8], F32)
    c_gb = cplace("c_gb", [128, 8], F32)
    c_g1 = cplace("c_g1", [128, 8], F32)
    c_g2 = cplace("c_g2", [128, 8], F32)
    c_csl = cplace("c_csl", [128, NDC * B], F32)
    c_csi = cplace("c_csi", [128, NDC * B], F32)
    c_mx = cplace("c_mx", [128, 1], F32)
    c_mxt = cplace("c_mxt", [128, 1], F32)
    c_zr = cplace("c_zr", [128, 1], F32)
    c_zr4 = cplace("c_zr4", [128, 4], F32)
    c_zcp = cplace("c_zcp", [128, 4], F32)
    c_negC = cplace("c_negC", [128, 1], F32)
    c_q25 = cplace("c_q25", [128, 1], F32)
    c_padv = cplace("c_padv", [128, DA], BF16)
    c_km1 = cplace("c_km1", [128, 512], BF16)
    c_k0 = cplace("c_k0", [128, 512], BF16)
    c_127 = cplace("c_127", [128, 1], F32)
    c_r127 = cplace("c_r127", [128, 1], F32)
    c_sgn = cplace("c_sgn", [128, 8], F32)
    c_mag = cplace("c_mag", [128, 8], F32)
    c_mag2 = cplace("c_mag2", [128, 8], F32)
    c_mag3 = cplace("c_mag3", [128, 8], F32)

    R1, R2, R3, R4, R5 = 11456, 44224, 76992, 109760, 158912
    xt = [place("xt0", [128, NDC * 512], F32, R1),
          place("xt1", [128, NDC * 512], F32, R1 + 16384)]
    kt2 = [place(f"kt2_{i}", [128, 2048], BF16, R1 + 4096 * i) for i in range(2)]
    qt2 = [place(f"qt2_{i}", [128, 1024], BF16, R1 + 8192 + 2048 * i) for i in range(2)]
    pstf = [place(f"pstf{i}", [128, 512], F32, R1 + 12288 + 2048 * i) for i in range(2)]
    psth = [place(f"psth{i}", [128, 512], BF16, R1 + 16384 + 1024 * i) for i in range(2)]
    pstl = [place(f"pstl{i}", [128, 512], BF16, R1 + 18432 + 1024 * i) for i in range(2)]
    vtmp = place("vtmp", [128, 2048], F32, R1)
    q8t = place("q8t", [128, 2048], I8, R1 + 8192)
    wgt = [place(f"wgt{i}", [128, NDC * 128], BF16, R2 + 2048 * i) for i in range(2)]
    wut = [place(f"wut{i}", [128, NDC * 128], BF16, R2 + 4096 + 2048 * i) for i in range(2)]
    wdt = [place(f"wdt{i}", [128, NIC * 128], BF16, R1 + 12288 + 5632 * i) for i in range(2)]
    sc22 = place("sc22", [22, 128], F32, R1 + 28672)
    sc8r = place("sc8r", [8, 128], F32, R1 + 29248)

    xb_t = place("xb_t", [128, NDC * 512], BF16, R2)
    xq_t = place("xq_t", [128, NDC * 512], BF16, R2 + 8192)
    muS = place("muS", [128, 512], F32, R2 + 16384)
    mu2 = place("mu2", [128, 512], F32, R2 + 18432)
    varS = place("varS", [128, 512], F32, R2 + 20480)
    lnvS = place("lnvS", [128, 512], F32, R2 + 22528)
    rstdS = place("rstdS", [128, 512], F32, R2 + 24576)
    mrtS = place("mrtS", [128, 512], F32, R2 + 26624)
    tmpA = place("tmpA", [128, 512], F32, R2 + 28672)
    tmpB = place("tmpB", [128, 512], F32, R2 + 30720)
    wtl = [place(f"wtl{i}", [128, 1024], F32, R2 + 4096 * i) for i in range(2)]
    wqbs = [place(f"wqbs{i}", [128, NDC * 128], BF16, R2 + 8192 + 2048 * i) for i in range(2)]
    silq = place("silq", [128, 512], F32, R2 + 16384)
    silq2 = place("silq2", [128, 512], F32, R2 + 18432)
    xno = [place(f"xno{i}", [128, 512], F32, R2 + 20480 + 2048 * i) for i in range(2)]
    outt = [place(f"outt{i}", [128, 512], F32, R2 + 24576 + 2048 * i) for i in range(2)]
    evt = place("evt", [128, 512], F32, R2 + 28672)
    tld = [place(f"tld{i}", [128, 1024], BF16, R2 + 2048 * i) for i in range(2)]

    sc48 = place("sc48", [48, 128], F32, R3)
    wvb = place("wvb", [128, NDC * DA], BF16, R3)
    wob = place("wob", [128, NDC * 1024], BF16, R5)
    a_cx = place("a_cx", [128, NDC * TOWN], BF16, R3)

    a_h1 = place("a_h1", [128, NDC * FR], BF16, R4)
    aE = [place("aE0", [128, 12 * 512], BF16, R4),
          place("aE1", [128, 12 * 512], BF16, R4 + 12288)]
    aEm = place("aEm", [128, 3072], BF16, R4 + 24576)
    c_mm = place("c_mm", [128, 2048], BF16, R4 + 30720)
    tstg = [place(f"tstg{i}", [128, 128], BF16, R4 + 34816 + 256 * i) for i in range(4)]
    a_afm = place("a_afm", [128, NDC * TOWN], BF16, R3)
    a_h2 = place("a_h2", [128, NDC * TOWN], F32, R4)
    a_hr = place("a_hr", [128, NIC * TOWN], BF16, R4)
    wpf = place("wpf", [128, 3072], F32, R4)
    wps = place("wps", [128, IFF], F32, R4 + 12288)
    wp8 = place("wp8", [128, IFF], I8, R4 + 23552)
    wpq = place("wpq", [128, IFF], BF16, R4 + 26368)
    wrep = place("wrep", [128, ICORE], F32, R4 + 32000)
    wsc = place("wsc", [128, 8], F32, R4 + 33408)
    wrow = place("wrow", [1, ICORE], F32, R4 + 33440)
    winv = place("winv", [1, ICORE], F32, R4 + 36288)
    wstg = [place(f"wstg{i}", [128, 128], BF16, R4 + 37696 + 256 * i) for i in range(2)]

    wmf = place("wmf", [128, NDC * 768], F32, R5)
    modp = place("modp", [4, 768], F32, R5 + 36864)
    c_bvaf = place("c_bvaf", [1, DA], F32, R5 + 40960)
    c_mrow = place("c_mrow", [1, 1024], F32, R5 + 45056)
    a_v = place("a_v", [128, NTC * DA], BF16, R5)

    ps = [nc.alloc_psum_tensor(f"pb{i}", [128, 1024], F32) for i in range(4)]

    with (
        nc.semaphore("qS") as qS,
        nc.semaphore("qG") as qG,
        nc.semaphore("tE") as tE,
        nc.semaphore("vE") as vE,
        nc.semaphore("sE") as sE,
        nc.semaphore("cC") as cC,
        nc.Block() as blk,
    ):
        @blk.sync
        def _(sync):
            for t, d in ((c_bq, bq2d), (c_bk, bk2d), (c_bo, bo2d), (c_idB, idnBd),
                         (c_idF, idnFd), (c_ln1, lnod), (c_1rf, onesRFd),
                         (c_t1, t1d), (c_t2, t2d), (c_padv, padvd),
                         (c_pads, padsd), (c_bvaf, bvad)):
                i = sync.dma_start(out=t[:, :], in_=d)
                L.inc(i, qS)
            i = sync.dma_start(out=c_clip[0:1, 0:1], in_=clipxd)
            L.inc(i, qS)
            i = sync.dma_start(out=c_clip[0:1, 1:2], in_=clipdd)
            L.inc(i, qS)
            L.set("p0_small", qS)
            i = sync.dma_start(out=c_csl[:, :], in_=cTd)
            L.inc(i, qS)
            L.set("p0_c", qS)
            for dc in range(NDC):
                i = sync.dma_start(
                    out=wmf[:, dc * 768:(dc + 1) * 768],
                    in_=wmodsd.rearrange("(c p) j -> c p j", p=128)[dc])
                L.inc(i, qS)
            L.set("p0_wmod", qS)

        @blk.vector
        def _(vector):
            vector.memset(c_eps[:, :], 1e-5)
            vector.memset(c_127[:, :], 127.0)
            vector.memset(c_negC[:, :], -30.0)
            vector.memset(c_q25[:, :], 0.25)
            vector.memset(c_km1[:, :], -1.0)
            vector.memset(c_k0[:, :], 0.0)
            vector.reciprocal(c_r127[:, :], c_127[:, :])
            vector.memset(c_mx[:, :], 0.0)
            L.wait(vector, qS, "p0_small")
            vector.tensor_copy(c_1rb[:, :], c_1rf[:, :])
            vector.tensor_copy(c_bva[:, :], c_bvaf[:, :])
            L.set("v_wmb", vE)

        PRE6 = ()

        def pre6(eng, which=0):
            if which not in PRE6:
                return
            eng.add_instruction(mybir.InstLoadActFuncSet(
                name=nc.get_next_instruction_name(), act_func_set_id=6,
                ins=[], outs=[]))

        @blk.scalar
        def _(scalar):
            L.wait(scalar, qS, "p0_c")
            i = scalar.activation(c_csi[:, :], c_csl[:, :], AF.Silu)
            L.inc(i, sE)
            L.set("s_csi", sE)

        @blk.tensor
        def _(tensor):
            L.wait(tensor, qS, "p0_wmod")
            L.wait(tensor, sE, "s_csi")
            for nt, (n0, n1) in enumerate(((0, 512), (512, 768))):
                for dc in range(NDC):
                    i = tensor.matmul(ps[2 + nt].ap()[0:4, 0:n1 - n0],
                                      c_csi[:, dc * 4:(dc + 1) * 4],
                                      wmf[:, dc * 768 + n0:dc * 768 + n1],
                                      start=(dc == 0), stop=(dc == NDC - 1))
            L.inc(i, tE)
            L.set("t_modp", tE)

        @blk.scalar
        def _(scalar):
            L.wait(scalar, tE, "t_modp")
            scalar.activation(modp[:, 0:512], ps[2].ap()[0:4, 0:512], AF.Identity)
            i = scalar.activation(modp[:, 512:768], ps[3].ap()[0:4, 0:256],
                                  AF.Identity)
            L.inc(i, sE)
            L.set("s_modp", sE)
            pre6(scalar, 1)

        @blk.gpsimd
        def _(g):
            L.wait(g, sE, "s_modp")
            i = g.dma_start(out=agmod_i.rearrange("(b j) -> b j", b=B),
                            in_=modp[:, :])
            L.inc(i, qG)
            g.wait_ge(qG, L.at(qG))
            i = g.collective_compute("AllGather", ALU.bypass, replica_groups=RG,
                                     ins=[agmod_i.ap().opt()],
                                     outs=[agmod_o.ap().opt()])
            L.inc(i, cC)
            L.wait(g, cC, None) if False else g.wait_ge(cC, L.at(cC))
            rb = g.alloc_register("rb")
            g.reg_load(rb, c_pads[0:1, 2:3])
            rbv = er.make_scalar_value(rb, min_val=0, max_val=3)
            i = g.dma_start(
                out=modrow_d.rearrange("(r j) -> r j", r=8),
                in_=agmod_o.rearrange("(r b j) -> b r j", r=8, b=B,
                                      j=768)[bass.ds(rbv, 1)])
            L.inc(i, qG)
            L.set("g_modrow", qG)

        @blk.sync
        def _(sync):
            L.wait(sync, qG, "g_modrow")
            i = sync.dma_start(out=sc48[:, :],
                               in_=modrow_d.rearrange("(q p) -> q p", q=48, p=128))
            L.inc(i, qS)
            L.set("p0_sc48", qS)

        @blk.tensor
        def _(tensor):
            L.wait(tensor, qS, "p0_sc48")
            i = tensor.transpose(ps[2].ap()[:, 0:48], sc48[:, :], c_idF[0:48, 0:48])
            L.inc(i, tE)
            L.set("t_mod48", tE)

        @blk.vector
        def _(vector):
            L.wait(vector, tE, "t_mod48")
            i = vector.tensor_copy(c_mod[:, :], ps[2].ap()[:, 0:48])
            L.inc(i, vE)
            L.set("v_cmod", vE)

        @blk.vector
        def _(vector):
            L.wait(vector, vE, "v_cmod")
            for src0, dstg in ((16, c_ga), (40, c_gb)):
                L.chain(vector, vE, vector.tensor_single_scalar(
                    c_sgn[:, :], c_mod[:, src0:src0 + 8], -1.0, ALU.mult))
                L.chain(vector, vE, vector.tensor_max(
                    c_mag[:, 0:8], c_mod[:, src0:src0 + 8], c_sgn[:, :]))
                L.chain(vector, vE, vector.tensor_scalar(
                    c_mag2[:, :], c_mag[:, 0:8], c_q25[:, 0:1], None, ALU.max))
                L.chain(vector, vE, vector.tensor_single_scalar(
                    c_mag3[:, :], c_mag2[:, :], -1.0, ALU.mult))
                L.chain(vector, vE, vector.tensor_single_scalar(
                    c_sgn[:, :], c_mod[:, src0:src0 + 8], 1e20, ALU.mult))
                L.chain(vector, vE, vector.tensor_tensor(
                    c_mag3[:, :], c_sgn[:, :], c_mag3[:, :], ALU.max))
                L.chain(vector, vE, vector.tensor_tensor(
                    dstg[:, :], c_mag3[:, :], c_mag2[:, :], ALU.min))
            vector.tensor_single_scalar(c_g1[:, :], c_mod[:, 8:16], 1.0, ALU.add)
            i = vector.tensor_single_scalar(c_g2[:, :], c_mod[:, 32:40], 1.0,
                                            ALU.add)
            L.inc(i, vE)
            L.set("v_mod", vE)

        if tap is not None and tap[0] == "gax":
            @blk.gpsimd
            def _(g):
                L.wait(g, vE, "v_mod")
                i = g.dma_start(out=dbg.rearrange("(a p) t -> a p t", a=1)[0][:, 0:8], in_=c_ga[:, :])
                L.inc(i, qG)
                i = g.dma_start(out=dbg.rearrange("(a p) t -> a p t", a=1)[0][:, 8:16], in_=c_gb[:, :])
                L.inc(i, qG)
                i = g.dma_start(out=dbg.rearrange("(a p) t -> a p t", a=1)[0][:, 16:24], in_=c_g1[:, :])
                L.inc(i, qG)
                i = g.dma_start(out=dbg.rearrange("(a p) t -> a p t", a=1)[0][:, 24:32], in_=c_g2[:, :])
                L.inc(i, qG)
                i = g.dma_start(out=dbg.rearrange("(a p) t -> a p t", a=1)[0][:, 32:40], in_=c_sgn[:, :])
                L.inc(i, qG)
                i = g.dma_start(out=dbg.rearrange("(a p) t -> a p t", a=1)[0][:, 40:48], in_=c_mag[:, 0:8])
                L.inc(i, qG)
                i = g.dma_start(out=dbg.rearrange("(a p) t -> a p t", a=1)[0][:, 48:56], in_=c_mag2[:, :])
                L.inc(i, qG)
                i = g.dma_start(out=dbg.rearrange("(a p) t -> a p t", a=1)[0][:, 56:64], in_=c_mod[:, 16:24])
                L.inc(i, qG)
                i = g.dma_start(out=dbg.rearrange("(a p) t -> a p t", a=1)[0][:, 64:72], in_=c_mag3[:, :])
                L.inc(i, qG)
                g.wait_ge(qG, L.at(qG))

        if tap is not None and tap[0] == "mod":
            @blk.gpsimd
            def _(g):
                L.wait(g, vE, "v_mod")
                i = g.dma_start(out=dbg, in_=c_mod[:, :])
                L.inc(i, qG)
                g.wait_ge(qG, L.at(qG))
        if stage < 2:
            _finish(nc, L, blk, qG, out, backing)
            return nc

        @blk.sync
        def _(sync):
            i = sync.dma_start(out=wpf[:, 0:IFF], in_=wdNd)
            L.inc(i, qS)
            L.set("p0_wd", qS)

        @blk.vector
        def _(vector):
            L.wait(vector, qS, "p0_wd")
            L.chain(vector, vE, vector.reduce_max(
                wsc[:, 0:1], wpf[:, 0:IFF], axis=AX,
                apply_absolute_value=True))
            i = vector.tensor_tensor(wsc[:, 1:2], wsc[:, 0:1], c_r127[:, 0:1],
                                     ALU.mult)
            L.inc(i, vE)
            L.set("v_wdsc", vE)

        @blk.scalar
        def _(scalar):
            L.wait(scalar, vE, "v_wdsc")
            L.chain(scalar, sE, scalar.activation(wsc[:, 3:4], wsc[:, 1:2],
                                                  AF.Ln))
            i = scalar.activation(wsc[:, 2:3], wsc[:, 3:4], AF.Exp, scale=-1.0)
            L.inc(i, sE)
            L.set("s_wdinv", sE)

        @blk.vector
        def _(vector):
            L.wait(vector, sE, "s_wdinv")
            vector.tensor_scalar(wps[:, 0:IFF], wpf[:, 0:IFF], wsc[:, 2:3], None,
                                 ALU.mult)
            vector.tensor_copy(wp8[:, 0:IFF], wps[:, 0:IFF])
            i = vector.tensor_copy(wpq[:, 0:IFF], wp8[:, 0:IFF])
            L.inc(i, vE)
            L.set("v_wdq", vE)

        @blk.gpsimd
        def _(g):
            L.wait(g, vE, "v_wdq")
            i = g.dma_start(out=agsc_i.rearrange("(a b) -> a b", b=1)[704:832],
                            in_=wsc[:, 1:2])
            L.inc(i, qG)
            L.set("g_sd", qG)

        for ic in range(NIC):
            @blk.tensor
            def _(tensor, ic=ic):
                if ic == 0:
                    L.wait(tensor, vE, "v_wdq")
                if ic >= 2:
                    L.wait(tensor, vE, f"v_wdT{ic - 2}")
                i = tensor.transpose(
                    ps[2 + ic % 2].ap().bitcast(BF16)[:, 0:128],
                    wpq[:, ic * 128:(ic + 1) * 128], c_idB[:, :])
                L.inc(i, tE)
                L.set(f"t_wdT{ic}", tE)

            @blk.vector
            def _(vector, ic=ic):
                L.wait(vector, tE, f"t_wdT{ic}")
                if ic >= 2:
                    L.wait(vector, qG, f"g_wdT{ic - 2}")
                i = vector.tensor_copy(
                    wstg[ic % 2][:, :],
                    ps[2 + ic % 2].ap().bitcast(BF16)[:, 0:128])
                L.inc(i, vE)
                L.set(f"v_wdT{ic}", vE)

            @blk.gpsimd
            def _(g, ic=ic):
                L.wait(g, vE, f"v_wdT{ic}")
                i = g.dma_start(out=agwd_i.ap()[ic * 128:(ic + 1) * 128, :],
                                in_=wstg[ic % 2][:, :])
                L.inc(i, qG)
                L.set(f"g_wdT{ic}", qG)
        L.set("g_wd_all", qG)

        for wname, wN, wT, agi, sc_off in (("g", wgNd, wgTd, agwg_i, 0),
                                           ("u", wuNd, wuTd, agwu_i, 352)):
            @blk.sync
            def _(sync, wN=wN, wname=wname):
                L.wait(sync, vE, "v_wdq" if wname == "g" else "v_wgq7")
                for k, rows in ((0, 128), (1, 128), (2, 96)):
                    i = sync.dma_start(out=wpf[0:rows, k * 1024:(k + 1) * 1024],
                                       in_=wN[k * 128:k * 128 + rows, :])
                    L.inc(i, qS)
                L.set(f"p_w{wname}n", qS)

            @blk.vector
            def _(vector, wname=wname):
                L.wait(vector, qS, f"p_w{wname}n")
                for k, rows in ((0, 128), (1, 128), (2, 96)):
                    L.chain(vector, vE, vector.reduce_max(
                        wsc[0:rows, k:k + 1],
                        wpf[0:rows, k * 1024:(k + 1) * 1024],
                        axis=AX, apply_absolute_value=True))
                i = vector.tensor_scalar(wsc[:, 4:7], wsc[:, 0:3],
                                         c_r127[:, 0:1], None, ALU.mult)
                L.inc(i, vE)
                L.set(f"v_w{wname}sc", vE)

            @blk.tensor
            def _(tensor, wname=wname):
                L.wait(tensor, vE, f"v_w{wname}sc")
                for k, rows in ((0, 128), (1, 128), (2, 96)):
                    i = tensor.transpose(ps[2].ap()[0:1, k * 128:k * 128 + rows],
                                         wsc[0:rows, 4 + k:5 + k],
                                         c_idF[0:rows, 0:rows])
                L.inc(i, tE)
                L.set(f"t_w{wname}r", tE)

            @blk.scalar
            def _(scalar, wname=wname):
                L.wait(scalar, tE, f"t_w{wname}r")
                i = scalar.activation(wrow[0:1, :], ps[2].ap()[0:1, 0:ICORE],
                                      AF.Identity)
                L.inc(i, sE)
                L.set(f"s_w{wname}row", sE)

            @blk.gpsimd
            def _(g, wname=wname, sc_off=sc_off):
                L.wait(g, sE, f"s_w{wname}row")
                i = g.dma_start(
                    out=agsc_i.rearrange("(a b) -> a b",
                                         a=1)[0:1, sc_off:sc_off + ICORE],
                    in_=wrow[0:1, :])
                L.inc(i, qG)
                L.set(f"g_w{wname}sc", qG)

            @blk.scalar
            def _(scalar, wname=wname):
                L.wait(scalar, sE, f"s_w{wname}row")
                L.wait(scalar, qG, f"g_w{wname}sc")
                L.chain(scalar, sE, scalar.activation(
                    wrep[0:1, 0:ICORE], wrow[0:1, :], AF.Ln))
                i = scalar.activation(winv[0:1, :], wrep[0:1, 0:ICORE],
                                      AF.Exp, scale=-1.0)
                L.inc(i, sE)
                L.set(f"s_w{wname}inv", sE)

            @blk.tensor
            def _(tensor, wname=wname):
                L.wait(tensor, sE, f"s_w{wname}inv")
                i = tensor.matmul(ps[3].ap()[:, 0:ICORE], c_1rf[0:1, :],
                                  winv[0:1, :], start=True, stop=True)
                L.inc(i, tE)
                L.set(f"t_w{wname}rep", tE)

            @blk.vector
            def _(vector, wname=wname):
                L.wait(vector, tE, f"t_w{wname}rep")
                i = vector.tensor_copy(wrep[:, :], ps[3].ap()[:, 0:ICORE])
                L.inc(i, vE)
                L.set(f"v_w{wname}rep", vE)

            for ec in range(NDC):
                @blk.sync
                def _(sync, wT=wT, ec=ec, wname=wname):
                    if ec == 0:
                        L.wait(sync, vE, f"v_w{wname}sc")
                    else:
                        L.wait(sync, vE, f"v_w{wname}q{ec - 1}")
                    i = sync.dma_start(out=wpf[:, 0:ICORE],
                                       in_=wT[ec * 128:(ec + 1) * 128, :])
                    L.inc(i, qS)
                    L.set(f"p_w{wname}t{ec}", qS)

                @blk.vector
                def _(vector, ec=ec, wname=wname):
                    L.wait(vector, qS, f"p_w{wname}t{ec}")
                    if ec == 0:
                        L.wait(vector, vE, f"v_w{wname}rep")
                    if ec >= 1:
                        L.wait(vector, qG, f"g_w{wname}q{ec - 1}")
                    L.chain(vector, vE, vector.tensor_tensor(
                        wps[:, 0:ICORE], wpf[:, 0:ICORE], wrep[:, :], ALU.mult))
                    L.chain(vector, vE, vector.tensor_copy(
                        wp8[:, 0:ICORE], wps[:, 0:ICORE]))
                    i = vector.tensor_copy(wpq[:, 0:ICORE], wp8[:, 0:ICORE])
                    L.inc(i, vE)
                    L.set(f"v_w{wname}q{ec}", vE)

                @blk.gpsimd
                def _(g, agi=agi, ec=ec, wname=wname):
                    L.wait(g, vE, f"v_w{wname}q{ec}")
                    i = g.dma_start(out=agi.ap()[ec * 128:(ec + 1) * 128, :],
                                    in_=wpq[:, 0:ICORE])
                    L.inc(i, qG)
                    L.set(f"g_w{wname}q{ec}", qG)
            L.mark[f"g_w{wname}_all"] = L.at(qG)

        @blk.gpsimd
        def _(g):
            g.wait_ge(qG, L.mark["g_wu_all"])
            g.wait_ge(qG, L.mark["g_wd_all"])
            g.wait_ge(qG, L.mark["g_sd"])
            for ins_, outs_, nm in ((agwg_i, agwg_o, "ag_wg"),
                                    (agwu_i, agwu_o, "ag_wu"),
                                    (agwd_i, agwd_o, "ag_wd"),
                                    (agsc_i, agsc_o, "ag_sc")):
                i = g.collective_compute("AllGather", ALU.bypass, replica_groups=RG,
                                         ins=[ins_.ap().opt()],
                                         outs=[outs_.ap().opt()])
                L.inc(i, cC)
                L.set(nm, cC)
        L.mark["p0b_v_done"] = L.at(vE)

        if tap is not None and tap[0] == "wgq":
            @blk.sync
            def _(sync):
                L.wait(sync, cC, "ag_wg")
                i = sync.dma_start(
                    out=sc22[:, :] if False else wpf[:, 0:IFF],
                    in_=agwg_o.rearrange("(a b) -> a b", b=IFF)[0:128])
                L.inc(i, qS)
                L.set("tap_wgq", qS)

            @blk.vector
            def _(vector):
                L.wait(vector, qS, "tap_wgq")
                i = vector.tensor_copy(wps[:, 0:IFF], wpf[:, 0:IFF].bitcast(BF16)[:, 0:IFF])
                L.inc(i, vE)
                L.set("tap_wgq_v", vE)

            @blk.gpsimd
            def _(g):
                L.wait(g, vE, "tap_wgq_v")
                i = g.dma_start(out=dbg, in_=wps[:, 0:IFF])
                L.inc(i, qG)
                g.wait_ge(qG, L.at(qG))
        if stage < 3:
            _finish(nc, L, blk, qG, out, backing)
            return nc

        def emit_ln(tag, src_dram, n_tok, h_out, h_stride, g_t, s_t, s_off,
                    first_wait=None):
            for tt in range(n_tok // 512):
                r = tt % 2

                @blk.sync
                def _(sync, tt=tt, r=r):
                    if tt == 0 and first_wait is not None:
                        sync.wait_ge(qG, L.mark[first_wait])
                    if tt >= 2:
                        L.wait(sync, vE, f"{tag}_ap{tt - 2}")
                    i = sync.dma_start(
                        out=xt[r].rearrange("p (c t) -> p c t", c=NDC),
                        in_=src_dram.rearrange(
                            "(c p) t -> p c t",
                            p=128)[:, :, tt * 512:(tt + 1) * 512])
                    L.inc(i, qS)
                    L.set(f"{tag}_x{tt}", qS)

                @blk.vector
                def _(vector, tt=tt, r=r):
                    L.wait(vector, qS, f"{tag}_x{tt}")
                    if tt >= 1:
                        L.wait(vector, tE, f"{tag}_st{tt - 1}")
                    for dc in range(NDC):
                        i = vector.tensor_copy(xb_t[:, dc * 512:(dc + 1) * 512],
                                               xt[r][:, dc * 512:(dc + 1) * 512])
                    L.inc(i, vE)
                    L.set(f"{tag}_xb{tt}", vE)

                @blk.scalar
                def _(scalar, tt=tt, r=r):
                    L.wait(scalar, qS, f"{tag}_x{tt}")
                    if tt >= 1:
                        L.wait(scalar, tE, f"{tag}_st{tt - 1}")
                    for dc in range(NDC):
                        i = scalar.activation(xq_t[:, dc * 512:(dc + 1) * 512],
                                              xt[r][:, dc * 512:(dc + 1) * 512],
                                              AF.Square)
                    L.inc(i, sE)
                    L.set(f"{tag}_xq{tt}", sE)

                @blk.tensor
                def _(tensor, tt=tt):
                    L.wait(tensor, vE, f"{tag}_xb{tt}")
                    L.wait(tensor, sE, f"{tag}_xq{tt}")
                    if tt >= 1:
                        L.wait(tensor, vE, f"{tag}_mrt{tt - 1}")
                        L.wait(tensor, sE, f"{tag}_mu{tt - 1}")
                    for dc in range(NDC):
                        tensor.matmul(ps[0].ap()[:, 0:512], c_ln1[:, :],
                                      xb_t[:, dc * 512:(dc + 1) * 512],
                                      start=(dc == 0), stop=(dc == NDC - 1))
                    for dc in range(NDC):
                        i = tensor.matmul(ps[0].ap()[:, 512:1024], c_ln1[:, :],
                                          xq_t[:, dc * 512:(dc + 1) * 512],
                                          start=(dc == 0), stop=(dc == NDC - 1))
                    L.inc(i, tE)
                    L.set(f"{tag}_st{tt}", tE)

                @blk.scalar
                def _(scalar, tt=tt):
                    L.wait(scalar, tE, f"{tag}_st{tt}")
                    scalar.activation(muS[:, :], ps[0].ap()[:, 0:512], AF.Identity)
                    i = scalar.activation(mu2[:, :], ps[0].ap()[:, 0:512], AF.Square)
                    L.inc(i, sE)
                    L.set(f"{tag}_mu{tt}", sE)

                @blk.vector
                def _(vector, tt=tt):
                    L.wait(vector, sE, f"{tag}_mu{tt}")
                    i = vector.tensor_sub(varS[:, :], ps[0].ap()[:, 512:1024],
                                          mu2[:, :])
                    L.inc(i, vE)
                    L.set(f"{tag}_var{tt}", vE)

                @blk.scalar
                def _(scalar, tt=tt):
                    L.wait(scalar, vE, f"{tag}_var{tt}")
                    scalar.activation(lnvS[:, :], varS[:, :], AF.Ln,
                                      bias=c_eps[:, 0:1])
                    i = scalar.activation(rstdS[:, :], lnvS[:, :], AF.Exp,
                                          scale=-0.5)
                    L.inc(i, sE)
                    L.set(f"{tag}_rstd{tt}", sE)

                @blk.vector
                def _(vector, tt=tt, r=r):
                    L.wait(vector, sE, f"{tag}_rstd{tt}")
                    if tag == "L1" and tt == 0:
                        L.wait(vector, vE, "v_mod")
                    i = vector.tensor_mul(mrtS[:, :], muS[:, :], rstdS[:, :])
                    L.inc(i, vE)
                    L.set(f"{tag}_mrt{tt}", vE)
                    for dc in range(NDC):
                        vector.tensor_mul(tmpA[:, :],
                                          xt[r][:, dc * 512:(dc + 1) * 512],
                                          rstdS[:, :])
                        vector.tensor_sub(tmpB[:, :], tmpA[:, :], mrtS[:, :])
                        i = vector.tensor_scalar(
                            h_out[:, dc * h_stride + tt * 512:
                                  dc * h_stride + tt * 512 + 512],
                            tmpB[:, :], g_t[:, dc:dc + 1],
                            s_t[:, s_off + dc:s_off + dc + 1],
                            ALU.mult, ALU.add)
                    L.inc(i, vE)
                    L.set(f"{tag}_ap{tt}", vE)
            L.mark[f"{tag}_done_v"] = L.at(vE)

        emit_ln("L1", xTd, FR, a_h1, FR, c_g1, c_mod, 0)

        if tap is not None and tap[0] == "h1":
            @blk.gpsimd
            def _(g):
                g.wait_ge(vE, L.mark["L1_done_v"])
                for dc in range(NDC):
                    i = g.dma_start(
                        out=dbg.rearrange("(c p) t -> c p t", p=128)[dc],
                        in_=a_h1[:, dc * FR:(dc + 1) * FR])
                    L.inc(i, qG)
                g.wait_ge(qG, L.at(qG))
        if stage < 4:
            _finish(nc, L, blk, qG, out, backing)
            return nc


        for ec in range(NDC):
            @blk.sync
            def _(sync, ec=ec):
                if ec == 0:
                    L.wait(sync, vE, "L1_done_v")
                if ec >= 1:
                    L.wait(sync, vE, f"v_wc{ec - 1}")
                i = sync.dma_start(
                    out=wtl[ec % 2][:, 0:DA] if False else wtl[0][:, 0:1024],
                    in_=wvTd.rearrange("(c p) d -> c p d", p=128)[ec][:, 0:1024])
                L.inc(i, qS)
                i = sync.dma_start(
                    out=wtl[1][:, 0:16],
                    in_=wvTd.rearrange("(c p) d -> c p d", p=128)[ec][:, 1024:1040])
                L.inc(i, qS)
                L.set(f"v_wl{ec}", qS)

            @blk.vector
            def _(vector, ec=ec):
                L.wait(vector, qS, f"v_wl{ec}")
                vector.tensor_copy(wvb[:, ec * DA:ec * DA + 1024], wtl[0][:, 0:1024])
                i = vector.tensor_copy(wvb[:, ec * DA + 1024:ec * DA + 1040],
                                       wtl[1][:, 0:16])
                L.inc(i, vE)
                L.set(f"v_wc{ec}", vE)

        for tc in range(NTC):
            @blk.tensor
            def _(tensor, tc=tc):
                if tc == 0:
                    L.wait(tensor, vE, f"v_wc{NDC - 1}")
                if tc >= 1:
                    L.wait(tensor, vE, f"v_evv{tc - 1}")
                for part, (n0, n1) in enumerate(((0, 512), (512, 1024),
                                                 (1024, 1040))):
                    pv = ps[2].ap()[:, n0:n1] if part < 2 else ps[3].ap()[:, 0:16]
                    for ec in range(NDC):
                        tensor.matmul(
                            pv,
                            a_h1[:, ec * FR + tc * 128:ec * FR + tc * 128 + 128],
                            wvb[:, ec * DA + n0:ec * DA + n1],
                            start=(ec == 0), stop=False)
                    i = tensor.matmul(pv, c_1rb[0:1, :],
                                      c_bva[0:1, n0:n1], start=False, stop=True)
                L.inc(i, tE)
                L.set(f"v_mmv{tc}", tE)

            @blk.vector
            def _(vector, tc=tc):
                L.wait(vector, tE, f"v_mmv{tc}")
                vector.tensor_copy(a_v[:, tc * DA:tc * DA + 1024],
                                   ps[2].ap()[:, 0:1024])
                i = vector.tensor_copy(a_v[:, tc * DA + 1024:tc * DA + 1040],
                                       ps[3].ap()[:, 0:16])
                L.inc(i, vE)
                L.set(f"v_evv{tc}", vE)
        L.mark["v_done"] = L.at(vE)

        for nm, wdram, bias_t, ntt, toff in (("q", wqTd, c_bq, 4, 512),
                                             ("k", wkTd, c_bk, 6, 0)):
            for dc in range(NDC):
                r = dc % 2

                @blk.sync
                def _(sync, dc=dc, r=r, nm=nm, wdram=wdram):
                    if nm == "q" and dc == 0:
                        L.wait(sync, vE, f"v_wc{NDC - 1}")
                    if nm == "k" and dc < 2:
                        L.wait(sync, vE, "q_wc7")
                    if dc >= 2:
                        L.wait(sync, vE, f"{nm}_wc{dc - 2}")
                    for ec in range(NDC):
                        i = sync.dma_start(
                            out=wtl[r][:, ec * 128:ec * 128 + 128],
                            in_=wdram.rearrange("(c p) d -> c p d",
                                                p=128)[ec][:, dc * 128:dc * 128 + 128])
                        L.inc(i, qS)
                    L.set(f"{nm}_wl{dc}", qS)

                @blk.vector
                def _(vector, dc=dc, r=r, nm=nm, ntt=ntt):
                    L.wait(vector, qS, f"{nm}_wl{dc}")
                    if nm == "k" and dc < 2:
                        L.wait(vector, tE, "q_mm31")
                    if dc >= 2:
                        L.wait(vector, tE, f"{nm}_mm{(dc - 2) * ntt + ntt - 1}")
                    i = vector.tensor_copy(wqbs[r][:, :], wtl[r][:, :])
                    L.inc(i, vE)
                    L.set(f"{nm}_wc{dc}", vE)

                def emit_pmm(tensor, dc, tt, idx, r, nm, ntt, toff):
                    if tt == 0:
                        L.wait(tensor, vE, f"{nm}_wc{dc}")
                    pr = dc * ntt + tt - 2
                    if pr >= 0:
                        L.wait(tensor, sE, f"{nm}_ev{pr}")
                    if nm == "k" and dc == 0 and tt == 0:
                        L.wait(tensor, vE, "v_done")
                    for ec in range(NDC):
                        i = tensor.matmul(
                            ps[idx].ap()[:, 0:512],
                            wqbs[r][:, ec * 128:ec * 128 + 128],
                            a_h1[:, ec * FR + toff + tt * 512:
                                 ec * FR + toff + tt * 512 + 512],
                            start=(ec == 0), stop=(ec == NDC - 1))
                    L.inc(i, tE)
                    L.set(f"{nm}_mm{dc * ntt + tt}", tE)

                for th in range(ntt // 2):
                    tts = (2 * th, 2 * th + 1)
                    for tt in tts:
                        @blk.tensor
                        def _(tensor, dc=dc, tt=tt, r=r, nm=nm, ntt=ntt,
                              toff=toff):
                            emit_pmm(tensor, dc, tt, (dc * ntt + tt) % 2,
                                     r, nm, ntt, toff)

                    if nm == "q":
                        @blk.scalar
                        def _(scalar, dc=dc, tts=tts, ntt=ntt, bias_t=bias_t):
                            for tt in tts:
                                it = dc * ntt + tt
                                L.wait(scalar, tE, f"q_mm{it}")
                                if it >= 2:
                                    L.wait(scalar, qG, f"q_sp{it - 2}")
                                i = scalar.activation(psth[it % 2][:, :],
                                                      ps[it % 2].ap()[:, 0:512],
                                                      AF.Identity,
                                                      bias=bias_t[:, dc:dc + 1])
                                L.inc(i, sE)
                                L.set(f"q_ev{it}", sE)

                        for tt in tts:
                            @blk.gpsimd
                            def _(g, dc=dc, tt=tt, ntt=ntt):
                                it = dc * ntt + tt
                                L.wait(g, sE, f"q_ev{it}")
                                i = g.dma_start(
                                    out=qhd.rearrange("(c p) t -> c p t",
                                                      p=128)[dc][
                                        :, tt * 512:tt * 512 + 512],
                                    in_=psth[it % 2][:, :])
                                L.inc(i, qG)
                                L.set(f"q_sp{it}", qG)
                    else:
                        @blk.scalar
                        def _(scalar, dc=dc, tts=tts, ntt=ntt, bias_t=bias_t):
                            for tt in tts:
                                it = dc * ntt + tt
                                L.wait(scalar, tE, f"k_mm{it}")
                                if it >= 2:
                                    L.wait(scalar, vE, f"k_lo{it - 2}")
                                L.chain(scalar, sE, scalar.activation(
                                    pstf[it % 2][:, :],
                                    ps[it % 2].ap()[:, 0:512],
                                    AF.Identity, bias=bias_t[:, dc:dc + 1]))
                                i = scalar.activation(psth[it % 2][:, :],
                                                      pstf[it % 2][:, :],
                                                      AF.Identity)
                                L.inc(i, sE)
                                L.set(f"k_ev{it}", sE)

                        for tt in tts:
                            @blk.vector
                            def _(vector, dc=dc, tt=tt, ntt=ntt):
                                it = dc * ntt + tt
                                L.wait(vector, sE, f"k_ev{it}")
                                if it >= 2:
                                    L.wait(vector, qG, f"k_sp{it - 2}")
                                i = vector.tensor_sub(pstl[it % 2][:, :],
                                                      pstf[it % 2][:, :],
                                                      psth[it % 2][:, :])
                                L.inc(i, vE)
                                L.set(f"k_lo{it}", vE)

                            @blk.gpsimd
                            def _(g, dc=dc, tt=tt, ntt=ntt):
                                it = dc * ntt + tt
                                L.wait(g, vE, f"k_lo{it}")
                                i = g.dma_start(
                                    out=khd.rearrange("(c p) t -> c p t",
                                                      p=128)[dc][
                                        :, tt * 512:tt * 512 + 512],
                                    in_=psth[it % 2][:, :])
                                L.inc(i, qG)
                                i = g.dma_start(
                                    out=kld.rearrange("(c p) t -> c p t",
                                                      p=128)[dc][
                                        :, tt * 512:tt * 512 + 512],
                                    in_=pstl[it % 2][:, :])
                                L.inc(i, qG)
                                L.set(f"k_sp{it}", qG)
            L.mark[f"{nm}_done"] = L.at(qG)

        if tap is not None and tap[0] == "qlo":
            @blk.gpsimd
            def _(g):
                g.wait_ge(qG, L.mark["q_done"])
                i = g.dma_start(out=dbg.rearrange("(a p) t -> a p t", a=2)[0],
                                in_=qhd.rearrange("(c p) t -> c p t", p=128)[0])
                L.inc(i, qG)
                i = g.dma_start(out=dbg.rearrange("(a p) t -> a p t", a=2)[1],
                                in_=qld.rearrange("(c p) t -> c p t", p=128)[0])
                L.inc(i, qG)
                g.wait_ge(qG, L.at(qG))

        @blk.gpsimd
        def _(g):
            g.wait_ge(qG, L.mark["k_done"])
            rk = g.alloc_register("rkp")
            g.reg_load(rk, c_pads[0:1, 0:1])
            rkv = er.make_scalar_value(rk, min_val=0, max_val=2560,
                                       guaranteed_mod_val=512)
            for dc in range(NDC):
                i = g.dma_start(
                    out=khd.rearrange("(c p) t -> c p t", p=128)[dc][
                        :, bass.ds(rkv, 512)],
                    in_=c_km1[:, :])
                L.inc(i, qG)
                i = g.dma_start(
                    out=kld.rearrange("(c p) t -> c p t", p=128)[dc][
                        :, bass.ds(rkv, 512)],
                    in_=c_k0[:, :])
                L.inc(i, qG)
            L.set("kpad_done", qG)

        @blk.vector
        def _(vector):
            vector.wait_ge(vE, L.mark["v_done"])
            rv = vector.alloc_register("rv")
            vector.reg_load(rv, c_pads[0:1, 1:2])
            rvv = er.make_scalar_value(rv, min_val=0, max_val=20800)
            for k in range(4):
                i = vector.tensor_copy(a_v[:, bass.ds(rvv + k * DA, DA)],
                                       c_padv[:, :])
            L.inc(i, vE)
            L.set("pads_done", vE)

        if tap is not None and tap[0] == "av":
            @blk.gpsimd
            def _(g):
                L.wait(g, vE, "pads_done")
                i = g.dma_start(out=dbg, in_=a_v[:, :])
                L.inc(i, qG)
                g.wait_ge(qG, L.at(qG))
        if stage < 5:
            _finish(nc, L, blk, qG, out, backing)
            return nc


        def sspan(slot, jc):
            if slot == 0:
                return (0, 512) if jc <= 7 else (128 * (jc - 8), 512)
            if slot == 3:
                return (0, 128 * (jc + 1)) if jc <= 3 else (0, 512)
            if jc <= 3:
                return (0, 128 * (jc + 1))
            if jc <= 7:
                return (0, 512)
            return (128 * (jc - 8), 512)

        def vjcs(slot, ib):
            if slot == 0:
                return list(range(0, ib + 9))
            if slot == 3:
                return list(range(ib, 12))
            return list(range(ib, ib + 9))

        def esrc(slot, jc, ib, Eb):
            if slot == 0:
                if jc <= 3:
                    return aEm[:, 512 * jc + 128 * ib:512 * jc + 128 * ib + 128]
                if jc == ib + 8:
                    return aEm[:, 2560 + 128 * ib:2560 + 128 * ib + 128]
            elif slot == 3:
                if jc >= 8:
                    return aEm[:, 512 * (jc - 8) + 128 * ib:
                               512 * (jc - 8) + 128 * ib + 128]
                if jc == ib:
                    return aEm[:, 2048 + 128 * ib:2048 + 128 * ib + 128]
            else:
                if jc == ib:
                    return aEm[:, 2048 + 128 * ib:2048 + 128 * ib + 128]
                if jc == ib + 8:
                    return aEm[:, 2560 + 128 * ib:2560 + 128 * ib + 128]
            return Eb[:, 512 * jc + 128 * ib:512 * jc + 128 * ib + 128]

        @blk.sync
        def _(sync):
            L.wait(sync, vE, "pads_done")
            i = sync.dma_start(out=c_mm[:, :], in_=m0d)
            L.inc(i, qS)
            L.set("mm0_ready", qS)

        def emit_pair_load(p):
            w, h = p // 16, p % 16
            r = p % 2

            @blk.sync
            def _(sync, p=p, w=w, h=h, r=r):
                if p == 0:
                    sync.wait_ge(qG, L.mark["q_done"])
                    sync.wait_ge(qG, L.mark["kpad_done"])
                if p >= 2:
                    L.wait(sync, tE, f"sc{p - 2}_5")
                i = sync.dma_start(
                    out=kt2[r][0:64, 0:1536],
                    in_=khd[64 * h:64 * h + 64, 512 * w:512 * w + 1536])
                L.inc(i, qS)
                i = sync.dma_start(
                    out=kt2[r][64:128, 0:1536],
                    in_=kld[64 * h:64 * h + 64, 512 * w:512 * w + 1536])
                L.inc(i, qS)
                i = sync.dma_start(
                    out=qt2[r][0:64, 0:512],
                    in_=qhd[64 * h:64 * h + 64, 512 * w:512 * w + 512])
                L.inc(i, qS)
                i = sync.dma_start(
                    out=qt2[r][64:128, 0:512],
                    in_=qhd[64 * h:64 * h + 64, 512 * w:512 * w + 512])
                L.inc(i, qS)
                L.set(f"ptl{p}", qS)



        emit_pair_load(0)
        emit_pair_load(1)

        for p in range(64):
            w, h = p // 16, p % 16
            slot = w
            dc = h // 2
            r0 = 64 * (h % 2)
            Eb = aE[p % 2]
            qc0 = dc * TOWN + 512 * w

            def emit_sc(tensor, p, slot, g):
                r = p % 2
                if g == 0:
                    L.wait(tensor, qS, f"ptl{p}")
                if p == 0 and g == 0:
                    L.wait(tensor, vE, "pads_done")
                if g >= 3:
                    L.wait(tensor, sE, f"exp{p}_{g - 3}")
                elif p >= 1:
                    L.wait(tensor, sE, f"exp{p - 1}_{g + 3}")
                for jc in (2 * g, 2 * g + 1):
                    n0, n1 = sspan(slot, jc)
                    po = ps[g % 3].ap()[:, (jc % 2) * 512 + n0:
                                        (jc % 2) * 512 + n1]
                    i = tensor.matmul(po,
                                      kt2[r][:, 128 * jc:128 * jc + 128],
                                      qt2[r][:, n0:n1],
                                      start=True, stop=True)
                L.inc(i, tE)
                L.set(f"sc{p}_{g}", tE)

            for half in range(2):
                @blk.tensor
                def _(tensor, p=p, slot=slot, half=half):
                    for g in (3 * half, 3 * half + 1, 3 * half + 2):
                        emit_sc(tensor, p, slot, g)

                @blk.scalar
                def _(scalar, p=p, Eb=Eb, slot=slot, half=half):
                    for g in (3 * half, 3 * half + 1, 3 * half + 2):
                        L.wait(scalar, tE, f"sc{p}_{g}")
                        if p >= 2 and g == 0:
                            L.wait(scalar, tE, f"vmm{p - 2}")
                            L.wait(scalar, vE, f"msk{p - 2}")
                        e0, _ = sspan(slot, 2 * g)
                        _, e1 = sspan(slot, 2 * g + 1)
                        i = scalar.activation(
                            Eb[:, 1024 * g + e0:1024 * g + 512 + e1],
                            ps[g % 3].ap()[:, e0:512 + e1], AF.Exp)
                        L.inc(i, sE)
                        L.set(f"exp{p}_{g}", sE)

            if p + 2 < 64:
                emit_pair_load(p + 2)

            @blk.vector
            def _(vector, p=p, slot=slot, Eb=Eb):
                L.wait(vector, sE, f"exp{p}_5")
                if p >= 1:
                    L.wait(vector, tE, f"vmm{p - 1}")
                if slot == 0:
                    if p == 0:
                        L.wait(vector, qS, "mm0_ready")
                    vector.tensor_tensor(aEm[:, 0:2048], Eb[:, 0:2048],
                                         c_mm[:, :], ALU.mult)
                    for ib in range(4):
                        i = vector.tensor_tensor(
                            aEm[:, 2560 + 128 * ib:2688 + 128 * ib],
                            Eb[:, 4096 + 640 * ib:4224 + 640 * ib],
                            c_t2[:, :], ALU.mult)
                elif slot == 3:
                    if p == 48:
                        L.wait(vector, qS, "mm3_ready")
                    vector.tensor_tensor(aEm[:, 0:2048], Eb[:, 4096:6144],
                                         c_mm[:, :], ALU.mult)
                    for ib in range(4):
                        i = vector.tensor_tensor(
                            aEm[:, 2048 + 128 * ib:2176 + 128 * ib],
                            Eb[:, 640 * ib:640 * ib + 128],
                            c_t1[:, :], ALU.mult)
                else:
                    for ib in range(4):
                        vector.tensor_tensor(
                            aEm[:, 2048 + 128 * ib:2176 + 128 * ib],
                            Eb[:, 640 * ib:640 * ib + 128],
                            c_t1[:, :], ALU.mult)
                        i = vector.tensor_tensor(
                            aEm[:, 2560 + 128 * ib:2688 + 128 * ib],
                            Eb[:, 4096 + 640 * ib:4224 + 640 * ib],
                            c_t2[:, :], ALU.mult)
                L.inc(i, vE)
                L.set(f"msk{p}", vE)

            if tap is not None and tap[0] == f"e{p}":
                @blk.gpsimd
                def _(g, p=p, Eb=Eb):
                    L.wait(g, vE, f"msk{p}")
                    i = g.dma_start(out=dbg.rearrange("(a p) t -> a p t", a=1)[0][:, 0:6144],
                                    in_=Eb[:, :])
                    L.inc(i, qG)
                    i = g.dma_start(out=dbg.rearrange("(a p) t -> a p t", a=1)[0][:, 6144:9216],
                                    in_=aEm[:, :])
                    L.inc(i, qG)
                    g.wait_ge(qG, L.at(qG))

            if p == 15:
                @blk.sync
                def _(sync):
                    L.wait(sync, vE, "msk15")
                    i = sync.dma_start(out=c_mm[:, :], in_=m3d)
                    L.inc(i, qS)
                    L.set("mm3_ready", qS)

            @blk.tensor
            def _(tensor, p=p, w=w, h=h, slot=slot, Eb=Eb):
                L.wait(tensor, vE, f"msk{p}")
                if p >= 1:
                    L.wait(tensor, vE, f"evac{p - 1}")
                for ib in range(4):
                    pv = ps[3].ap()[:, 128 * ib:128 * ib + 65]
                    jcs = vjcs(slot, ib)
                    for ji, jc in enumerate(jcs):
                        i = tensor.matmul(
                            pv, esrc(slot, jc, ib, Eb),
                            a_v[:, (4 * w + jc) * DA + 65 * h:
                                (4 * w + jc) * DA + 65 * h + 65],
                            start=(ji == 0), stop=(ji == len(jcs) - 1))
                L.inc(i, tE)
                L.set(f"vmm{p}", tE)

            @blk.vector
            def _(vector, p=p):
                L.wait(vector, tE, f"vmm{p}")
                for ib in range(4):
                    L.chain(vector, vE, vector.tensor_copy(
                        c_zcp[:, ib:ib + 1],
                        ps[3].ap()[:, 128 * ib + 64:128 * ib + 65]))
                L.chain(vector, vE, vector.reciprocal(c_zr4[:, :],
                                                      c_zcp[:, :]))
                L.set(f"zinv{p}", vE)

            @blk.vector
            def _(vector, p=p, w=w, h=h):
                L.wait(vector, vE, f"zinv{p}")
                if p >= 1:
                    L.wait(vector, qG, f"gatm{p - 1}")
                for ib in range(4):
                    i = vector.tensor_scalar(
                        tstg[ib][:, 0:64],
                        ps[3].ap()[:, 128 * ib:128 * ib + 64],
                        c_zr4[:, ib:ib + 1], None, ALU.mult)
                L.inc(i, vE)
                L.set(f"evac{p}", vE)

            if tap is not None and tap[0].startswith("z") and tap[0][1:].isdigit() and p == int(tap[0][1:]):
                @blk.vector
                def _(vector, p=p):
                    L.wait(vector, tE, f"vmm{p}")
                    for ib in range(4):
                        i = vector.tensor_copy(
                            evt[:, 65 * ib:65 * ib + 65],
                            ps[2 + ib // 2].ap()[:, 512 * (ib % 2):
                                                 512 * (ib % 2) + 65])
                    L.inc(i, vE)
                    L.set("z0v", vE)

                @blk.gpsimd
                def _(g):
                    L.wait(g, vE, "z0v")
                    i = g.dma_start(
                        out=dbg.rearrange("(a p) t -> a p t", a=1)[0][:, 0:260],
                        in_=evt[:, 0:260])
                    L.inc(i, qG)
                    g.wait_ge(qG, L.at(qG))

            if tap is not None and tap[0] == f"zz{p}":
                @blk.gpsimd
                def _(g, p=p):
                    L.wait(g, vE, f"evac{p}")
                    i = g.dma_start(out=dbg.rearrange("(a p) t -> a p t", a=1)[0][:, 0:4], in_=c_zcp[:, :])
                    L.inc(i, qG)
                    i = g.dma_start(out=dbg.rearrange("(a p) t -> a p t", a=1)[0][:, 4:8], in_=c_zr4[:, :])
                    L.inc(i, qG)
                    for ib in range(4):
                        i = g.dma_start(out=dbg.rearrange("(a p) t -> a p t", a=1)[0][:, 8 + 64 * ib:72 + 64 * ib], in_=tstg[ib][:, 0:64])
                        L.inc(i, qG)
                    g.wait_ge(qG, L.at(qG))

            @blk.gpsimd
            def _(g, p=p, w=w, h=h):
                L.wait(g, vE, f"evac{p}")
                for ib in range(4):
                    i = g.dma_start(
                        out=atm_d.ap()[4 * w + ib][:, 64 * h:64 * h + 64],
                        in_=tstg[ib][:, 0:64])
                    L.inc(i, qG)
                L.set(f"gatm{p}", qG)

        if tap is not None and tap[0] == "av2":
            @blk.gpsimd
            def _(g):
                L.wait(g, qG, "gatm63")
                i = g.dma_start(out=dbg, in_=a_v[:, :])
                L.inc(i, qG)
                g.wait_ge(qG, L.at(qG))

        if tap is not None and tap[0] == "atm":
            @blk.gpsimd
            def _(g):
                L.wait(g, qG, "gatm63")
                for tc in range(16):
                    i = g.dma_start(
                        out=dbg.rearrange("(tc p) d -> tc p d", p=128)[tc],
                        in_=atm_d.ap()[tc])
                    L.inc(i, qG)
                g.wait_ge(qG, L.at(qG))

        for tc in range(16):
            @blk.sync
            def _(sync, tc=tc):
                if tc == 0:
                    L.wait(sync, qG, "gatm63")
                if tc >= 2:
                    L.wait(sync, tE, f"ttpl{tc - 2}")
                i = sync.dma_start(out=tld[tc % 2][:, :],
                                   in_=atm_d.ap()[tc])
                L.inc(i, qS)
                L.set(f"tldm{tc}", qS)

            for dc2 in range(NDC):
                it = tc * NDC + dc2
                r = it % 2

                @blk.tensor
                def _(tensor, tc=tc, dc2=dc2, it=it, r=r):
                    if dc2 == 0:
                        L.wait(tensor, qS, f"tldm{tc}")
                    if it >= 2:
                        L.wait(tensor, vE, f"tcp{it - 2}")
                    i = tensor.transpose(
                        ps[2 + r].ap().bitcast(BF16)[:, 0:128],
                        tld[tc % 2][:, 128 * dc2:128 * dc2 + 128],
                        c_idB[:, :])
                    L.inc(i, tE)
                    L.set(f"ttp{it}", tE)
                    if dc2 == NDC - 1:
                        L.mark[f"ttpl{tc}"] = L.at(tE)

                @blk.vector
                def _(vector, tc=tc, dc2=dc2, it=it, r=r):
                    L.wait(vector, tE, f"ttp{it}")
                    i = vector.tensor_copy(
                        a_afm[:, dc2 * TOWN + 128 * tc:
                              dc2 * TOWN + 128 * tc + 128],
                        ps[2 + r].ap().bitcast(BF16)[:, 0:128])
                    L.inc(i, vE)
                    L.set(f"tcp{it}", vE)
        L.mark["afm_done"] = L.at(vE)

        if tap is not None and tap[0] == "afm":
            @blk.gpsimd
            def _(g):
                g.wait_ge(vE, L.mark["afm_done"])
                for dc2 in range(NDC):
                    i = g.dma_start(
                        out=dbg.rearrange("(c p) t -> c p t", p=128)[dc2],
                        in_=a_afm[:, dc2 * TOWN:(dc2 + 1) * TOWN])
                    L.inc(i, qG)
                g.wait_ge(qG, L.at(qG))
        if stage < 6:
            _finish(nc, L, blk, qG, out, backing)
            return nc

        for ec in range(NDC):
            @blk.sync
            def _(sync, ec=ec):
                if ec == 0:
                    L.wait(sync, vE, "afm_done")
                if ec >= 2:
                    L.wait(sync, vE, f"wo_wc{ec - 2}")
                i = sync.dma_start(
                    out=wtl[ec % 2][:, :],
                    in_=woTd.rearrange("(c p) d -> c p d", p=128)[ec])
                L.inc(i, qS)
                L.set(f"wo_wl{ec}", qS)

            @blk.vector
            def _(vector, ec=ec):
                L.wait(vector, qS, f"wo_wl{ec}")
                i = vector.tensor_copy(wob[:, ec * 1024:(ec + 1) * 1024],
                                       wtl[ec % 2][:, :])
                L.inc(i, vE)
                L.set(f"wo_wc{ec}", vE)

        for ec in range(NDC):
            for th in range(2):
                tts = (2 * th, 2 * th + 1)
                for tt in tts:
                    @blk.tensor
                    def _(tensor, ec=ec, tt=tt):
                        it = ec * 4 + tt
                        if it == 0:
                            L.wait(tensor, vE, f"wo_wc{NDC - 1}")
                        if it >= 2:
                            L.wait(tensor, sE, f"oev{it - 2}")
                        for dc2 in range(NDC):
                            i = tensor.matmul(
                                ps[it % 2].ap()[:, 0:512],
                                wob[:, dc2 * 1024 + 128 * ec:
                                    dc2 * 1024 + 128 * ec + 128],
                                a_afm[:, dc2 * TOWN + 512 * tt:
                                      dc2 * TOWN + 512 * tt + 512],
                                start=(dc2 == 0), stop=(dc2 == NDC - 1))
                        L.inc(i, tE)
                        L.set(f"omm{it}", tE)

                @blk.scalar
                def _(scalar, ec=ec, tts=tts):
                    for tt in tts:
                        it = ec * 4 + tt
                        L.wait(scalar, tE, f"omm{it}")
                        if it >= 2:
                            L.wait(scalar, vE, f"ost{it - 2}")
                        dst = evt[:, :] if it % 2 == 0 else silq2[:, :]
                        i = scalar.activation(dst, ps[it % 2].ap()[:, 0:512],
                                              AF.Identity,
                                              bias=c_bo[:, ec:ec + 1])
                        L.inc(i, sE)
                        L.set(f"oev{it}", sE)

                for tt in tts:
                    @blk.sync
                    def _(sync, ec=ec, tt=tt):
                        it = ec * 4 + tt
                        if it >= 2:
                            L.wait(sync, vE, f"ost{it - 2}")
                        i = sync.dma_start(
                            out=xno[it % 2][:, :],
                            in_=xTd.rearrange("(c p) t -> c p t", p=128)[ec][
                                :, 512 + 512 * tt:1024 + 512 * tt])
                        L.inc(i, qS)
                        L.set(f"oxl{it}", qS)

                    @blk.vector
                    def _(vector, ec=ec, tt=tt):
                        it = ec * 4 + tt
                        L.wait(vector, sE, f"oev{it}")
                        L.wait(vector, qS, f"oxl{it}")
                        if it >= 2:
                            L.wait(vector, qG, f"oxs{it - 2}")
                        src = evt[:, :] if it % 2 == 0 else silq2[:, :]
                        i = vector.scalar_tensor_tensor(
                            outt[it % 2][:, :], src, c_ga[:, ec:ec + 1],
                            xno[it % 2][:, :], ALU.mult, ALU.add)
                        L.inc(i, vE)
                        L.set(f"ost{it}", vE)

                    @blk.gpsimd
                    def _(g, ec=ec, tt=tt):
                        it = ec * 4 + tt
                        L.wait(g, vE, f"ost{it}")
                        i = g.dma_start(
                            out=xnew_d.rearrange("(c p) t -> c p t", p=128)[ec][
                                :, 512 * tt:512 * tt + 512],
                            in_=outt[it % 2][:, :])
                        L.inc(i, qG)
                        L.set(f"oxs{it}", qG)
        L.mark["xnew_done"] = L.at(qG)

        if tap is not None and tap[0] == "xnew":
            @blk.gpsimd
            def _(g):
                g.wait_ge(qG, L.mark["xnew_done"])
                for dc in range(NDC):
                    i = g.dma_start(
                        out=dbg.rearrange("(c p) t -> c p t", p=128)[dc],
                        in_=xnew_d.rearrange("(c p) t -> c p t", p=128)[dc])
                    L.inc(i, qG)
                g.wait_ge(qG, L.at(qG))
        if stage < 7:
            _finish(nc, L, blk, qG, out, backing)
            return nc

        emit_ln("L2", xnew_d, TOWN, a_h2, TOWN, c_g2, c_mod, 24,
                first_wait="xnew_done")

        @blk.vector
        def _(vector):
            vector.wait_ge(vE, L.mark["L2_done_v"])
            L.chain(vector, vE, vector.memset(c_mx[:, :], 0.0))
            for dc in range(NDC):
                L.chain(vector, vE, vector.reduce_max(
                    c_mxt[:, :], a_h2[:, dc * TOWN:(dc + 1) * TOWN], axis=AX,
                    apply_absolute_value=True))
                L.chain(vector, vE, vector.tensor_max(c_mx[:, :], c_mxt[:, :],
                                                      c_mx[:, :]))
            L.set("m1_local", vE)

        @blk.gpsimd
        def _(g):
            L.wait(g, vE, "m1_local")
            i = g.dma_start(out=agm1_i.rearrange("(p x) -> p x", x=1),
                            in_=c_mx[:, :])
            L.inc(i, qG)
            L.set("m1_sent", qG)
            g.wait_ge(qG, L.at(qG))
            i = g.collective_compute("AllGather", ALU.bypass, replica_groups=RG,
                                     ins=[agm1_i.ap().opt()],
                                     outs=[agm1_o.ap().opt()])
            L.inc(i, cC)
            L.set("ag_m1", cC)

        @blk.sync
        def _(sync):
            L.wait(sync, cC, "ag_m1")
            i = sync.dma_start(out=c_mrow[0:1, :],
                               in_=agm1_o.rearrange("(a t) -> a t", a=1))
            L.inc(i, qS)
            L.set("m1_row", qS)

        @blk.vector
        def _(vector):
            L.wait(vector, qS, "m1_row")
            L.chain(vector, vE, vector.reduce_max(c_sc[0:1, 0:1], c_mrow[0:1, :],
                                                  axis=AX))
            L.set("m1_gmax", vE)

        @blk.scalar
        def _(scalar):
            L.wait(scalar, qS, "p0_small")
            L.chain(scalar, sE, scalar.activation(c_sc[0:1, 6:7],
                                                  c_clip[0:1, 0:1], AF.Exp,
                                                  scale=-1.0))
            L.chain(scalar, sE, scalar.activation(c_sc[0:1, 14:15],
                                                  c_clip[0:1, 1:2], AF.Exp,
                                                  scale=-1.0))
            L.set("sigE", sE)

        @blk.vector
        def _(vector):
            L.wait(vector, sE, "sigE")
            L.chain(vector, vE, vector.tensor_single_scalar(
                c_sc[0:1, 7:8], c_sc[0:1, 6:7], 1.0, ALU.add))
            L.chain(vector, vE, vector.reciprocal(c_sc[0:1, 1:2],
                                                  c_sc[0:1, 7:8]))
            L.chain(vector, vE, vector.tensor_single_scalar(
                c_sc[0:1, 15:16], c_sc[0:1, 14:15], 1.0, ALU.add))
            L.chain(vector, vE, vector.reciprocal(c_sc[0:1, 9:10],
                                                  c_sc[0:1, 15:16]))
            L.set("sig_done", vE)

        @blk.vector
        def _(vector):
            L.wait(vector, vE, "m1_gmax")
            L.wait(vector, vE, "sig_done")
            L.chain(vector, vE, vector.tensor_mul(c_sc[0:1, 2:3], c_sc[0:1, 0:1],
                                                  c_sc[0:1, 1:2]))
            L.chain(vector, vE, vector.tensor_mul(c_sc[0:1, 3:4], c_sc[0:1, 2:3],
                                                  c_r127[0:1, 0:1]))
            L.set("s1_done", vE)

        @blk.scalar
        def _(scalar):
            L.wait(scalar, vE, "s1_done")
            L.chain(scalar, sE, scalar.activation(c_sc[0:1, 4:5], c_sc[0:1, 3:4],
                                                  AF.Ln))
            L.chain(scalar, sE, scalar.activation(c_sc[0:1, 5:6], c_sc[0:1, 4:5],
                                                  AF.Exp, scale=-1.0))
            L.set("inv1_done", sE)

        @blk.tensor
        def _(tensor):
            L.wait(tensor, sE, "inv1_done")
            tensor.matmul(ps[0].ap()[:, 0:1], c_1rf[0:1, :], c_sc[0:1, 5:6],
                          start=True, stop=True)
            i = tensor.matmul(ps[0].ap()[:, 1:2], c_1rf[0:1, :], c_sc[0:1, 3:4],
                              start=True, stop=True)
            L.inc(i, tE)
            L.set("rep1", tE)

        @blk.vector
        def _(vector):
            L.wait(vector, tE, "rep1")
            L.chain(vector, vE, vector.tensor_copy(c_inv1[:, :],
                                                   ps[0].ap()[:, 0:1]))
            L.chain(vector, vE, vector.tensor_copy(c_s1r[:, :],
                                                   ps[0].ap()[:, 1:2]))
            L.set("rep1v", vE)

        for dc in range(NDC):
            @blk.vector
            def _(vector, dc=dc):
                if dc == 0:
                    L.wait(vector, vE, "rep1v")
                L.chain(vector, vE, vector.tensor_scalar(
                    vtmp[:, :], a_h2[:, dc * TOWN:(dc + 1) * TOWN],
                    c_inv1[:, 0:1], None, ALU.mult))
                L.chain(vector, vE, vector.tensor_copy(q8t[:, :], vtmp[:, :]))
                i = vector.tensor_copy(a_cx[:, dc * TOWN:(dc + 1) * TOWN],
                                       q8t[:, :])
                L.inc(i, vE)
                L.set(f"cx{dc}", vE)
        L.mark["cx_done"] = L.at(vE)

        if tap is not None and tap[0] == "cx":
            @blk.gpsimd
            def _(g):
                g.wait_ge(vE, L.mark["cx_done"])
                for dc in range(NDC):
                    i = g.dma_start(
                        out=dbg.rearrange("(c p) t -> c p t", p=128)[dc],
                        in_=a_cx[:, dc * TOWN:(dc + 1) * TOWN])
                    L.inc(i, qG)
                g.wait_ge(qG, L.at(qG))
        if stage < 8:
            _finish(nc, L, blk, qG, out, backing)
            return nc

        @blk.gpsimd
        def _(g):
            L.wait(g, cC, "ag_sc")
            sf = scflat.rearrange("(a t) -> a t", a=1)
            so = agsc_o.rearrange("(a t) -> a t", a=1)
            for r_ in range(8):
                i = g.dma_start(out=sf[0:1, r_ * ICORE:(r_ + 1) * ICORE],
                                in_=so[0:1, r_ * 832:r_ * 832 + ICORE])
                L.inc(i, qG)
                i = g.dma_start(out=sf[0:1, IFF + r_ * ICORE:IFF + (r_ + 1) * ICORE],
                                in_=so[0:1, r_ * 832 + ICORE:r_ * 832 + 2 * ICORE])
                L.inc(i, qG)
                i = g.dma_start(out=sf[0:1, 2 * IFF + r_ * 128:2 * IFF + (r_ + 1) * 128],
                                in_=so[0:1, r_ * 832 + 704:r_ * 832 + 832])
                L.inc(i, qG)
            L.set("scflat_done", qG)

        @blk.sync
        def _(sync):
            L.wait(sync, qG, "scflat_done")
            L.wait(sync, vE, "L2_done_v")
            i = sync.dma_start(out=sc22[:, :],
                               in_=scflat.rearrange("(q p) -> q p", q=52,
                                                    p=128)[0:22])
            L.inc(i, qS)
            L.set("sg22", qS)

        @blk.tensor
        def _(tensor):
            L.wait(tensor, qS, "sg22")
            i = tensor.transpose(ps[1].ap()[:, 0:22], sc22[:, :],
                                 c_idF[0:22, 0:22])
            L.inc(i, tE)
            L.set("sgT", tE)

        @blk.vector
        def _(vector):
            L.wait(vector, tE, "sgT")
            L.chain(vector, vE, vector.tensor_copy(c_sg[:, :],
                                                   ps[1].ap()[:, 0:22]))
            L.set("sgv", vE)

        @blk.sync
        def _(sync):
            L.wait(sync, vE, "sgv")
            i = sync.dma_start(
                out=sc22[:, :],
                in_=scflat.rearrange("(q p) -> q p", q=52, p=128)[22:44])
            L.inc(i, qS)
            L.set("su22", qS)

        @blk.tensor
        def _(tensor):
            L.wait(tensor, qS, "su22")
            i = tensor.transpose(ps[1].ap()[:, 0:22], sc22[:, :],
                                 c_idF[0:22, 0:22])
            L.inc(i, tE)
            L.set("suT", tE)

        @blk.vector
        def _(vector):
            L.wait(vector, tE, "suT")
            L.chain(vector, vE, vector.tensor_copy(c_su[:, :],
                                                   ps[1].ap()[:, 0:22]))
            L.set("suv", vE)

        @blk.sync
        def _(sync):
            L.wait(sync, vE, "suv")
            i = sync.dma_start(
                out=sc8r[:, :],
                in_=scflat.rearrange("(q p) -> q p", q=52, p=128)[44:52])
            L.inc(i, qS)
            L.set("sd8", qS)

        @blk.tensor
        def _(tensor):
            L.wait(tensor, qS, "sd8")
            i = tensor.transpose(ps[1].ap()[:, 0:8], sc8r[:, :], c_idF[0:8, 0:8])
            L.inc(i, tE)
            L.set("sdT", tE)

        @blk.vector
        def _(vector):
            L.wait(vector, tE, "sdT")
            L.chain(vector, vE, vector.tensor_copy(c_sd[:, :],
                                                   ps[1].ap()[:, 0:8]))
            L.chain(vector, vE, vector.tensor_scalar(c_sgc[:, :], c_sg[:, :],
                                                     c_s1r[:, 0:1], None,
                                                     ALU.mult))
            L.chain(vector, vE, vector.tensor_scalar(c_suc[:, :], c_su[:, :],
                                                     c_s1r[:, 0:1], None,
                                                     ALU.mult))
            L.wait(vector, qG, "m1_sent")
            L.chain(vector, vE, vector.memset(c_mx[:, :], 0.0))
            L.set("ffnsc_done", vE)

        def wtile_dmas(sync, agout, r, ic, which):
            g0 = 128 * ic
            runs = []
            while g0 < 128 * (ic + 1):
                s_ = g0 // ICORE
                l0 = g0 % ICORE
                w_ = min(ICORE - l0, 128 * (ic + 1) - g0)
                runs.append((s_, l0, g0 - 128 * ic, w_))
                g0 += w_
            for s_, l0, off, w_ in runs:
                i = sync.dma_start(
                    out=(wgt if which == "g" else wut)[r].rearrange(
                        "p (c i) -> p c i", c=NDC)[:, :, off:off + w_],
                    in_=agout.rearrange("(s c p i) -> s p c i", s=8, c=NDC,
                                        p=128)[s_][:, :, l0:l0 + w_])
                L.inc(i, qS)

        for ic in range(NIC):
            r = ic % 2

            @blk.sync
            def _(sync, ic=ic, r=r):
                if ic == 0:
                    sync.wait_ge(cC, L.mark["ag_wg"])
                    sync.wait_ge(cC, L.mark["ag_wu"])
                    L.wait(sync, vE, "L2_done_v")
                if ic >= 2:
                    L.wait(sync, tE, f"gu_mm{ic - 2}")
                wtile_dmas(sync, agwg_o, r, ic, "g")
                wtile_dmas(sync, agwu_o, r, ic, "u")
                L.set(f"gu_wl{ic}", qS)

            for tt in range(4):
                @blk.tensor
                def _(tensor, ic=ic, r=r, tt=tt):
                    if tt == 0:
                        L.wait(tensor, qS, f"gu_wl{ic}")
                        if ic == 0:
                            tensor.wait_ge(vE, L.mark["cx_done"])
                            tensor.wait_ge(vE, L.mark["ffnsc_done"])
                    if ic >= 1:
                        L.wait(tensor, vE, f"hrt{ic - 1}_{tt}")
                    for ec in range(NDC):
                        tensor.matmul(
                            ps[tt].ap()[:, 0:512],
                            wgt[r][:, ec * 128:ec * 128 + 128],
                            a_cx[:, ec * TOWN + 512 * tt:
                                 ec * TOWN + 512 * tt + 512],
                            start=(ec == 0), stop=(ec == NDC - 1))
                    for ec in range(NDC):
                        i = tensor.matmul(
                            ps[tt].ap()[:, 512:1024],
                            wut[r][:, ec * 128:ec * 128 + 128],
                            a_cx[:, ec * TOWN + 512 * tt:
                                 ec * TOWN + 512 * tt + 512],
                            start=(ec == 0), stop=(ec == NDC - 1))
                    L.inc(i, tE)
                    L.set(f"gu_mmt{ic}_{tt}", tE)
                    if tt == 3:
                        L.mark[f"gu_mm{ic}"] = L.at(tE)

            for hv in range(2):
                @blk.scalar
                def _(scalar, ic=ic, hv=hv):
                    for tt in (2 * hv, 2 * hv + 1):
                        L.wait(scalar, tE, f"gu_mmt{ic}_{tt}")
                        if tt >= 2:
                            L.wait(scalar, vE, f"hrt{ic}_{tt - 2}")
                        i = scalar.activation(
                            silq[:, :] if tt % 2 == 0 else silq2[:, :],
                            ps[tt].ap()[:, 0:512], AF.Silu,
                            scale=c_sgc[:, ic:ic + 1])
                        L.inc(i, sE)
                        L.set(f"sil{ic}_{tt}", sE)

                @blk.vector
                def _(vector, ic=ic, hv=hv):
                    for tt in (2 * hv, 2 * hv + 1):
                        L.wait(vector, sE, f"sil{ic}_{tt}")
                        i = vector.scalar_tensor_tensor(
                            a_hr[:, ic * TOWN + 512 * tt:
                                 ic * TOWN + 512 * tt + 512],
                            ps[tt].ap()[:, 512:1024], c_suc[:, ic:ic + 1],
                            silq[:, :] if tt % 2 == 0 else silq2[:, :],
                            ALU.mult, ALU.mult)
                        L.inc(i, vE)
                        L.set(f"hrt{ic}_{tt}", vE)

            @blk.vector
            def _(vector, ic=ic):
                L.chain(vector, vE, vector.reduce_max(
                    c_mxt[:, :], a_hr[:, ic * TOWN:(ic + 1) * TOWN], axis=AX,
                    apply_absolute_value=True))
                L.chain(vector, vE, vector.tensor_max(c_mx[:, :], c_mxt[:, :],
                                                      c_mx[:, :]))
                L.set(f"hr{ic}", vE)
        L.mark["hr_done"] = L.at(vE)

        @blk.scalar
        def _(scalar):
            L.wait(scalar, vE, f"hrt{NIC - 1}_3")
            pre6(scalar, 2)

        @blk.gpsimd
        def _(g):
            g.wait_ge(vE, L.mark["hr_done"])
            i = g.dma_start(out=agm2_i.rearrange("(p x) -> p x", x=1),
                            in_=c_mx[:, :])
            L.inc(i, qG)
            g.wait_ge(qG, L.at(qG))
            i = g.collective_compute("AllGather", ALU.bypass, replica_groups=RG,
                                     ins=[agm2_i.ap().opt()],
                                     outs=[agm2_o.ap().opt()])
            L.inc(i, cC)
            L.set("ag_m2", cC)

        @blk.sync
        def _(sync):
            L.wait(sync, cC, "ag_m2")
            i = sync.dma_start(out=c_mrow[0:1, :],
                               in_=agm2_o.rearrange("(a t) -> a t", a=1))
            L.inc(i, qS)
            L.set("m2_row", qS)

        @blk.vector
        def _(vector):
            L.wait(vector, qS, "m2_row")
            L.wait(vector, vE, "sig_done")
            L.chain(vector, vE, vector.reduce_max(c_sc[0:1, 8:9],
                                                  c_mrow[0:1, :], axis=AX))
            L.chain(vector, vE, vector.tensor_mul(c_sc[0:1, 10:11],
                                                  c_sc[0:1, 8:9],
                                                  c_sc[0:1, 9:10]))
            L.chain(vector, vE, vector.tensor_mul(c_sc[0:1, 11:12],
                                                  c_sc[0:1, 10:11],
                                                  c_r127[0:1, 0:1]))
            L.set("s2_done", vE)

        @blk.scalar
        def _(scalar):
            L.wait(scalar, vE, "s2_done")
            L.chain(scalar, sE, scalar.activation(c_sc[0:1, 12:13],
                                                  c_sc[0:1, 11:12], AF.Ln))
            L.chain(scalar, sE, scalar.activation(c_sc[0:1, 13:14],
                                                  c_sc[0:1, 12:13], AF.Exp,
                                                  scale=-1.0))
            L.set("inv2_done", sE)

        @blk.tensor
        def _(tensor):
            L.wait(tensor, sE, "inv2_done")
            tensor.matmul(ps[0].ap()[:, 0:1], c_1rf[0:1, :], c_sc[0:1, 13:14],
                          start=True, stop=True)
            i = tensor.matmul(ps[0].ap()[:, 1:2], c_1rf[0:1, :], c_sc[0:1, 11:12],
                              start=True, stop=True)
            L.inc(i, tE)
            L.set("rep2", tE)

        @blk.vector
        def _(vector):
            L.wait(vector, tE, "rep2")
            L.chain(vector, vE, vector.tensor_copy(c_inv2[:, :],
                                                   ps[0].ap()[:, 0:1]))
            L.chain(vector, vE, vector.tensor_copy(c_s2r[:, :],
                                                   ps[0].ap()[:, 1:2]))
            L.chain(vector, vE, vector.tensor_scalar(c_sdc[:, :], c_sd[:, :],
                                                     c_s2r[:, 0:1], None,
                                                     ALU.mult))
            L.set("rep2v", vE)

        for ic in range(NIC):
            @blk.vector
            def _(vector, ic=ic):
                if ic == 0:
                    L.wait(vector, vE, "rep2v")
                L.chain(vector, vE, vector.tensor_scalar(
                    vtmp[:, :], a_hr[:, ic * TOWN:(ic + 1) * TOWN],
                    c_inv2[:, 0:1], None, ALU.mult))
                L.chain(vector, vE, vector.tensor_copy(q8t[:, :], vtmp[:, :]))
                i = vector.tensor_copy(a_hr[:, ic * TOWN:(ic + 1) * TOWN],
                                       q8t[:, :])
                L.inc(i, vE)
                L.set(f"ch{ic}", vE)
        L.mark["ch_done"] = L.at(vE)
        if stage < 9:
            _finish(nc, L, blk, qG, out, backing)
            return nc

        for ec in range(NDC):
            r = ec % 2

            @blk.sync
            def _(sync, ec=ec, r=r):
                if ec == 0:
                    sync.wait_ge(cC, L.mark["ag_wd"])
                    L.wait(sync, vE, "L2_done_v")
                if ec >= 2:
                    L.wait(sync, tE, f"d_mm{ec - 2}_3")
                i = sync.dma_start(
                    out=wdt[r].rearrange("p (ic e) -> p ic e", ic=NIC),
                    in_=agwd_o.rearrange("(s ic p e) -> s p ic e", s=8, ic=NIC,
                                         p=128)[ec])
                L.inc(i, qS)
                L.set(f"d_wl{ec}", qS)

            for th in range(2):
                tts = (2 * th, 2 * th + 1)
                for tt in tts:
                    @blk.tensor
                    def _(tensor, ec=ec, tt=tt, r=r):
                        it = ec * 4 + tt
                        if tt == 0:
                            L.wait(tensor, qS, f"d_wl{ec}")
                        if it == 0:
                            tensor.wait_ge(vE, L.mark["ch_done"])
                        if it >= 2:
                            L.wait(tensor, sE, f"d_ev{it - 2}")
                        for ic in range(NIC):
                            i = tensor.matmul(
                                ps[it % 2].ap()[:, 0:512],
                                wdt[r][:, ic * 128:ic * 128 + 128],
                                a_hr[:, ic * TOWN + 512 * tt:
                                     ic * TOWN + 512 * tt + 512],
                                start=(ic == 0), stop=(ic == NIC - 1))
                        L.inc(i, tE)
                        L.set(f"d_mm{ec}_{tt}", tE)

                @blk.scalar
                def _(scalar, ec=ec, tts=tts):
                    for tt in tts:
                        it = ec * 4 + tt
                        L.wait(scalar, tE, f"d_mm{ec}_{tt}")
                        if it >= 2:
                            L.wait(scalar, vE, f"d_st{it - 2}")
                        dst = evt[:, :] if it % 2 == 0 else silq2[:, :]
                        i = scalar.activation(dst, ps[it % 2].ap()[:, 0:512],
                                              AF.Identity,
                                              scale=c_sdc[:, ec:ec + 1])
                        L.inc(i, sE)
                        L.set(f"d_ev{it}", sE)

                for tt in tts:
                    @blk.sync
                    def _(sync, ec=ec, tt=tt):
                        it = ec * 4 + tt
                        if it >= 2:
                            L.wait(sync, vE, f"d_st{it - 2}")
                        i = sync.dma_start(
                            out=xno[it % 2][:, :],
                            in_=xnew_d.rearrange("(c p) t -> c p t", p=128)[ec][
                                :, 512 * tt:512 * tt + 512])
                        L.inc(i, qS)
                        L.set(f"d_xl{it}", qS)

                    @blk.vector
                    def _(vector, ec=ec, tt=tt):
                        it = ec * 4 + tt
                        L.wait(vector, sE, f"d_ev{it}")
                        L.wait(vector, qS, f"d_xl{it}")
                        if it >= 2:
                            L.wait(vector, qG, f"d_os{it - 2}")
                        src_ = evt[:, :] if it % 2 == 0 else silq2[:, :]
                        i = vector.scalar_tensor_tensor(
                            outt[it % 2][:, :], src_, c_gb[:, ec:ec + 1],
                            xno[it % 2][:, :], ALU.mult, ALU.add)
                        L.inc(i, vE)
                        L.set(f"d_st{it}", vE)

                    @blk.gpsimd
                    def _(g, ec=ec, tt=tt):
                        it = ec * 4 + tt
                        L.wait(g, vE, f"d_st{it}")
                        i = g.dma_start(
                            out=out.rearrange("(c p) t -> c p t", p=128)[ec][
                                :, 512 * tt:512 * tt + 512],
                            in_=outt[it % 2][:, :])
                        L.inc(i, qG)
                        L.set(f"d_os{it}", qG)

        @blk.gpsimd
        def _(g):
            g.wait_ge(qG, L.at(qG))

        return nc



def _finish(nc, L, blk, qG, out, backing):
    @blk.gpsimd
    def _(g):
        av = backing.ap().bitcast(F32)
        for dc in range(NDC):
            i = g.dma_start(out=out.rearrange("(c p) t -> c p t", p=128)[dc],
                            in_=av[:, 0:TOWN])
            L.inc(i, qG)
        g.wait_ge(qG, L.at(qG))


def _stage_host(inputs, core):
    b, s = core // 2, core % 2
    x = np.asarray(inputs["L0_attn_x"])
    g0 = 2048 * s
    fr = np.zeros((FR, E), np.float32)
    lo, hi = g0 - 512, g0 + 2560
    cl, ch = max(lo, 0), min(hi, S)
    fr[cl - lo:ch - lo] = x[b, cl:ch]
    d = {
        "xT": np.ascontiguousarray(fr.T),
        "wqT": np.ascontiguousarray(np.asarray(inputs["L0_attn_wq"]).T),
        "wkT": np.ascontiguousarray(np.asarray(inputs["L0_attn_wk"]).T),
        "woT": np.ascontiguousarray(np.asarray(inputs["L0_attn_wo"]).T),
        "cTd": np.ascontiguousarray(
            np.asarray(inputs["L0_c"]).T.reshape(NDC, 128, B)
            .transpose(1, 0, 2).reshape(128, NDC * B)),
        "clipx": np.asarray(inputs["L0_ffn_clip_x"], np.float32).reshape(1, 1),
        "clipd": np.asarray(inputs["L0_ffn_clip_down"], np.float32).reshape(1, 1),
        "wmods": np.ascontiguousarray(
            np.asarray(inputs["L0_wmod"])[:, 768 * core:768 * (core + 1)]),
    }
    wv = np.asarray(inputs["L0_attn_wv"])
    wvT = np.zeros((E, DA), np.float32)
    bvA = np.zeros((1, DA), np.float32)
    bv = np.asarray(inputs["L0_attn_bv"])
    for h in range(H):
        wvT[:, 65 * h:65 * h + 64] = wv[64 * h:64 * h + 64, :].T
        bvA[0, 65 * h:65 * h + 64] = bv[64 * h:64 * h + 64]
        bvA[0, 65 * h + 64] = 1.0
    d["wvTa"] = wvT
    d["bva"] = bvA
    pv = np.full((128, DA), -1.0, np.float32)
    pv[:, 64::65] = 1.0
    d["padvd"] = pv.astype(BF)
    for nm, src in (("bq2", "L0_attn_bq"), ("bk2", "L0_attn_bk"),
                    ("bo2", "L0_attn_bo")):
        d[nm] = np.ascontiguousarray(
            np.asarray(inputs[src]).reshape(NDC, 128).T)
    wg = np.asarray(inputs["L0_ffn_w_gate"])
    wu = np.asarray(inputs["L0_ffn_w_up"])
    wd = np.asarray(inputs["L0_ffn_w_down"])
    d["wgN"] = np.ascontiguousarray(wg[ICORE * core:ICORE * (core + 1)])
    d["wuN"] = np.ascontiguousarray(wu[ICORE * core:ICORE * (core + 1)])
    d["wgTs"] = np.ascontiguousarray(wg.T[:, ICORE * core:ICORE * (core + 1)])
    d["wuTs"] = np.ascontiguousarray(wu.T[:, ICORE * core:ICORE * (core + 1)])
    d["wdN"] = np.ascontiguousarray(wd[128 * core:128 * (core + 1)])
    pk = 0 if s == 0 else 2560
    ptc = 0 if s == 0 else 20
    woff = [1536, 1536, 1536, 1536]
    if s == 0:
        woff[0] = 0
    else:
        woff[3] = 1024
    d["padsi"] = np.array([[pk, ptc * DA, b, ptc] + woff], np.uint32)
    p = np.arange(128)[:, None]
    i_ = np.arange(512)[None, :]
    m0 = np.ones((128, 2048), np.float32)
    m3 = np.ones((128, 2048), np.float32)
    if s == 1:
        for jc in range(4):
            m0[:, 512 * jc:512 * jc + 512] = (i_ <= 128 * jc + p)
    if s == 0:
        for k in range(4):
            m3[:, 512 * k:512 * k + 512] = (i_ >= 128 * k + p)
    d["m0d"] = m0.astype(BF)
    d["m3d"] = m3.astype(BF)
    q = np.arange(128)[None, :]
    d["t1d"] = (q <= p).astype(BF)
    d["t2d"] = (q >= p).astype(BF)
    d["idnB"] = np.eye(128).astype(BF)
    d["idnF"] = np.eye(128).astype(np.float32)
    d["lnod"] = np.full((128, 128), 1.0 / 1024.0).astype(BF)
    d["onesRFd"] = np.ones((1, 128), np.float32)
    return d


def kernel(**inputs):
    nc = build()
    in_maps = [_stage_host(inputs, c) for c in range(8)]
    res = run_bass_kernel_spmd(nc, in_maps, core_ids=list(range(8)))
    outp = np.zeros((B, S, E), np.float32)
    for c in range(8):
        b, s = c // 2, c % 2
        outp[b, 2048 * s:2048 * (s + 1), :] = res.results[c]["out"].T
    return outp


# version 20 (final)
# speedup vs baseline: 1.2255x; 1.0323x over previous
import numpy as np
import ml_dtypes
import concourse.bass as bass
import concourse.mybir as mybir
import concourse.expressions_rust as er
from concourse.bass_utils import run_bass_kernel_spmd

F32 = mybir.dt.float32
BF16 = mybir.dt.bfloat16
I8 = mybir.dt.int8
U32 = mybir.dt.uint32
AF = mybir.ActivationFunctionType
ALU = mybir.AluOpType
AX = mybir.AxisListType.X
BF = ml_dtypes.bfloat16

B, S, E, H, HD = 4, 4096, 1024, 16, 64
TOWN, FR = 2048, 3072
NDC = E // 128
IFF, ICORE, NIC = 2816, 352, 22
DA = H * (HD + 1)
NTC = FR // 128
USABLE = 208832
RG = [list(range(8))]


class Led:
    def __init__(self):
        self.c = {}
        self.mark = {}

    def inc(self, instr, sem, k=None):
        if k is None:
            k = 16 if sem.name in ("qS", "qG") else 1
        instr.then_inc(sem, k)
        self.c[sem.name] = self.c.get(sem.name, 0) + k
        return self.c[sem.name]

    def at(self, sem):
        return self.c.get(sem.name, 0)

    def set(self, name, sem):
        self.mark[name] = self.at(sem)

    def wait(self, eng, sem, name):
        eng.wait_ge(sem, self.mark[name])

    def chain(self, eng, sem, instr):
        instr.then_inc(sem, 1)
        self.c[sem.name] = self.c.get(sem.name, 0) + 1
        eng.wait_ge(sem, self.c[sem.name])


def build(stage=99, tap=None):
    nc = bass.Bass("TRN2", target_bir_lowering=False)
    L = Led()

    def din(name, shape, dt=F32):
        return nc.dram_tensor(name, shape, dt, kind="ExternalInput").ap()

    xTd = din("xT", [E, FR])
    wqTd = din("wqT", [E, E])
    wkTd = din("wkT", [E, E])
    wvTd = din("wvTa", [E, DA])
    woTd = din("woT", [E, E])
    bq2d = din("bq2", [128, NDC])
    bk2d = din("bk2", [128, NDC])
    bo2d = din("bo2", [128, NDC])
    bvad = din("bva", [1, DA])
    cTd = din("cTd", [128, NDC * B])
    wmodsd = din("wmods", [E, 768])
    clipxd = din("clipx", [1, 1])
    clipdd = din("clipd", [1, 1])
    wgNd = din("wgN", [ICORE, E])
    wgTd = din("wgTs", [E, ICORE])
    wuNd = din("wuN", [ICORE, E])
    wuTd = din("wuTs", [E, ICORE])
    wdNd = din("wdN", [128, IFF])
    padsd = din("padsi", [1, 8], U32)
    padvd = din("padvd", [128, DA], BF16)
    m0d = din("m0d", [128, 2048], BF16)
    m3d = din("m3d", [128, 2048], BF16)
    t1d = din("t1d", [128, 128], BF16)
    t2d = din("t2d", [128, 128], BF16)
    idnBd = din("idnB", [128, 128], BF16)
    idnFd = din("idnF", [128, 128], F32)
    lnod = din("lnod", [128, 128], BF16)
    onesRFd = din("onesRFd", [1, 128], F32)

    out = nc.dram_tensor("out", [E, TOWN], F32, kind="ExternalOutput").ap()
    dbg = None
    if tap is not None:
        dbg = nc.dram_tensor("dbg", list(tap[1]), F32, kind="ExternalOutput").ap()

    dt_ = nc.dram_tensor
    agmod_i = dt_("agmod_i", [B * 768], F32)
    agmod_o = dt_("agmod_o", [8 * B * 768], F32)
    modrow_d = dt_("modrow_d", [6 * E], F32)
    agwg_i = dt_("agwg_i", [E, ICORE], BF16)
    agwg_o = dt_("agwg_o", [8 * E * ICORE], BF16)
    agwu_i = dt_("agwu_i", [E, ICORE], BF16)
    agwu_o = dt_("agwu_o", [8 * E * ICORE], BF16)
    agwd_i = dt_("agwd_i", [IFF, 128], BF16)
    agwd_o = dt_("agwd_o", [8 * IFF * 128], BF16)
    agsc_i = dt_("agsc_i", [832], F32)
    agsc_o = dt_("agsc_o", [8 * 832], F32)
    scflat = dt_("scflat", [2 * IFF + E], F32)
    agm1_i = dt_("agm1_i", [128], F32)
    agm1_o = dt_("agm1_o", [8 * 128], F32)
    agm2_i = dt_("agm2_i", [128], F32)
    agm2_o = dt_("agm2_o", [8 * 128], F32)
    xnew_d = dt_("xnew_d", [E, TOWN], F32)
    atm_d = dt_("atm_d", [16, 128, E], BF16)
    qhd = dt_("qhd", [E, TOWN], BF16)
    qld = dt_("qld", [E, TOWN], BF16)
    khd = dt_("khd", [E, FR], BF16)
    kld = dt_("kld", [E, FR], BF16)

    backing = nc.alloc_sbuf_tensor("arena", [128, USABLE], mybir.dt.uint8)
    base_addr = nc.lookup_mloc(backing).addr

    def place(name, shape, dtype, off):
        nbytes = int(np.prod(shape[1:])) * mybir.dt.size(dtype)
        assert off % 32 == 0 and off + nbytes <= USABLE, (name, off, nbytes)
        return nc.alloc_sbuf_tensor_at(name, list(shape), dtype,
                                       offset=base_addr + off)

    o = [0]

    def cplace(name, shape, dtype):
        nbytes = int(np.prod(shape[1:])) * mybir.dt.size(dtype)
        h = place(name, shape, dtype, o[0])
        o[0] = (o[0] + nbytes + 31) // 32 * 32
        assert o[0] <= 11456, (name, o[0])
        return h

    c_bq = cplace("c_bq", [128, NDC], F32)
    c_bk = cplace("c_bk", [128, NDC], F32)
    c_bo = cplace("c_bo", [128, NDC], F32)
    c_eps = cplace("c_eps", [128, 1], F32)
    c_idB = cplace("c_idB", [128, 128], BF16)
    c_idF = cplace("c_idF", [128, 128], F32)
    c_ln1 = cplace("c_ln1", [128, 128], BF16)
    c_1rf = cplace("c_1rf", [1, 128], F32)
    c_1rb = cplace("c_1rb", [1, 128], BF16)
    c_t1 = cplace("c_t1", [128, 128], BF16)
    c_t2 = cplace("c_t2", [128, 128], BF16)
    c_bva = cplace("c_bva", [1, DA], BF16)
    c_pads = cplace("c_pads", [1, 8], U32)
    c_clip = cplace("c_clip", [1, 2], F32)
    c_sg = cplace("c_sg", [128, NIC], F32)
    c_su = cplace("c_su", [128, NIC], F32)
    c_sd = cplace("c_sd", [128, NDC], F32)
    c_sgc = cplace("c_sgc", [128, NIC], F32)
    c_suc = cplace("c_suc", [128, NIC], F32)
    c_sdc = cplace("c_sdc", [128, NDC], F32)
    c_inv1 = cplace("c_inv1", [128, 1], F32)
    c_inv2 = cplace("c_inv2", [128, 1], F32)
    c_s1r = cplace("c_s1r", [128, 1], F32)
    c_s2r = cplace("c_s2r", [128, 1], F32)
    c_sc = cplace("c_sc", [1, 16], F32)
    c_mod = cplace("c_mod", [128, 48], F32)
    c_ga = cplace("c_ga", [128, 8], F32)
    c_gb = cplace("c_gb", [128, 8], F32)
    c_g1 = cplace("c_g1", [128, 8], F32)
    c_g2 = cplace("c_g2", [128, 8], F32)
    c_csl = cplace("c_csl", [128, NDC * B], F32)
    c_csi = cplace("c_csi", [128, NDC * B], F32)
    c_mx = cplace("c_mx", [128, 1], F32)
    c_mxt = cplace("c_mxt", [128, 1], F32)
    c_zr = cplace("c_zr", [128, 1], F32)
    c_zr4 = cplace("c_zr4", [128, 4], F32)
    c_zcp = cplace("c_zcp", [128, 4], F32)
    c_negC = cplace("c_negC", [128, 1], F32)
    c_q25 = cplace("c_q25", [128, 1], F32)
    c_padv = cplace("c_padv", [128, DA], BF16)
    c_km1 = cplace("c_km1", [128, 512], BF16)
    c_k0 = cplace("c_k0", [128, 512], BF16)
    c_127 = cplace("c_127", [128, 1], F32)
    c_r127 = cplace("c_r127", [128, 1], F32)
    c_sgn = cplace("c_sgn", [128, 8], F32)
    c_mag = cplace("c_mag", [128, 8], F32)
    c_mag2 = cplace("c_mag2", [128, 8], F32)
    c_mag3 = cplace("c_mag3", [128, 8], F32)

    R1, R2, R3, R4, R5 = 11456, 44224, 76992, 109760, 158912
    xt = [place("xt0", [128, NDC * 512], F32, R1),
          place("xt1", [128, NDC * 512], F32, R1 + 16384)]
    kt2 = [place(f"kt2_{i}", [128, 2048], BF16, R1 + 4096 * i) for i in range(2)]
    qt2 = [place(f"qt2_{i}", [128, 1024], BF16, R1 + 8192 + 2048 * i) for i in range(2)]
    pstf = [place(f"pstf{i}", [128, 512], F32, R1 + 12288 + 2048 * i) for i in range(2)]
    psth = [place(f"psth{i}", [128, 512], BF16, R1 + 16384 + 1024 * i) for i in range(2)]
    pstl = [place(f"pstl{i}", [128, 512], BF16, R1 + 18432 + 1024 * i) for i in range(2)]
    vtmp = place("vtmp", [128, 2048], F32, R1)
    q8t = place("q8t", [128, 2048], I8, R1 + 8192)
    wgt = [place(f"wgt{i}", [128, NDC * 128], BF16, R2 + 2048 * i) for i in range(2)]
    wut = [place(f"wut{i}", [128, NDC * 128], BF16, R2 + 4096 + 2048 * i) for i in range(2)]
    wdt = [place(f"wdt{i}", [128, NIC * 128], BF16, R1 + 12288 + 5632 * i) for i in range(2)]
    sc22 = place("sc22", [22, 128], F32, R1 + 28672)
    sc8r = place("sc8r", [8, 128], F32, R1 + 29248)

    xb_t = place("xb_t", [128, NDC * 512], BF16, R2)
    xq_t = place("xq_t", [128, NDC * 512], BF16, R2 + 8192)
    muS = place("muS", [128, 512], F32, R2 + 16384)
    mu2 = place("mu2", [128, 512], F32, R2 + 18432)
    varS = place("varS", [128, 512], F32, R2 + 20480)
    lnvS = place("lnvS", [128, 512], F32, R2 + 22528)
    rstdS = place("rstdS", [128, 512], F32, R2 + 24576)
    mrtS = place("mrtS", [128, 512], F32, R2 + 26624)
    tmpA = place("tmpA", [128, 512], F32, R2 + 28672)
    tmpB = place("tmpB", [128, 512], F32, R2 + 30720)
    wtl = [place(f"wtl{i}", [128, 1024], F32, R2 + 4096 * i) for i in range(2)]
    wqbs = [place(f"wqbs{i}", [128, NDC * 128], BF16, R2 + 8192 + 2048 * i) for i in range(2)]
    silq = place("silq", [128, 512], F32, R2 + 16384)
    silq2 = place("silq2", [128, 512], F32, R2 + 18432)
    xno = [place(f"xno{i}", [128, 512], F32, R2 + 20480 + 2048 * i) for i in range(2)]
    outt = [place(f"outt{i}", [128, 512], F32, R2 + 24576 + 2048 * i) for i in range(2)]
    evt = place("evt", [128, 512], F32, R2 + 28672)
    tld = [place(f"tld{i}", [128, 1024], BF16, R2 + 2048 * i) for i in range(2)]

    sc48 = place("sc48", [48, 128], F32, R3)
    wvb = place("wvb", [128, NDC * DA], BF16, R3)
    wob = place("wob", [128, NDC * 1024], BF16, R5)
    a_cx = place("a_cx", [128, NDC * TOWN], BF16, R3)

    a_h1 = place("a_h1", [128, NDC * FR], BF16, R4)
    aE = [place("aE0", [128, 12 * 512], BF16, R4),
          place("aE1", [128, 12 * 512], BF16, R4 + 12288)]
    aEm = place("aEm", [128, 3072], BF16, R4 + 24576)
    c_mm = place("c_mm", [128, 2048], BF16, R4 + 30720)
    tstg = [place(f"tstg{i}", [128, 128], BF16, R4 + 34816 + 256 * i) for i in range(4)]
    a_afm = place("a_afm", [128, NDC * TOWN], BF16, R3)
    a_h2 = place("a_h2", [128, NDC * TOWN], F32, R4)
    a_hr = place("a_hr", [128, NIC * TOWN], BF16, R4)
    wpf = place("wpf", [128, 3072], F32, R4)
    wps = place("wps", [128, IFF], F32, R4 + 12288)
    wp8 = place("wp8", [128, IFF], I8, R4 + 23552)
    wpq = place("wpq", [128, IFF], BF16, R4 + 26368)
    wrep = place("wrep", [128, ICORE], F32, R4 + 32000)
    wsc = place("wsc", [128, 8], F32, R4 + 33408)
    wrow = place("wrow", [1, ICORE], F32, R4 + 33440)
    winv = place("winv", [1, ICORE], F32, R4 + 36288)
    wstg = [place(f"wstg{i}", [128, 128], BF16, R4 + 37696 + 256 * i) for i in range(2)]

    wmf = place("wmf", [128, NDC * 768], F32, R5)
    modp = place("modp", [4, 768], F32, R5 + 36864)
    c_bvaf = place("c_bvaf", [1, DA], F32, R5 + 40960)
    c_mrow = place("c_mrow", [1, 1024], F32, R5 + 45056)
    a_v = place("a_v", [128, NTC * DA], BF16, R5)

    ps = [nc.alloc_psum_tensor(f"pb{i}", [128, 1024], F32) for i in range(4)]

    with (
        nc.semaphore("qS") as qS,
        nc.semaphore("qG") as qG,
        nc.semaphore("tE") as tE,
        nc.semaphore("vE") as vE,
        nc.semaphore("sE") as sE,
        nc.semaphore("cC") as cC,
        nc.Block() as blk,
    ):
        @blk.sync
        def _(sync):
            for t, d in ((c_bq, bq2d), (c_bk, bk2d), (c_bo, bo2d), (c_idB, idnBd),
                         (c_idF, idnFd), (c_ln1, lnod), (c_1rf, onesRFd),
                         (c_t1, t1d), (c_t2, t2d), (c_padv, padvd),
                         (c_pads, padsd), (c_bvaf, bvad)):
                i = sync.dma_start(out=t[:, :], in_=d)
                L.inc(i, qS)
            i = sync.dma_start(out=c_clip[0:1, 0:1], in_=clipxd)
            L.inc(i, qS)
            i = sync.dma_start(out=c_clip[0:1, 1:2], in_=clipdd)
            L.inc(i, qS)
            L.set("p0_small", qS)
            i = sync.dma_start(out=c_csl[:, :], in_=cTd)
            L.inc(i, qS)
            L.set("p0_c", qS)
            for dc in range(NDC):
                i = sync.dma_start(
                    out=wmf[:, dc * 768:(dc + 1) * 768],
                    in_=wmodsd.rearrange("(c p) j -> c p j", p=128)[dc])
                L.inc(i, qS)
            L.set("p0_wmod", qS)

        @blk.vector
        def _(vector):
            vector.memset(c_eps[:, :], 1e-5)
            vector.memset(c_127[:, :], 127.0)
            vector.memset(c_negC[:, :], -30.0)
            vector.memset(c_q25[:, :], 0.25)
            vector.memset(c_km1[:, :], -1.0)
            vector.memset(c_k0[:, :], 0.0)
            vector.reciprocal(c_r127[:, :], c_127[:, :])
            vector.memset(c_mx[:, :], 0.0)
            L.wait(vector, qS, "p0_small")
            vector.tensor_copy(c_1rb[:, :], c_1rf[:, :])
            vector.tensor_copy(c_bva[:, :], c_bvaf[:, :])
            L.set("v_wmb", vE)

        PRE6 = ()

        def pre6(eng, which=0):
            if which not in PRE6:
                return
            eng.add_instruction(mybir.InstLoadActFuncSet(
                name=nc.get_next_instruction_name(), act_func_set_id=6,
                ins=[], outs=[]))

        @blk.scalar
        def _(scalar):
            L.wait(scalar, qS, "p0_c")
            i = scalar.activation(c_csi[:, :], c_csl[:, :], AF.Silu)
            L.inc(i, sE)
            L.set("s_csi", sE)

        @blk.tensor
        def _(tensor):
            L.wait(tensor, qS, "p0_wmod")
            L.wait(tensor, sE, "s_csi")
            for nt, (n0, n1) in enumerate(((0, 512), (512, 768))):
                for dc in range(NDC):
                    i = tensor.matmul(ps[2 + nt].ap()[0:4, 0:n1 - n0],
                                      c_csi[:, dc * 4:(dc + 1) * 4],
                                      wmf[:, dc * 768 + n0:dc * 768 + n1],
                                      start=(dc == 0), stop=(dc == NDC - 1))
            L.inc(i, tE)
            L.set("t_modp", tE)

        @blk.scalar
        def _(scalar):
            L.wait(scalar, tE, "t_modp")
            scalar.activation(modp[:, 0:512], ps[2].ap()[0:4, 0:512], AF.Identity)
            i = scalar.activation(modp[:, 512:768], ps[3].ap()[0:4, 0:256],
                                  AF.Identity)
            L.inc(i, sE)
            L.set("s_modp", sE)
            pre6(scalar, 1)

        @blk.gpsimd
        def _(g):
            L.wait(g, sE, "s_modp")
            i = g.dma_start(out=agmod_i.rearrange("(b j) -> b j", b=B),
                            in_=modp[:, :])
            L.inc(i, qG)
            g.wait_ge(qG, L.at(qG))
            i = g.collective_compute("AllGather", ALU.bypass, replica_groups=RG,
                                     ins=[agmod_i.ap().opt()],
                                     outs=[agmod_o.ap().opt()])
            L.inc(i, cC)
            L.wait(g, cC, None) if False else g.wait_ge(cC, L.at(cC))
            rb = g.alloc_register("rb")
            g.reg_load(rb, c_pads[0:1, 2:3])
            rbv = er.make_scalar_value(rb, min_val=0, max_val=3)
            i = g.dma_start(
                out=modrow_d.rearrange("(r j) -> r j", r=8),
                in_=agmod_o.rearrange("(r b j) -> b r j", r=8, b=B,
                                      j=768)[bass.ds(rbv, 1)])
            L.inc(i, qG)
            L.set("g_modrow", qG)

        @blk.sync
        def _(sync):
            L.wait(sync, qG, "g_modrow")
            i = sync.dma_start(out=sc48[:, :],
                               in_=modrow_d.rearrange("(q p) -> q p", q=48, p=128))
            L.inc(i, qS)
            L.set("p0_sc48", qS)

        @blk.tensor
        def _(tensor):
            L.wait(tensor, qS, "p0_sc48")
            i = tensor.transpose(ps[2].ap()[:, 0:48], sc48[:, :], c_idF[0:48, 0:48])
            L.inc(i, tE)
            L.set("t_mod48", tE)

        @blk.vector
        def _(vector):
            L.wait(vector, tE, "t_mod48")
            i = vector.tensor_copy(c_mod[:, :], ps[2].ap()[:, 0:48])
            L.inc(i, vE)
            L.set("v_cmod", vE)

        @blk.vector
        def _(vector):
            L.wait(vector, vE, "v_cmod")
            for src0, dstg in ((16, c_ga), (40, c_gb)):
                L.chain(vector, vE, vector.tensor_single_scalar(
                    c_sgn[:, :], c_mod[:, src0:src0 + 8], -1.0, ALU.mult))
                L.chain(vector, vE, vector.tensor_max(
                    c_mag[:, 0:8], c_mod[:, src0:src0 + 8], c_sgn[:, :]))
                L.chain(vector, vE, vector.tensor_scalar(
                    c_mag2[:, :], c_mag[:, 0:8], c_q25[:, 0:1], None, ALU.max))
                L.chain(vector, vE, vector.tensor_single_scalar(
                    c_mag3[:, :], c_mag2[:, :], -1.0, ALU.mult))
                L.chain(vector, vE, vector.tensor_single_scalar(
                    c_sgn[:, :], c_mod[:, src0:src0 + 8], 1e20, ALU.mult))
                L.chain(vector, vE, vector.tensor_tensor(
                    c_mag3[:, :], c_sgn[:, :], c_mag3[:, :], ALU.max))
                L.chain(vector, vE, vector.tensor_tensor(
                    dstg[:, :], c_mag3[:, :], c_mag2[:, :], ALU.min))
            vector.tensor_single_scalar(c_g1[:, :], c_mod[:, 8:16], 1.0, ALU.add)
            i = vector.tensor_single_scalar(c_g2[:, :], c_mod[:, 32:40], 1.0,
                                            ALU.add)
            L.inc(i, vE)
            L.set("v_mod", vE)

        if tap is not None and tap[0] == "gax":
            @blk.gpsimd
            def _(g):
                L.wait(g, vE, "v_mod")
                i = g.dma_start(out=dbg.rearrange("(a p) t -> a p t", a=1)[0][:, 0:8], in_=c_ga[:, :])
                L.inc(i, qG)
                i = g.dma_start(out=dbg.rearrange("(a p) t -> a p t", a=1)[0][:, 8:16], in_=c_gb[:, :])
                L.inc(i, qG)
                i = g.dma_start(out=dbg.rearrange("(a p) t -> a p t", a=1)[0][:, 16:24], in_=c_g1[:, :])
                L.inc(i, qG)
                i = g.dma_start(out=dbg.rearrange("(a p) t -> a p t", a=1)[0][:, 24:32], in_=c_g2[:, :])
                L.inc(i, qG)
                i = g.dma_start(out=dbg.rearrange("(a p) t -> a p t", a=1)[0][:, 32:40], in_=c_sgn[:, :])
                L.inc(i, qG)
                i = g.dma_start(out=dbg.rearrange("(a p) t -> a p t", a=1)[0][:, 40:48], in_=c_mag[:, 0:8])
                L.inc(i, qG)
                i = g.dma_start(out=dbg.rearrange("(a p) t -> a p t", a=1)[0][:, 48:56], in_=c_mag2[:, :])
                L.inc(i, qG)
                i = g.dma_start(out=dbg.rearrange("(a p) t -> a p t", a=1)[0][:, 56:64], in_=c_mod[:, 16:24])
                L.inc(i, qG)
                i = g.dma_start(out=dbg.rearrange("(a p) t -> a p t", a=1)[0][:, 64:72], in_=c_mag3[:, :])
                L.inc(i, qG)
                g.wait_ge(qG, L.at(qG))

        if tap is not None and tap[0] == "mod":
            @blk.gpsimd
            def _(g):
                L.wait(g, vE, "v_mod")
                i = g.dma_start(out=dbg, in_=c_mod[:, :])
                L.inc(i, qG)
                g.wait_ge(qG, L.at(qG))
        if stage < 2:
            _finish(nc, L, blk, qG, out, backing)
            return nc

        @blk.sync
        def _(sync):
            i = sync.dma_start(out=wpf[:, 0:IFF], in_=wdNd)
            L.inc(i, qS)
            L.set("p0_wd", qS)

        @blk.vector
        def _(vector):
            L.wait(vector, qS, "p0_wd")
            L.chain(vector, vE, vector.reduce_max(
                wsc[:, 0:1], wpf[:, 0:IFF], axis=AX,
                apply_absolute_value=True))
            i = vector.tensor_tensor(wsc[:, 1:2], wsc[:, 0:1], c_r127[:, 0:1],
                                     ALU.mult)
            L.inc(i, vE)
            L.set("v_wdsc", vE)

        @blk.scalar
        def _(scalar):
            L.wait(scalar, vE, "v_wdsc")
            L.chain(scalar, sE, scalar.activation(wsc[:, 3:4], wsc[:, 1:2],
                                                  AF.Ln))
            i = scalar.activation(wsc[:, 2:3], wsc[:, 3:4], AF.Exp, scale=-1.0)
            L.inc(i, sE)
            L.set("s_wdinv", sE)

        @blk.vector
        def _(vector):
            L.wait(vector, sE, "s_wdinv")
            vector.tensor_scalar(wps[:, 0:IFF], wpf[:, 0:IFF], wsc[:, 2:3], None,
                                 ALU.mult)
            vector.tensor_copy(wp8[:, 0:IFF], wps[:, 0:IFF])
            i = vector.tensor_copy(wpq[:, 0:IFF], wp8[:, 0:IFF])
            L.inc(i, vE)
            L.set("v_wdq", vE)

        @blk.gpsimd
        def _(g):
            L.wait(g, vE, "v_wdq")
            i = g.dma_start(out=agsc_i.rearrange("(a b) -> a b", b=1)[704:832],
                            in_=wsc[:, 1:2])
            L.inc(i, qG)
            L.set("g_sd", qG)

        for ic in range(NIC):
            @blk.tensor
            def _(tensor, ic=ic):
                if ic == 0:
                    L.wait(tensor, vE, "v_wdq")
                if ic >= 2:
                    L.wait(tensor, vE, f"v_wdT{ic - 2}")
                i = tensor.transpose(
                    ps[2 + ic % 2].ap().bitcast(BF16)[:, 0:128],
                    wpq[:, ic * 128:(ic + 1) * 128], c_idB[:, :])
                L.inc(i, tE)
                L.set(f"t_wdT{ic}", tE)

            @blk.vector
            def _(vector, ic=ic):
                L.wait(vector, tE, f"t_wdT{ic}")
                if ic >= 2:
                    L.wait(vector, qG, f"g_wdT{ic - 2}")
                i = vector.tensor_copy(
                    wstg[ic % 2][:, :],
                    ps[2 + ic % 2].ap().bitcast(BF16)[:, 0:128])
                L.inc(i, vE)
                L.set(f"v_wdT{ic}", vE)

            @blk.gpsimd
            def _(g, ic=ic):
                L.wait(g, vE, f"v_wdT{ic}")
                i = g.dma_start(out=agwd_i.ap()[ic * 128:(ic + 1) * 128, :],
                                in_=wstg[ic % 2][:, :])
                L.inc(i, qG)
                L.set(f"g_wdT{ic}", qG)
        L.set("g_wd_all", qG)

        for wname, wN, wT, agi, sc_off in (("g", wgNd, wgTd, agwg_i, 0),
                                           ("u", wuNd, wuTd, agwu_i, 352)):
            @blk.sync
            def _(sync, wN=wN, wname=wname):
                L.wait(sync, vE, "v_wdq" if wname == "g" else "v_wgq7")
                for k, rows in ((0, 128), (1, 128), (2, 96)):
                    i = sync.dma_start(out=wpf[0:rows, k * 1024:(k + 1) * 1024],
                                       in_=wN[k * 128:k * 128 + rows, :])
                    L.inc(i, qS)
                L.set(f"p_w{wname}n", qS)

            @blk.vector
            def _(vector, wname=wname):
                L.wait(vector, qS, f"p_w{wname}n")
                for k, rows in ((0, 128), (1, 128), (2, 96)):
                    L.chain(vector, vE, vector.reduce_max(
                        wsc[0:rows, k:k + 1],
                        wpf[0:rows, k * 1024:(k + 1) * 1024],
                        axis=AX, apply_absolute_value=True))
                i = vector.tensor_scalar(wsc[:, 4:7], wsc[:, 0:3],
                                         c_r127[:, 0:1], None, ALU.mult)
                L.inc(i, vE)
                L.set(f"v_w{wname}sc", vE)

            @blk.tensor
            def _(tensor, wname=wname):
                L.wait(tensor, vE, f"v_w{wname}sc")
                for k, rows in ((0, 128), (1, 128), (2, 96)):
                    i = tensor.transpose(ps[2].ap()[0:1, k * 128:k * 128 + rows],
                                         wsc[0:rows, 4 + k:5 + k],
                                         c_idF[0:rows, 0:rows])
                L.inc(i, tE)
                L.set(f"t_w{wname}r", tE)

            @blk.scalar
            def _(scalar, wname=wname):
                L.wait(scalar, tE, f"t_w{wname}r")
                i = scalar.activation(wrow[0:1, :], ps[2].ap()[0:1, 0:ICORE],
                                      AF.Identity)
                L.inc(i, sE)
                L.set(f"s_w{wname}row", sE)

            @blk.gpsimd
            def _(g, wname=wname, sc_off=sc_off):
                L.wait(g, sE, f"s_w{wname}row")
                i = g.dma_start(
                    out=agsc_i.rearrange("(a b) -> a b",
                                         a=1)[0:1, sc_off:sc_off + ICORE],
                    in_=wrow[0:1, :])
                L.inc(i, qG)
                L.set(f"g_w{wname}sc", qG)

            @blk.scalar
            def _(scalar, wname=wname):
                L.wait(scalar, sE, f"s_w{wname}row")
                L.wait(scalar, qG, f"g_w{wname}sc")
                L.chain(scalar, sE, scalar.activation(
                    wrep[0:1, 0:ICORE], wrow[0:1, :], AF.Ln))
                i = scalar.activation(winv[0:1, :], wrep[0:1, 0:ICORE],
                                      AF.Exp, scale=-1.0)
                L.inc(i, sE)
                L.set(f"s_w{wname}inv", sE)

            @blk.tensor
            def _(tensor, wname=wname):
                L.wait(tensor, sE, f"s_w{wname}inv")
                i = tensor.matmul(ps[3].ap()[:, 0:ICORE], c_1rf[0:1, :],
                                  winv[0:1, :], start=True, stop=True)
                L.inc(i, tE)
                L.set(f"t_w{wname}rep", tE)

            @blk.vector
            def _(vector, wname=wname):
                L.wait(vector, tE, f"t_w{wname}rep")
                i = vector.tensor_copy(wrep[:, :], ps[3].ap()[:, 0:ICORE])
                L.inc(i, vE)
                L.set(f"v_w{wname}rep", vE)

            for ec in range(NDC):
                @blk.sync
                def _(sync, wT=wT, ec=ec, wname=wname):
                    if ec == 0:
                        L.wait(sync, vE, f"v_w{wname}sc")
                    else:
                        L.wait(sync, vE, f"v_w{wname}q{ec - 1}")
                    i = sync.dma_start(out=wpf[:, 0:ICORE],
                                       in_=wT[ec * 128:(ec + 1) * 128, :])
                    L.inc(i, qS)
                    L.set(f"p_w{wname}t{ec}", qS)

                @blk.vector
                def _(vector, ec=ec, wname=wname):
                    L.wait(vector, qS, f"p_w{wname}t{ec}")
                    if ec == 0:
                        L.wait(vector, vE, f"v_w{wname}rep")
                    if ec >= 1:
                        L.wait(vector, qG, f"g_w{wname}q{ec - 1}")
                    L.chain(vector, vE, vector.tensor_tensor(
                        wps[:, 0:ICORE], wpf[:, 0:ICORE], wrep[:, :], ALU.mult))
                    L.chain(vector, vE, vector.tensor_copy(
                        wp8[:, 0:ICORE], wps[:, 0:ICORE]))
                    i = vector.tensor_copy(wpq[:, 0:ICORE], wp8[:, 0:ICORE])
                    L.inc(i, vE)
                    L.set(f"v_w{wname}q{ec}", vE)

                @blk.gpsimd
                def _(g, agi=agi, ec=ec, wname=wname):
                    L.wait(g, vE, f"v_w{wname}q{ec}")
                    i = g.dma_start(out=agi.ap()[ec * 128:(ec + 1) * 128, :],
                                    in_=wpq[:, 0:ICORE])
                    L.inc(i, qG)
                    L.set(f"g_w{wname}q{ec}", qG)
            L.mark[f"g_w{wname}_all"] = L.at(qG)

        @blk.gpsimd
        def _(g):
            g.wait_ge(qG, L.mark["g_wu_all"])
            g.wait_ge(qG, L.mark["g_wd_all"])
            g.wait_ge(qG, L.mark["g_sd"])
            for ins_, outs_, nm in ((agwg_i, agwg_o, "ag_wg"),
                                    (agwu_i, agwu_o, "ag_wu"),
                                    (agwd_i, agwd_o, "ag_wd"),
                                    (agsc_i, agsc_o, "ag_sc")):
                i = g.collective_compute("AllGather", ALU.bypass, replica_groups=RG,
                                         ins=[ins_.ap().opt()],
                                         outs=[outs_.ap().opt()])
                L.inc(i, cC)
                L.set(nm, cC)
        L.mark["p0b_v_done"] = L.at(vE)

        if tap is not None and tap[0] == "wgq":
            @blk.sync
            def _(sync):
                L.wait(sync, cC, "ag_wg")
                i = sync.dma_start(
                    out=sc22[:, :] if False else wpf[:, 0:IFF],
                    in_=agwg_o.rearrange("(a b) -> a b", b=IFF)[0:128])
                L.inc(i, qS)
                L.set("tap_wgq", qS)

            @blk.vector
            def _(vector):
                L.wait(vector, qS, "tap_wgq")
                i = vector.tensor_copy(wps[:, 0:IFF], wpf[:, 0:IFF].bitcast(BF16)[:, 0:IFF])
                L.inc(i, vE)
                L.set("tap_wgq_v", vE)

            @blk.gpsimd
            def _(g):
                L.wait(g, vE, "tap_wgq_v")
                i = g.dma_start(out=dbg, in_=wps[:, 0:IFF])
                L.inc(i, qG)
                g.wait_ge(qG, L.at(qG))
        if stage < 3:
            _finish(nc, L, blk, qG, out, backing)
            return nc

        def emit_ln(tag, src_dram, n_tok, h_out, h_stride, g_t, s_t, s_off,
                    first_wait=None):
            for tt in range(n_tok // 512):
                r = tt % 2

                @blk.sync
                def _(sync, tt=tt, r=r):
                    if tt == 0 and first_wait is not None:
                        sync.wait_ge(qG, L.mark[first_wait])
                    if tt >= 2:
                        L.wait(sync, vE, f"{tag}_ap{tt - 2}")
                    i = sync.dma_start(
                        out=xt[r].rearrange("p (c t) -> p c t", c=NDC),
                        in_=src_dram.rearrange(
                            "(c p) t -> p c t",
                            p=128)[:, :, tt * 512:(tt + 1) * 512])
                    L.inc(i, qS)
                    L.set(f"{tag}_x{tt}", qS)

                @blk.vector
                def _(vector, tt=tt, r=r):
                    L.wait(vector, qS, f"{tag}_x{tt}")
                    if tt >= 1:
                        L.wait(vector, tE, f"{tag}_st{tt - 1}")
                    for dc in range(NDC):
                        i = vector.tensor_copy(xb_t[:, dc * 512:(dc + 1) * 512],
                                               xt[r][:, dc * 512:(dc + 1) * 512])
                    L.inc(i, vE)
                    L.set(f"{tag}_xb{tt}", vE)

                @blk.scalar
                def _(scalar, tt=tt, r=r):
                    L.wait(scalar, qS, f"{tag}_x{tt}")
                    if tt >= 1:
                        L.wait(scalar, tE, f"{tag}_st{tt - 1}")
                    for dc in range(NDC):
                        i = scalar.activation(xq_t[:, dc * 512:(dc + 1) * 512],
                                              xt[r][:, dc * 512:(dc + 1) * 512],
                                              AF.Square)
                    L.inc(i, sE)
                    L.set(f"{tag}_xq{tt}", sE)

                @blk.tensor
                def _(tensor, tt=tt):
                    L.wait(tensor, vE, f"{tag}_xb{tt}")
                    L.wait(tensor, sE, f"{tag}_xq{tt}")
                    if tt >= 1:
                        L.wait(tensor, vE, f"{tag}_mrt{tt - 1}")
                        L.wait(tensor, sE, f"{tag}_mu{tt - 1}")
                    for dc in range(NDC):
                        tensor.matmul(ps[0].ap()[:, 0:512], c_ln1[:, :],
                                      xb_t[:, dc * 512:(dc + 1) * 512],
                                      start=(dc == 0), stop=(dc == NDC - 1))
                    for dc in range(NDC):
                        i = tensor.matmul(ps[0].ap()[:, 512:1024], c_ln1[:, :],
                                          xq_t[:, dc * 512:(dc + 1) * 512],
                                          start=(dc == 0), stop=(dc == NDC - 1))
                    L.inc(i, tE)
                    L.set(f"{tag}_st{tt}", tE)

                @blk.scalar
                def _(scalar, tt=tt):
                    L.wait(scalar, tE, f"{tag}_st{tt}")
                    scalar.activation(muS[:, :], ps[0].ap()[:, 0:512], AF.Identity)
                    i = scalar.activation(mu2[:, :], ps[0].ap()[:, 0:512], AF.Square)
                    L.inc(i, sE)
                    L.set(f"{tag}_mu{tt}", sE)

                @blk.vector
                def _(vector, tt=tt):
                    L.wait(vector, sE, f"{tag}_mu{tt}")
                    i = vector.tensor_sub(varS[:, :], ps[0].ap()[:, 512:1024],
                                          mu2[:, :])
                    L.inc(i, vE)
                    L.set(f"{tag}_var{tt}", vE)

                @blk.scalar
                def _(scalar, tt=tt):
                    L.wait(scalar, vE, f"{tag}_var{tt}")
                    scalar.activation(lnvS[:, :], varS[:, :], AF.Ln,
                                      bias=c_eps[:, 0:1])
                    i = scalar.activation(rstdS[:, :], lnvS[:, :], AF.Exp,
                                          scale=-0.5)
                    L.inc(i, sE)
                    L.set(f"{tag}_rstd{tt}", sE)

                @blk.vector
                def _(vector, tt=tt, r=r):
                    L.wait(vector, sE, f"{tag}_rstd{tt}")
                    if tag == "L1" and tt == 0:
                        L.wait(vector, vE, "v_mod")
                    i = vector.tensor_mul(mrtS[:, :], muS[:, :], rstdS[:, :])
                    L.inc(i, vE)
                    L.set(f"{tag}_mrt{tt}", vE)
                    for dc in range(NDC):
                        vector.tensor_mul(tmpA[:, :],
                                          xt[r][:, dc * 512:(dc + 1) * 512],
                                          rstdS[:, :])
                        vector.tensor_sub(tmpB[:, :], tmpA[:, :], mrtS[:, :])
                        i = vector.tensor_scalar(
                            h_out[:, dc * h_stride + tt * 512:
                                  dc * h_stride + tt * 512 + 512],
                            tmpB[:, :], g_t[:, dc:dc + 1],
                            s_t[:, s_off + dc:s_off + dc + 1],
                            ALU.mult, ALU.add)
                    L.inc(i, vE)
                    L.set(f"{tag}_ap{tt}", vE)
            L.mark[f"{tag}_done_v"] = L.at(vE)

        emit_ln("L1", xTd, FR, a_h1, FR, c_g1, c_mod, 0)

        if tap is not None and tap[0] == "h1":
            @blk.gpsimd
            def _(g):
                g.wait_ge(vE, L.mark["L1_done_v"])
                for dc in range(NDC):
                    i = g.dma_start(
                        out=dbg.rearrange("(c p) t -> c p t", p=128)[dc],
                        in_=a_h1[:, dc * FR:(dc + 1) * FR])
                    L.inc(i, qG)
                g.wait_ge(qG, L.at(qG))
        if stage < 4:
            _finish(nc, L, blk, qG, out, backing)
            return nc


        for ec in range(NDC):
            @blk.sync
            def _(sync, ec=ec):
                if ec == 0:
                    L.wait(sync, vE, "L1_done_v")
                if ec >= 1:
                    L.wait(sync, vE, f"v_wc{ec - 1}")
                i = sync.dma_start(
                    out=wtl[ec % 2][:, 0:DA] if False else wtl[0][:, 0:1024],
                    in_=wvTd.rearrange("(c p) d -> c p d", p=128)[ec][:, 0:1024])
                L.inc(i, qS)
                i = sync.dma_start(
                    out=wtl[1][:, 0:16],
                    in_=wvTd.rearrange("(c p) d -> c p d", p=128)[ec][:, 1024:1040])
                L.inc(i, qS)
                L.set(f"v_wl{ec}", qS)

            @blk.vector
            def _(vector, ec=ec):
                L.wait(vector, qS, f"v_wl{ec}")
                vector.tensor_copy(wvb[:, ec * DA:ec * DA + 1024], wtl[0][:, 0:1024])
                i = vector.tensor_copy(wvb[:, ec * DA + 1024:ec * DA + 1040],
                                       wtl[1][:, 0:16])
                L.inc(i, vE)
                L.set(f"v_wc{ec}", vE)

        for tc in range(NTC):
            @blk.tensor
            def _(tensor, tc=tc):
                if tc == 0:
                    L.wait(tensor, vE, f"v_wc{NDC - 1}")
                if tc >= 1:
                    L.wait(tensor, vE, f"v_evv{tc - 1}")
                for part, (n0, n1) in enumerate(((0, 512), (512, 1024),
                                                 (1024, 1040))):
                    pv = ps[2].ap()[:, n0:n1] if part < 2 else ps[3].ap()[:, 0:16]
                    for ec in range(NDC):
                        tensor.matmul(
                            pv,
                            a_h1[:, ec * FR + tc * 128:ec * FR + tc * 128 + 128],
                            wvb[:, ec * DA + n0:ec * DA + n1],
                            start=(ec == 0), stop=False)
                    i = tensor.matmul(pv, c_1rb[0:1, :],
                                      c_bva[0:1, n0:n1], start=False, stop=True)
                L.inc(i, tE)
                L.set(f"v_mmv{tc}", tE)

            @blk.vector
            def _(vector, tc=tc):
                L.wait(vector, tE, f"v_mmv{tc}")
                vector.tensor_copy(a_v[:, tc * DA:tc * DA + 1024],
                                   ps[2].ap()[:, 0:1024])
                i = vector.tensor_copy(a_v[:, tc * DA + 1024:tc * DA + 1040],
                                       ps[3].ap()[:, 0:16])
                L.inc(i, vE)
                L.set(f"v_evv{tc}", vE)
        L.mark["v_done"] = L.at(vE)

        for nm, wdram, bias_t, ntt, toff in (("q", wqTd, c_bq, 4, 512),
                                             ("k", wkTd, c_bk, 6, 0)):
            for dc in range(NDC):
                r = dc % 2

                @blk.sync
                def _(sync, dc=dc, r=r, nm=nm, wdram=wdram):
                    if nm == "q" and dc == 0:
                        L.wait(sync, vE, f"v_wc{NDC - 1}")
                    if nm == "k" and dc < 2:
                        L.wait(sync, vE, "q_wc7")
                    if dc >= 2:
                        L.wait(sync, vE, f"{nm}_wc{dc - 2}")
                    for ec in range(NDC):
                        i = sync.dma_start(
                            out=wtl[r][:, ec * 128:ec * 128 + 128],
                            in_=wdram.rearrange("(c p) d -> c p d",
                                                p=128)[ec][:, dc * 128:dc * 128 + 128])
                        L.inc(i, qS)
                    L.set(f"{nm}_wl{dc}", qS)

                @blk.vector
                def _(vector, dc=dc, r=r, nm=nm, ntt=ntt):
                    L.wait(vector, qS, f"{nm}_wl{dc}")
                    if nm == "k" and dc < 2:
                        L.wait(vector, tE, "q_mm31")
                    if dc >= 2:
                        L.wait(vector, tE, f"{nm}_mm{(dc - 2) * ntt + ntt - 1}")
                    i = vector.tensor_copy(wqbs[r][:, :], wtl[r][:, :])
                    L.inc(i, vE)
                    L.set(f"{nm}_wc{dc}", vE)

                def emit_pmm(tensor, dc, tt, idx, r, nm, ntt, toff):
                    if tt == 0:
                        L.wait(tensor, vE, f"{nm}_wc{dc}")
                    pr = dc * ntt + tt - 2
                    if pr >= 0:
                        L.wait(tensor, sE, f"{nm}_ev{pr}")
                    if nm == "k" and dc == 0 and tt == 0:
                        L.wait(tensor, vE, "v_done")
                    for ec in range(NDC):
                        i = tensor.matmul(
                            ps[idx].ap()[:, 0:512],
                            wqbs[r][:, ec * 128:ec * 128 + 128],
                            a_h1[:, ec * FR + toff + tt * 512:
                                 ec * FR + toff + tt * 512 + 512],
                            start=(ec == 0), stop=(ec == NDC - 1))
                    L.inc(i, tE)
                    L.set(f"{nm}_mm{dc * ntt + tt}", tE)

                for th in range(ntt // 2):
                    tts = (2 * th, 2 * th + 1)
                    for tt in tts:
                        @blk.tensor
                        def _(tensor, dc=dc, tt=tt, r=r, nm=nm, ntt=ntt,
                              toff=toff):
                            emit_pmm(tensor, dc, tt, (dc * ntt + tt) % 2,
                                     r, nm, ntt, toff)

                    if nm == "q":
                        @blk.scalar
                        def _(scalar, dc=dc, tts=tts, ntt=ntt, bias_t=bias_t):
                            for tt in tts:
                                it = dc * ntt + tt
                                L.wait(scalar, tE, f"q_mm{it}")
                                if it >= 2:
                                    L.wait(scalar, qG, f"q_sp{it - 2}")
                                i = scalar.activation(psth[it % 2][:, :],
                                                      ps[it % 2].ap()[:, 0:512],
                                                      AF.Identity,
                                                      bias=bias_t[:, dc:dc + 1])
                                L.inc(i, sE)
                                L.set(f"q_ev{it}", sE)

                        for tt in tts:
                            @blk.gpsimd
                            def _(g, dc=dc, tt=tt, ntt=ntt):
                                it = dc * ntt + tt
                                L.wait(g, sE, f"q_ev{it}")
                                i = g.dma_start(
                                    out=qhd.rearrange("(c p) t -> c p t",
                                                      p=128)[dc][
                                        :, tt * 512:tt * 512 + 512],
                                    in_=psth[it % 2][:, :])
                                L.inc(i, qG)
                                L.set(f"q_sp{it}", qG)
                    else:
                        @blk.scalar
                        def _(scalar, dc=dc, tts=tts, ntt=ntt, bias_t=bias_t):
                            for tt in tts:
                                it = dc * ntt + tt
                                L.wait(scalar, tE, f"k_mm{it}")
                                if it >= 2:
                                    L.wait(scalar, vE, f"k_lo{it - 2}")
                                L.chain(scalar, sE, scalar.activation(
                                    pstf[it % 2][:, :],
                                    ps[it % 2].ap()[:, 0:512],
                                    AF.Identity, bias=bias_t[:, dc:dc + 1]))
                                i = scalar.activation(psth[it % 2][:, :],
                                                      pstf[it % 2][:, :],
                                                      AF.Identity)
                                L.inc(i, sE)
                                L.set(f"k_ev{it}", sE)

                        for tt in tts:
                            @blk.vector
                            def _(vector, dc=dc, tt=tt, ntt=ntt):
                                it = dc * ntt + tt
                                L.wait(vector, sE, f"k_ev{it}")
                                if it >= 2:
                                    L.wait(vector, qG, f"k_sp{it - 2}")
                                i = vector.tensor_sub(pstl[it % 2][:, :],
                                                      pstf[it % 2][:, :],
                                                      psth[it % 2][:, :])
                                L.inc(i, vE)
                                L.set(f"k_lo{it}", vE)

                            @blk.gpsimd
                            def _(g, dc=dc, tt=tt, ntt=ntt):
                                it = dc * ntt + tt
                                L.wait(g, vE, f"k_lo{it}")
                                i = g.dma_start(
                                    out=khd.rearrange("(c p) t -> c p t",
                                                      p=128)[dc][
                                        :, tt * 512:tt * 512 + 512],
                                    in_=psth[it % 2][:, :])
                                L.inc(i, qG)
                                i = g.dma_start(
                                    out=kld.rearrange("(c p) t -> c p t",
                                                      p=128)[dc][
                                        :, tt * 512:tt * 512 + 512],
                                    in_=pstl[it % 2][:, :])
                                L.inc(i, qG)
                                L.set(f"k_sp{it}", qG)
            L.mark[f"{nm}_done"] = L.at(qG)

        if tap is not None and tap[0] == "qlo":
            @blk.gpsimd
            def _(g):
                g.wait_ge(qG, L.mark["q_done"])
                i = g.dma_start(out=dbg.rearrange("(a p) t -> a p t", a=2)[0],
                                in_=qhd.rearrange("(c p) t -> c p t", p=128)[0])
                L.inc(i, qG)
                i = g.dma_start(out=dbg.rearrange("(a p) t -> a p t", a=2)[1],
                                in_=qld.rearrange("(c p) t -> c p t", p=128)[0])
                L.inc(i, qG)
                g.wait_ge(qG, L.at(qG))

        @blk.gpsimd
        def _(g):
            g.wait_ge(qG, L.mark["k_done"])
            rk = g.alloc_register("rkp")
            g.reg_load(rk, c_pads[0:1, 0:1])
            rkv = er.make_scalar_value(rk, min_val=0, max_val=2560,
                                       guaranteed_mod_val=512)
            for dc in range(NDC):
                i = g.dma_start(
                    out=khd.rearrange("(c p) t -> c p t", p=128)[dc][
                        :, bass.ds(rkv, 512)],
                    in_=c_km1[:, :])
                L.inc(i, qG)
                i = g.dma_start(
                    out=kld.rearrange("(c p) t -> c p t", p=128)[dc][
                        :, bass.ds(rkv, 512)],
                    in_=c_k0[:, :])
                L.inc(i, qG)
            L.set("kpad_done", qG)

        @blk.vector
        def _(vector):
            vector.wait_ge(vE, L.mark["v_done"])
            rv = vector.alloc_register("rv")
            vector.reg_load(rv, c_pads[0:1, 1:2])
            rvv = er.make_scalar_value(rv, min_val=0, max_val=20800)
            for k in range(4):
                i = vector.tensor_copy(a_v[:, bass.ds(rvv + k * DA, DA)],
                                       c_padv[:, :])
            L.inc(i, vE)
            L.set("pads_done", vE)

        if tap is not None and tap[0] == "av":
            @blk.gpsimd
            def _(g):
                L.wait(g, vE, "pads_done")
                i = g.dma_start(out=dbg, in_=a_v[:, :])
                L.inc(i, qG)
                g.wait_ge(qG, L.at(qG))
        if stage < 5:
            _finish(nc, L, blk, qG, out, backing)
            return nc


        def sspan(slot, jc):
            if slot == 0:
                return (0, 512) if jc <= 7 else (128 * (jc - 8), 512)
            if slot == 3:
                return (0, 128 * (jc + 1)) if jc <= 3 else (0, 512)
            if jc <= 3:
                return (0, 128 * (jc + 1))
            if jc <= 7:
                return (0, 512)
            return (128 * (jc - 8), 512)

        def vjcs(slot, ib):
            if slot == 0:
                return list(range(0, ib + 9))
            if slot == 3:
                return list(range(ib, 12))
            return list(range(ib, ib + 9))

        def esrc(slot, jc, ib, Eb):
            if slot == 0:
                if jc <= 3:
                    return aEm[:, 512 * jc + 128 * ib:512 * jc + 128 * ib + 128]
                if jc == ib + 8:
                    return aEm[:, 2560 + 128 * ib:2560 + 128 * ib + 128]
            elif slot == 3:
                if jc >= 8:
                    return aEm[:, 512 * (jc - 8) + 128 * ib:
                               512 * (jc - 8) + 128 * ib + 128]
                if jc == ib:
                    return aEm[:, 2048 + 128 * ib:2048 + 128 * ib + 128]
            else:
                if jc == ib:
                    return aEm[:, 2048 + 128 * ib:2048 + 128 * ib + 128]
                if jc == ib + 8:
                    return aEm[:, 2560 + 128 * ib:2560 + 128 * ib + 128]
            return Eb[:, 512 * jc + 128 * ib:512 * jc + 128 * ib + 128]

        @blk.sync
        def _(sync):
            L.wait(sync, vE, "pads_done")
            i = sync.dma_start(out=c_mm[:, :], in_=m0d)
            L.inc(i, qS)
            L.set("mm0_ready", qS)

        def emit_pair_load(p):
            w, h = p // 16, p % 16
            r = p % 2

            @blk.sync
            def _(sync, p=p, w=w, h=h, r=r):
                if p == 0:
                    sync.wait_ge(qG, L.mark["q_done"])
                    sync.wait_ge(qG, L.mark["kpad_done"])
                if p >= 2:
                    L.wait(sync, tE, f"sc{p - 2}_5")
                i = sync.dma_start(
                    out=kt2[r][0:64, 0:1536],
                    in_=khd[64 * h:64 * h + 64, 512 * w:512 * w + 1536])
                L.inc(i, qS)
                i = sync.dma_start(
                    out=kt2[r][64:128, 0:1536],
                    in_=kld[64 * h:64 * h + 64, 512 * w:512 * w + 1536])
                L.inc(i, qS)
                i = sync.dma_start(
                    out=qt2[r][0:64, 0:512],
                    in_=qhd[64 * h:64 * h + 64, 512 * w:512 * w + 512])
                L.inc(i, qS)
                i = sync.dma_start(
                    out=qt2[r][64:128, 0:512],
                    in_=qhd[64 * h:64 * h + 64, 512 * w:512 * w + 512])
                L.inc(i, qS)
                L.set(f"ptl{p}", qS)



        emit_pair_load(0)
        emit_pair_load(1)
        expb = {}

        for p in range(64):
            w, h = p // 16, p % 16
            slot = w
            dc = h // 2
            r0 = 64 * (h % 2)
            Eb = aE[p % 2]
            qc0 = dc * TOWN + 512 * w

            def emit_sc(tensor, p, slot, g):
                r = p % 2
                if g == 0:
                    L.wait(tensor, qS, f"ptl{p}")
                if p == 0 and g == 0:
                    L.wait(tensor, vE, "pads_done")
                for jc in (2 * g, 2 * g + 1):
                    n0, n1 = sspan(slot, jc)
                    po = ps[g % 3].ap()[:, (jc % 2) * 512 + n0:
                                        (jc % 2) * 512 + n1]
                    i = tensor.matmul(po,
                                      kt2[r][:, 128 * jc:128 * jc + 128],
                                      qt2[r][:, n0:n1],
                                      start=True, stop=True)
                L.inc(i, tE)
                L.set(f"sc{p}_{g}", tE)

            expb[p] = L.at(sE)

            for g in range(6):
                @blk.tensor
                def _(tensor, p=p, slot=slot, g=g):
                    if g >= 3:
                        tensor.wait_ge(sE, expb[p] + g - 2)
                    elif p >= 1:
                        tensor.wait_ge(sE, expb[p - 1] + g + 4)
                    emit_sc(tensor, p, slot, g)

            @blk.scalar
            def _(scalar, p=p, Eb=Eb, slot=slot):
                for g in range(6):
                    L.wait(scalar, tE, f"sc{p}_{g}")
                    if p >= 2 and g == 0:
                        L.wait(scalar, tE, f"vmm{p - 2}")
                        L.wait(scalar, vE, f"msk{p - 2}")
                    e0, _ = sspan(slot, 2 * g)
                    _, e1 = sspan(slot, 2 * g + 1)
                    i = scalar.activation(
                        Eb[:, 1024 * g + e0:1024 * g + 512 + e1],
                        ps[g % 3].ap()[:, e0:512 + e1], AF.Exp)
                    L.inc(i, sE)
                    L.set(f"exp{p}_{g}", sE)
            assert L.at(sE) == expb[p] + 6

            if p + 2 < 64:
                emit_pair_load(p + 2)

            @blk.vector
            def _(vector, p=p, slot=slot, Eb=Eb):
                L.wait(vector, sE, f"exp{p}_5")
                if p >= 1:
                    L.wait(vector, tE, f"vmm{p - 1}")
                if slot == 0:
                    if p == 0:
                        L.wait(vector, qS, "mm0_ready")
                    vector.tensor_tensor(aEm[:, 0:2048], Eb[:, 0:2048],
                                         c_mm[:, :], ALU.mult)
                    for ib in range(4):
                        i = vector.tensor_tensor(
                            aEm[:, 2560 + 128 * ib:2688 + 128 * ib],
                            Eb[:, 4096 + 640 * ib:4224 + 640 * ib],
                            c_t2[:, :], ALU.mult)
                elif slot == 3:
                    if p == 48:
                        L.wait(vector, qS, "mm3_ready")
                    vector.tensor_tensor(aEm[:, 0:2048], Eb[:, 4096:6144],
                                         c_mm[:, :], ALU.mult)
                    for ib in range(4):
                        i = vector.tensor_tensor(
                            aEm[:, 2048 + 128 * ib:2176 + 128 * ib],
                            Eb[:, 640 * ib:640 * ib + 128],
                            c_t1[:, :], ALU.mult)
                else:
                    for ib in range(4):
                        vector.tensor_tensor(
                            aEm[:, 2048 + 128 * ib:2176 + 128 * ib],
                            Eb[:, 640 * ib:640 * ib + 128],
                            c_t1[:, :], ALU.mult)
                        i = vector.tensor_tensor(
                            aEm[:, 2560 + 128 * ib:2688 + 128 * ib],
                            Eb[:, 4096 + 640 * ib:4224 + 640 * ib],
                            c_t2[:, :], ALU.mult)
                L.inc(i, vE)
                L.set(f"msk{p}", vE)

            if tap is not None and tap[0] == f"e{p}":
                @blk.gpsimd
                def _(g, p=p, Eb=Eb):
                    L.wait(g, vE, f"msk{p}")
                    i = g.dma_start(out=dbg.rearrange("(a p) t -> a p t", a=1)[0][:, 0:6144],
                                    in_=Eb[:, :])
                    L.inc(i, qG)
                    i = g.dma_start(out=dbg.rearrange("(a p) t -> a p t", a=1)[0][:, 6144:9216],
                                    in_=aEm[:, :])
                    L.inc(i, qG)
                    g.wait_ge(qG, L.at(qG))

            if p == 15:
                @blk.sync
                def _(sync):
                    L.wait(sync, vE, "msk15")
                    i = sync.dma_start(out=c_mm[:, :], in_=m3d)
                    L.inc(i, qS)
                    L.set("mm3_ready", qS)

            @blk.tensor
            def _(tensor, p=p, w=w, h=h, slot=slot, Eb=Eb):
                L.wait(tensor, vE, f"msk{p}")
                if p >= 1:
                    L.wait(tensor, vE, f"evac{p - 1}")
                for ib in range(4):
                    pv = ps[3].ap()[:, 128 * ib:128 * ib + 65]
                    jcs = vjcs(slot, ib)
                    for ji, jc in enumerate(jcs):
                        i = tensor.matmul(
                            pv, esrc(slot, jc, ib, Eb),
                            a_v[:, (4 * w + jc) * DA + 65 * h:
                                (4 * w + jc) * DA + 65 * h + 65],
                            start=(ji == 0), stop=(ji == len(jcs) - 1))
                L.inc(i, tE)
                L.set(f"vmm{p}", tE)

            @blk.vector
            def _(vector, p=p):
                L.wait(vector, tE, f"vmm{p}")
                for ib in range(4):
                    L.chain(vector, vE, vector.tensor_copy(
                        c_zcp[:, ib:ib + 1],
                        ps[3].ap()[:, 128 * ib + 64:128 * ib + 65]))
                L.chain(vector, vE, vector.reciprocal(c_zr4[:, :],
                                                      c_zcp[:, :]))
                L.set(f"zinv{p}", vE)

            @blk.vector
            def _(vector, p=p, w=w, h=h):
                L.wait(vector, vE, f"zinv{p}")
                if p >= 1:
                    L.wait(vector, qG, f"gatm{p - 1}")
                for ib in range(4):
                    i = vector.tensor_scalar(
                        tstg[ib][:, 0:64],
                        ps[3].ap()[:, 128 * ib:128 * ib + 64],
                        c_zr4[:, ib:ib + 1], None, ALU.mult)
                L.inc(i, vE)
                L.set(f"evac{p}", vE)

            if tap is not None and tap[0].startswith("z") and tap[0][1:].isdigit() and p == int(tap[0][1:]):
                @blk.vector
                def _(vector, p=p):
                    L.wait(vector, tE, f"vmm{p}")
                    for ib in range(4):
                        i = vector.tensor_copy(
                            evt[:, 65 * ib:65 * ib + 65],
                            ps[2 + ib // 2].ap()[:, 512 * (ib % 2):
                                                 512 * (ib % 2) + 65])
                    L.inc(i, vE)
                    L.set("z0v", vE)

                @blk.gpsimd
                def _(g):
                    L.wait(g, vE, "z0v")
                    i = g.dma_start(
                        out=dbg.rearrange("(a p) t -> a p t", a=1)[0][:, 0:260],
                        in_=evt[:, 0:260])
                    L.inc(i, qG)
                    g.wait_ge(qG, L.at(qG))

            if tap is not None and tap[0] == f"zz{p}":
                @blk.gpsimd
                def _(g, p=p):
                    L.wait(g, vE, f"evac{p}")
                    i = g.dma_start(out=dbg.rearrange("(a p) t -> a p t", a=1)[0][:, 0:4], in_=c_zcp[:, :])
                    L.inc(i, qG)
                    i = g.dma_start(out=dbg.rearrange("(a p) t -> a p t", a=1)[0][:, 4:8], in_=c_zr4[:, :])
                    L.inc(i, qG)
                    for ib in range(4):
                        i = g.dma_start(out=dbg.rearrange("(a p) t -> a p t", a=1)[0][:, 8 + 64 * ib:72 + 64 * ib], in_=tstg[ib][:, 0:64])
                        L.inc(i, qG)
                    g.wait_ge(qG, L.at(qG))

            @blk.gpsimd
            def _(g, p=p, w=w, h=h):
                L.wait(g, vE, f"evac{p}")
                for ib in range(4):
                    i = g.dma_start(
                        out=atm_d.ap()[4 * w + ib][:, 64 * h:64 * h + 64],
                        in_=tstg[ib][:, 0:64])
                    L.inc(i, qG)
                L.set(f"gatm{p}", qG)

        if tap is not None and tap[0] == "av2":
            @blk.gpsimd
            def _(g):
                L.wait(g, qG, "gatm63")
                i = g.dma_start(out=dbg, in_=a_v[:, :])
                L.inc(i, qG)
                g.wait_ge(qG, L.at(qG))

        if tap is not None and tap[0] == "atm":
            @blk.gpsimd
            def _(g):
                L.wait(g, qG, "gatm63")
                for tc in range(16):
                    i = g.dma_start(
                        out=dbg.rearrange("(tc p) d -> tc p d", p=128)[tc],
                        in_=atm_d.ap()[tc])
                    L.inc(i, qG)
                g.wait_ge(qG, L.at(qG))

        for tc in range(16):
            @blk.sync
            def _(sync, tc=tc):
                if tc == 0:
                    L.wait(sync, qG, "gatm63")
                if tc >= 2:
                    L.wait(sync, tE, f"ttpl{tc - 2}")
                i = sync.dma_start(out=tld[tc % 2][:, :],
                                   in_=atm_d.ap()[tc])
                L.inc(i, qS)
                L.set(f"tldm{tc}", qS)

            for dc2 in range(NDC):
                it = tc * NDC + dc2
                r = it % 2

                @blk.tensor
                def _(tensor, tc=tc, dc2=dc2, it=it, r=r):
                    if dc2 == 0:
                        L.wait(tensor, qS, f"tldm{tc}")
                    if it >= 2:
                        L.wait(tensor, vE, f"tcp{it - 2}")
                    i = tensor.transpose(
                        ps[2 + r].ap().bitcast(BF16)[:, 0:128],
                        tld[tc % 2][:, 128 * dc2:128 * dc2 + 128],
                        c_idB[:, :])
                    L.inc(i, tE)
                    L.set(f"ttp{it}", tE)
                    if dc2 == NDC - 1:
                        L.mark[f"ttpl{tc}"] = L.at(tE)

                @blk.vector
                def _(vector, tc=tc, dc2=dc2, it=it, r=r):
                    L.wait(vector, tE, f"ttp{it}")
                    i = vector.tensor_copy(
                        a_afm[:, dc2 * TOWN + 128 * tc:
                              dc2 * TOWN + 128 * tc + 128],
                        ps[2 + r].ap().bitcast(BF16)[:, 0:128])
                    L.inc(i, vE)
                    L.set(f"tcp{it}", vE)
        L.mark["afm_done"] = L.at(vE)

        if tap is not None and tap[0] == "afm":
            @blk.gpsimd
            def _(g):
                g.wait_ge(vE, L.mark["afm_done"])
                for dc2 in range(NDC):
                    i = g.dma_start(
                        out=dbg.rearrange("(c p) t -> c p t", p=128)[dc2],
                        in_=a_afm[:, dc2 * TOWN:(dc2 + 1) * TOWN])
                    L.inc(i, qG)
                g.wait_ge(qG, L.at(qG))
        if stage < 6:
            _finish(nc, L, blk, qG, out, backing)
            return nc

        for ec in range(NDC):
            @blk.sync
            def _(sync, ec=ec):
                if ec == 0:
                    L.wait(sync, vE, "afm_done")
                if ec >= 2:
                    L.wait(sync, vE, f"wo_wc{ec - 2}")
                i = sync.dma_start(
                    out=wtl[ec % 2][:, :],
                    in_=woTd.rearrange("(c p) d -> c p d", p=128)[ec])
                L.inc(i, qS)
                L.set(f"wo_wl{ec}", qS)

            @blk.vector
            def _(vector, ec=ec):
                L.wait(vector, qS, f"wo_wl{ec}")
                i = vector.tensor_copy(wob[:, ec * 1024:(ec + 1) * 1024],
                                       wtl[ec % 2][:, :])
                L.inc(i, vE)
                L.set(f"wo_wc{ec}", vE)

        for ec in range(NDC):
            for th in range(2):
                tts = (2 * th, 2 * th + 1)
                for tt in tts:
                    @blk.tensor
                    def _(tensor, ec=ec, tt=tt):
                        it = ec * 4 + tt
                        if it == 0:
                            L.wait(tensor, vE, f"wo_wc{NDC - 1}")
                        if it >= 2:
                            L.wait(tensor, sE, f"oev{it - 2}")
                        for dc2 in range(NDC):
                            i = tensor.matmul(
                                ps[it % 2].ap()[:, 0:512],
                                wob[:, dc2 * 1024 + 128 * ec:
                                    dc2 * 1024 + 128 * ec + 128],
                                a_afm[:, dc2 * TOWN + 512 * tt:
                                      dc2 * TOWN + 512 * tt + 512],
                                start=(dc2 == 0), stop=(dc2 == NDC - 1))
                        L.inc(i, tE)
                        L.set(f"omm{it}", tE)

                @blk.scalar
                def _(scalar, ec=ec, tts=tts):
                    for tt in tts:
                        it = ec * 4 + tt
                        L.wait(scalar, tE, f"omm{it}")
                        if it >= 2:
                            L.wait(scalar, vE, f"ost{it - 2}")
                        dst = evt[:, :] if it % 2 == 0 else silq2[:, :]
                        i = scalar.activation(dst, ps[it % 2].ap()[:, 0:512],
                                              AF.Identity,
                                              bias=c_bo[:, ec:ec + 1])
                        L.inc(i, sE)
                        L.set(f"oev{it}", sE)

                for tt in tts:
                    @blk.sync
                    def _(sync, ec=ec, tt=tt):
                        it = ec * 4 + tt
                        if it >= 2:
                            L.wait(sync, vE, f"ost{it - 2}")
                        i = sync.dma_start(
                            out=xno[it % 2][:, :],
                            in_=xTd.rearrange("(c p) t -> c p t", p=128)[ec][
                                :, 512 + 512 * tt:1024 + 512 * tt])
                        L.inc(i, qS)
                        L.set(f"oxl{it}", qS)

                    @blk.vector
                    def _(vector, ec=ec, tt=tt):
                        it = ec * 4 + tt
                        L.wait(vector, sE, f"oev{it}")
                        L.wait(vector, qS, f"oxl{it}")
                        if it >= 2:
                            L.wait(vector, qG, f"oxs{it - 2}")
                        src = evt[:, :] if it % 2 == 0 else silq2[:, :]
                        i = vector.scalar_tensor_tensor(
                            outt[it % 2][:, :], src, c_ga[:, ec:ec + 1],
                            xno[it % 2][:, :], ALU.mult, ALU.add)
                        L.inc(i, vE)
                        L.set(f"ost{it}", vE)

                    @blk.gpsimd
                    def _(g, ec=ec, tt=tt):
                        it = ec * 4 + tt
                        L.wait(g, vE, f"ost{it}")
                        i = g.dma_start(
                            out=xnew_d.rearrange("(c p) t -> c p t", p=128)[ec][
                                :, 512 * tt:512 * tt + 512],
                            in_=outt[it % 2][:, :])
                        L.inc(i, qG)
                        L.set(f"oxs{it}", qG)
        L.mark["xnew_done"] = L.at(qG)

        if tap is not None and tap[0] == "xnew":
            @blk.gpsimd
            def _(g):
                g.wait_ge(qG, L.mark["xnew_done"])
                for dc in range(NDC):
                    i = g.dma_start(
                        out=dbg.rearrange("(c p) t -> c p t", p=128)[dc],
                        in_=xnew_d.rearrange("(c p) t -> c p t", p=128)[dc])
                    L.inc(i, qG)
                g.wait_ge(qG, L.at(qG))
        if stage < 7:
            _finish(nc, L, blk, qG, out, backing)
            return nc

        emit_ln("L2", xnew_d, TOWN, a_h2, TOWN, c_g2, c_mod, 24,
                first_wait="xnew_done")

        @blk.vector
        def _(vector):
            vector.wait_ge(vE, L.mark["L2_done_v"])
            L.chain(vector, vE, vector.memset(c_mx[:, :], 0.0))
            for dc in range(NDC):
                L.chain(vector, vE, vector.reduce_max(
                    c_mxt[:, :], a_h2[:, dc * TOWN:(dc + 1) * TOWN], axis=AX,
                    apply_absolute_value=True))
                L.chain(vector, vE, vector.tensor_max(c_mx[:, :], c_mxt[:, :],
                                                      c_mx[:, :]))
            L.set("m1_local", vE)

        @blk.gpsimd
        def _(g):
            L.wait(g, vE, "m1_local")
            i = g.dma_start(out=agm1_i.rearrange("(p x) -> p x", x=1),
                            in_=c_mx[:, :])
            L.inc(i, qG)
            L.set("m1_sent", qG)
            g.wait_ge(qG, L.at(qG))
            i = g.collective_compute("AllGather", ALU.bypass, replica_groups=RG,
                                     ins=[agm1_i.ap().opt()],
                                     outs=[agm1_o.ap().opt()])
            L.inc(i, cC)
            L.set("ag_m1", cC)

        @blk.sync
        def _(sync):
            L.wait(sync, cC, "ag_m1")
            i = sync.dma_start(out=c_mrow[0:1, :],
                               in_=agm1_o.rearrange("(a t) -> a t", a=1))
            L.inc(i, qS)
            L.set("m1_row", qS)

        @blk.vector
        def _(vector):
            L.wait(vector, qS, "m1_row")
            L.chain(vector, vE, vector.reduce_max(c_sc[0:1, 0:1], c_mrow[0:1, :],
                                                  axis=AX))
            L.set("m1_gmax", vE)

        @blk.scalar
        def _(scalar):
            L.wait(scalar, qS, "p0_small")
            L.chain(scalar, sE, scalar.activation(c_sc[0:1, 6:7],
                                                  c_clip[0:1, 0:1], AF.Exp,
                                                  scale=-1.0))
            L.chain(scalar, sE, scalar.activation(c_sc[0:1, 14:15],
                                                  c_clip[0:1, 1:2], AF.Exp,
                                                  scale=-1.0))
            L.set("sigE", sE)

        @blk.vector
        def _(vector):
            L.wait(vector, sE, "sigE")
            L.chain(vector, vE, vector.tensor_single_scalar(
                c_sc[0:1, 7:8], c_sc[0:1, 6:7], 1.0, ALU.add))
            L.chain(vector, vE, vector.reciprocal(c_sc[0:1, 1:2],
                                                  c_sc[0:1, 7:8]))
            L.chain(vector, vE, vector.tensor_single_scalar(
                c_sc[0:1, 15:16], c_sc[0:1, 14:15], 1.0, ALU.add))
            L.chain(vector, vE, vector.reciprocal(c_sc[0:1, 9:10],
                                                  c_sc[0:1, 15:16]))
            L.set("sig_done", vE)

        @blk.vector
        def _(vector):
            L.wait(vector, vE, "m1_gmax")
            L.wait(vector, vE, "sig_done")
            L.chain(vector, vE, vector.tensor_mul(c_sc[0:1, 2:3], c_sc[0:1, 0:1],
                                                  c_sc[0:1, 1:2]))
            L.chain(vector, vE, vector.tensor_mul(c_sc[0:1, 3:4], c_sc[0:1, 2:3],
                                                  c_r127[0:1, 0:1]))
            L.set("s1_done", vE)

        @blk.scalar
        def _(scalar):
            L.wait(scalar, vE, "s1_done")
            L.chain(scalar, sE, scalar.activation(c_sc[0:1, 4:5], c_sc[0:1, 3:4],
                                                  AF.Ln))
            L.chain(scalar, sE, scalar.activation(c_sc[0:1, 5:6], c_sc[0:1, 4:5],
                                                  AF.Exp, scale=-1.0))
            L.set("inv1_done", sE)

        @blk.tensor
        def _(tensor):
            L.wait(tensor, sE, "inv1_done")
            tensor.matmul(ps[0].ap()[:, 0:1], c_1rf[0:1, :], c_sc[0:1, 5:6],
                          start=True, stop=True)
            i = tensor.matmul(ps[0].ap()[:, 1:2], c_1rf[0:1, :], c_sc[0:1, 3:4],
                              start=True, stop=True)
            L.inc(i, tE)
            L.set("rep1", tE)

        @blk.vector
        def _(vector):
            L.wait(vector, tE, "rep1")
            L.chain(vector, vE, vector.tensor_copy(c_inv1[:, :],
                                                   ps[0].ap()[:, 0:1]))
            L.chain(vector, vE, vector.tensor_copy(c_s1r[:, :],
                                                   ps[0].ap()[:, 1:2]))
            L.set("rep1v", vE)

        for dc in range(NDC):
            @blk.vector
            def _(vector, dc=dc):
                if dc == 0:
                    L.wait(vector, vE, "rep1v")
                L.chain(vector, vE, vector.tensor_scalar(
                    vtmp[:, :], a_h2[:, dc * TOWN:(dc + 1) * TOWN],
                    c_inv1[:, 0:1], None, ALU.mult))
                L.chain(vector, vE, vector.tensor_copy(q8t[:, :], vtmp[:, :]))
                i = vector.tensor_copy(a_cx[:, dc * TOWN:(dc + 1) * TOWN],
                                       q8t[:, :])
                L.inc(i, vE)
                L.set(f"cx{dc}", vE)
        L.mark["cx_done"] = L.at(vE)

        if tap is not None and tap[0] == "cx":
            @blk.gpsimd
            def _(g):
                g.wait_ge(vE, L.mark["cx_done"])
                for dc in range(NDC):
                    i = g.dma_start(
                        out=dbg.rearrange("(c p) t -> c p t", p=128)[dc],
                        in_=a_cx[:, dc * TOWN:(dc + 1) * TOWN])
                    L.inc(i, qG)
                g.wait_ge(qG, L.at(qG))
        if stage < 8:
            _finish(nc, L, blk, qG, out, backing)
            return nc

        @blk.gpsimd
        def _(g):
            L.wait(g, cC, "ag_sc")
            sf = scflat.rearrange("(a t) -> a t", a=1)
            so = agsc_o.rearrange("(a t) -> a t", a=1)
            for r_ in range(8):
                i = g.dma_start(out=sf[0:1, r_ * ICORE:(r_ + 1) * ICORE],
                                in_=so[0:1, r_ * 832:r_ * 832 + ICORE])
                L.inc(i, qG)
                i = g.dma_start(out=sf[0:1, IFF + r_ * ICORE:IFF + (r_ + 1) * ICORE],
                                in_=so[0:1, r_ * 832 + ICORE:r_ * 832 + 2 * ICORE])
                L.inc(i, qG)
                i = g.dma_start(out=sf[0:1, 2 * IFF + r_ * 128:2 * IFF + (r_ + 1) * 128],
                                in_=so[0:1, r_ * 832 + 704:r_ * 832 + 832])
                L.inc(i, qG)
            L.set("scflat_done", qG)

        @blk.sync
        def _(sync):
            L.wait(sync, qG, "scflat_done")
            L.wait(sync, vE, "L2_done_v")
            i = sync.dma_start(out=sc22[:, :],
                               in_=scflat.rearrange("(q p) -> q p", q=52,
                                                    p=128)[0:22])
            L.inc(i, qS)
            L.set("sg22", qS)

        @blk.tensor
        def _(tensor):
            L.wait(tensor, qS, "sg22")
            i = tensor.transpose(ps[1].ap()[:, 0:22], sc22[:, :],
                                 c_idF[0:22, 0:22])
            L.inc(i, tE)
            L.set("sgT", tE)

        @blk.vector
        def _(vector):
            L.wait(vector, tE, "sgT")
            L.chain(vector, vE, vector.tensor_copy(c_sg[:, :],
                                                   ps[1].ap()[:, 0:22]))
            L.set("sgv", vE)

        @blk.sync
        def _(sync):
            L.wait(sync, vE, "sgv")
            i = sync.dma_start(
                out=sc22[:, :],
                in_=scflat.rearrange("(q p) -> q p", q=52, p=128)[22:44])
            L.inc(i, qS)
            L.set("su22", qS)

        @blk.tensor
        def _(tensor):
            L.wait(tensor, qS, "su22")
            i = tensor.transpose(ps[1].ap()[:, 0:22], sc22[:, :],
                                 c_idF[0:22, 0:22])
            L.inc(i, tE)
            L.set("suT", tE)

        @blk.vector
        def _(vector):
            L.wait(vector, tE, "suT")
            L.chain(vector, vE, vector.tensor_copy(c_su[:, :],
                                                   ps[1].ap()[:, 0:22]))
            L.set("suv", vE)

        @blk.sync
        def _(sync):
            L.wait(sync, vE, "suv")
            i = sync.dma_start(
                out=sc8r[:, :],
                in_=scflat.rearrange("(q p) -> q p", q=52, p=128)[44:52])
            L.inc(i, qS)
            L.set("sd8", qS)

        @blk.tensor
        def _(tensor):
            L.wait(tensor, qS, "sd8")
            i = tensor.transpose(ps[1].ap()[:, 0:8], sc8r[:, :], c_idF[0:8, 0:8])
            L.inc(i, tE)
            L.set("sdT", tE)

        @blk.vector
        def _(vector):
            L.wait(vector, tE, "sdT")
            L.chain(vector, vE, vector.tensor_copy(c_sd[:, :],
                                                   ps[1].ap()[:, 0:8]))
            L.chain(vector, vE, vector.tensor_scalar(c_sgc[:, :], c_sg[:, :],
                                                     c_s1r[:, 0:1], None,
                                                     ALU.mult))
            L.chain(vector, vE, vector.tensor_scalar(c_suc[:, :], c_su[:, :],
                                                     c_s1r[:, 0:1], None,
                                                     ALU.mult))
            L.wait(vector, qG, "m1_sent")
            L.chain(vector, vE, vector.memset(c_mx[:, :], 0.0))
            L.set("ffnsc_done", vE)

        def wtile_dmas(sync, agout, r, ic, which):
            g0 = 128 * ic
            runs = []
            while g0 < 128 * (ic + 1):
                s_ = g0 // ICORE
                l0 = g0 % ICORE
                w_ = min(ICORE - l0, 128 * (ic + 1) - g0)
                runs.append((s_, l0, g0 - 128 * ic, w_))
                g0 += w_
            for s_, l0, off, w_ in runs:
                i = sync.dma_start(
                    out=(wgt if which == "g" else wut)[r].rearrange(
                        "p (c i) -> p c i", c=NDC)[:, :, off:off + w_],
                    in_=agout.rearrange("(s c p i) -> s p c i", s=8, c=NDC,
                                        p=128)[s_][:, :, l0:l0 + w_])
                L.inc(i, qS)

        for ic in range(NIC):
            r = ic % 2

            @blk.sync
            def _(sync, ic=ic, r=r):
                if ic == 0:
                    sync.wait_ge(cC, L.mark["ag_wg"])
                    sync.wait_ge(cC, L.mark["ag_wu"])
                    L.wait(sync, vE, "L2_done_v")
                if ic >= 2:
                    L.wait(sync, tE, f"gu_mm{ic - 2}")
                wtile_dmas(sync, agwg_o, r, ic, "g")
                wtile_dmas(sync, agwu_o, r, ic, "u")
                L.set(f"gu_wl{ic}", qS)

            for tt in range(4):
                @blk.tensor
                def _(tensor, ic=ic, r=r, tt=tt):
                    if tt == 0:
                        L.wait(tensor, qS, f"gu_wl{ic}")
                        if ic == 0:
                            tensor.wait_ge(vE, L.mark["cx_done"])
                            tensor.wait_ge(vE, L.mark["ffnsc_done"])
                    if ic >= 1:
                        L.wait(tensor, vE, f"hrt{ic - 1}_{tt}")
                    for ec in range(NDC):
                        tensor.matmul(
                            ps[tt].ap()[:, 0:512],
                            wgt[r][:, ec * 128:ec * 128 + 128],
                            a_cx[:, ec * TOWN + 512 * tt:
                                 ec * TOWN + 512 * tt + 512],
                            start=(ec == 0), stop=(ec == NDC - 1))
                    for ec in range(NDC):
                        i = tensor.matmul(
                            ps[tt].ap()[:, 512:1024],
                            wut[r][:, ec * 128:ec * 128 + 128],
                            a_cx[:, ec * TOWN + 512 * tt:
                                 ec * TOWN + 512 * tt + 512],
                            start=(ec == 0), stop=(ec == NDC - 1))
                    L.inc(i, tE)
                    L.set(f"gu_mmt{ic}_{tt}", tE)
                    if tt == 3:
                        L.mark[f"gu_mm{ic}"] = L.at(tE)

            for hv in range(2):
                @blk.scalar
                def _(scalar, ic=ic, hv=hv):
                    for tt in (2 * hv, 2 * hv + 1):
                        L.wait(scalar, tE, f"gu_mmt{ic}_{tt}")
                        if tt >= 2:
                            L.wait(scalar, vE, f"hrt{ic}_{tt - 2}")
                        i = scalar.activation(
                            silq[:, :] if tt % 2 == 0 else silq2[:, :],
                            ps[tt].ap()[:, 0:512], AF.Silu,
                            scale=c_sgc[:, ic:ic + 1])
                        L.inc(i, sE)
                        L.set(f"sil{ic}_{tt}", sE)

                @blk.vector
                def _(vector, ic=ic, hv=hv):
                    for tt in (2 * hv, 2 * hv + 1):
                        L.wait(vector, sE, f"sil{ic}_{tt}")
                        i = vector.scalar_tensor_tensor(
                            a_hr[:, ic * TOWN + 512 * tt:
                                 ic * TOWN + 512 * tt + 512],
                            ps[tt].ap()[:, 512:1024], c_suc[:, ic:ic + 1],
                            silq[:, :] if tt % 2 == 0 else silq2[:, :],
                            ALU.mult, ALU.mult)
                        L.inc(i, vE)
                        L.set(f"hrt{ic}_{tt}", vE)

            @blk.vector
            def _(vector, ic=ic):
                L.chain(vector, vE, vector.reduce_max(
                    c_mxt[:, :], a_hr[:, ic * TOWN:(ic + 1) * TOWN], axis=AX,
                    apply_absolute_value=True))
                L.chain(vector, vE, vector.tensor_max(c_mx[:, :], c_mxt[:, :],
                                                      c_mx[:, :]))
                L.set(f"hr{ic}", vE)
        L.mark["hr_done"] = L.at(vE)

        @blk.scalar
        def _(scalar):
            L.wait(scalar, vE, f"hrt{NIC - 1}_3")
            pre6(scalar, 2)

        @blk.gpsimd
        def _(g):
            g.wait_ge(vE, L.mark["hr_done"])
            i = g.dma_start(out=agm2_i.rearrange("(p x) -> p x", x=1),
                            in_=c_mx[:, :])
            L.inc(i, qG)
            g.wait_ge(qG, L.at(qG))
            i = g.collective_compute("AllGather", ALU.bypass, replica_groups=RG,
                                     ins=[agm2_i.ap().opt()],
                                     outs=[agm2_o.ap().opt()])
            L.inc(i, cC)
            L.set("ag_m2", cC)

        @blk.sync
        def _(sync):
            L.wait(sync, cC, "ag_m2")
            i = sync.dma_start(out=c_mrow[0:1, :],
                               in_=agm2_o.rearrange("(a t) -> a t", a=1))
            L.inc(i, qS)
            L.set("m2_row", qS)

        @blk.vector
        def _(vector):
            L.wait(vector, qS, "m2_row")
            L.wait(vector, vE, "sig_done")
            L.chain(vector, vE, vector.reduce_max(c_sc[0:1, 8:9],
                                                  c_mrow[0:1, :], axis=AX))
            L.chain(vector, vE, vector.tensor_mul(c_sc[0:1, 10:11],
                                                  c_sc[0:1, 8:9],
                                                  c_sc[0:1, 9:10]))
            L.chain(vector, vE, vector.tensor_mul(c_sc[0:1, 11:12],
                                                  c_sc[0:1, 10:11],
                                                  c_r127[0:1, 0:1]))
            L.set("s2_done", vE)

        @blk.scalar
        def _(scalar):
            L.wait(scalar, vE, "s2_done")
            L.chain(scalar, sE, scalar.activation(c_sc[0:1, 12:13],
                                                  c_sc[0:1, 11:12], AF.Ln))
            L.chain(scalar, sE, scalar.activation(c_sc[0:1, 13:14],
                                                  c_sc[0:1, 12:13], AF.Exp,
                                                  scale=-1.0))
            L.set("inv2_done", sE)

        @blk.tensor
        def _(tensor):
            L.wait(tensor, sE, "inv2_done")
            tensor.matmul(ps[0].ap()[:, 0:1], c_1rf[0:1, :], c_sc[0:1, 13:14],
                          start=True, stop=True)
            i = tensor.matmul(ps[0].ap()[:, 1:2], c_1rf[0:1, :], c_sc[0:1, 11:12],
                              start=True, stop=True)
            L.inc(i, tE)
            L.set("rep2", tE)

        @blk.vector
        def _(vector):
            L.wait(vector, tE, "rep2")
            L.chain(vector, vE, vector.tensor_copy(c_inv2[:, :],
                                                   ps[0].ap()[:, 0:1]))
            L.chain(vector, vE, vector.tensor_copy(c_s2r[:, :],
                                                   ps[0].ap()[:, 1:2]))
            L.chain(vector, vE, vector.tensor_scalar(c_sdc[:, :], c_sd[:, :],
                                                     c_s2r[:, 0:1], None,
                                                     ALU.mult))
            L.set("rep2v", vE)

        for ic in range(NIC):
            @blk.vector
            def _(vector, ic=ic):
                if ic == 0:
                    L.wait(vector, vE, "rep2v")
                L.chain(vector, vE, vector.tensor_scalar(
                    vtmp[:, :], a_hr[:, ic * TOWN:(ic + 1) * TOWN],
                    c_inv2[:, 0:1], None, ALU.mult))
                L.chain(vector, vE, vector.tensor_copy(q8t[:, :], vtmp[:, :]))
                i = vector.tensor_copy(a_hr[:, ic * TOWN:(ic + 1) * TOWN],
                                       q8t[:, :])
                L.inc(i, vE)
                L.set(f"ch{ic}", vE)
        L.mark["ch_done"] = L.at(vE)
        if stage < 9:
            _finish(nc, L, blk, qG, out, backing)
            return nc

        for ec in range(NDC):
            r = ec % 2

            @blk.sync
            def _(sync, ec=ec, r=r):
                if ec == 0:
                    sync.wait_ge(cC, L.mark["ag_wd"])
                    L.wait(sync, vE, "L2_done_v")
                if ec >= 2:
                    L.wait(sync, tE, f"d_mm{ec - 2}_3")
                i = sync.dma_start(
                    out=wdt[r].rearrange("p (ic e) -> p ic e", ic=NIC),
                    in_=agwd_o.rearrange("(s ic p e) -> s p ic e", s=8, ic=NIC,
                                         p=128)[ec])
                L.inc(i, qS)
                L.set(f"d_wl{ec}", qS)

            for th in range(2):
                tts = (2 * th, 2 * th + 1)
                for tt in tts:
                    @blk.tensor
                    def _(tensor, ec=ec, tt=tt, r=r):
                        it = ec * 4 + tt
                        if tt == 0:
                            L.wait(tensor, qS, f"d_wl{ec}")
                        if it == 0:
                            tensor.wait_ge(vE, L.mark["ch_done"])
                        if it >= 2:
                            L.wait(tensor, sE, f"d_ev{it - 2}")
                        for ic in range(NIC):
                            i = tensor.matmul(
                                ps[it % 2].ap()[:, 0:512],
                                wdt[r][:, ic * 128:ic * 128 + 128],
                                a_hr[:, ic * TOWN + 512 * tt:
                                     ic * TOWN + 512 * tt + 512],
                                start=(ic == 0), stop=(ic == NIC - 1))
                        L.inc(i, tE)
                        L.set(f"d_mm{ec}_{tt}", tE)

                @blk.scalar
                def _(scalar, ec=ec, tts=tts):
                    for tt in tts:
                        it = ec * 4 + tt
                        L.wait(scalar, tE, f"d_mm{ec}_{tt}")
                        if it >= 2:
                            L.wait(scalar, vE, f"d_st{it - 2}")
                        dst = evt[:, :] if it % 2 == 0 else silq2[:, :]
                        i = scalar.activation(dst, ps[it % 2].ap()[:, 0:512],
                                              AF.Identity,
                                              scale=c_sdc[:, ec:ec + 1])
                        L.inc(i, sE)
                        L.set(f"d_ev{it}", sE)

                for tt in tts:
                    @blk.sync
                    def _(sync, ec=ec, tt=tt):
                        it = ec * 4 + tt
                        if it >= 2:
                            L.wait(sync, vE, f"d_st{it - 2}")
                        i = sync.dma_start(
                            out=xno[it % 2][:, :],
                            in_=xnew_d.rearrange("(c p) t -> c p t", p=128)[ec][
                                :, 512 * tt:512 * tt + 512])
                        L.inc(i, qS)
                        L.set(f"d_xl{it}", qS)

                    @blk.vector
                    def _(vector, ec=ec, tt=tt):
                        it = ec * 4 + tt
                        L.wait(vector, sE, f"d_ev{it}")
                        L.wait(vector, qS, f"d_xl{it}")
                        if it >= 2:
                            L.wait(vector, qG, f"d_os{it - 2}")
                        src_ = evt[:, :] if it % 2 == 0 else silq2[:, :]
                        i = vector.scalar_tensor_tensor(
                            outt[it % 2][:, :], src_, c_gb[:, ec:ec + 1],
                            xno[it % 2][:, :], ALU.mult, ALU.add)
                        L.inc(i, vE)
                        L.set(f"d_st{it}", vE)

                    @blk.gpsimd
                    def _(g, ec=ec, tt=tt):
                        it = ec * 4 + tt
                        L.wait(g, vE, f"d_st{it}")
                        i = g.dma_start(
                            out=out.rearrange("(c p) t -> c p t", p=128)[ec][
                                :, 512 * tt:512 * tt + 512],
                            in_=outt[it % 2][:, :])
                        L.inc(i, qG)
                        L.set(f"d_os{it}", qG)

        @blk.gpsimd
        def _(g):
            g.wait_ge(qG, L.at(qG))

        return nc



def _finish(nc, L, blk, qG, out, backing):
    @blk.gpsimd
    def _(g):
        av = backing.ap().bitcast(F32)
        for dc in range(NDC):
            i = g.dma_start(out=out.rearrange("(c p) t -> c p t", p=128)[dc],
                            in_=av[:, 0:TOWN])
            L.inc(i, qG)
        g.wait_ge(qG, L.at(qG))


def _stage_host(inputs, core):
    b, s = core // 2, core % 2
    x = np.asarray(inputs["L0_attn_x"])
    g0 = 2048 * s
    fr = np.zeros((FR, E), np.float32)
    lo, hi = g0 - 512, g0 + 2560
    cl, ch = max(lo, 0), min(hi, S)
    fr[cl - lo:ch - lo] = x[b, cl:ch]
    d = {
        "xT": np.ascontiguousarray(fr.T),
        "wqT": np.ascontiguousarray(np.asarray(inputs["L0_attn_wq"]).T),
        "wkT": np.ascontiguousarray(np.asarray(inputs["L0_attn_wk"]).T),
        "woT": np.ascontiguousarray(np.asarray(inputs["L0_attn_wo"]).T),
        "cTd": np.ascontiguousarray(
            np.asarray(inputs["L0_c"]).T.reshape(NDC, 128, B)
            .transpose(1, 0, 2).reshape(128, NDC * B)),
        "clipx": np.asarray(inputs["L0_ffn_clip_x"], np.float32).reshape(1, 1),
        "clipd": np.asarray(inputs["L0_ffn_clip_down"], np.float32).reshape(1, 1),
        "wmods": np.ascontiguousarray(
            np.asarray(inputs["L0_wmod"])[:, 768 * core:768 * (core + 1)]),
    }
    wv = np.asarray(inputs["L0_attn_wv"])
    wvT = np.zeros((E, DA), np.float32)
    bvA = np.zeros((1, DA), np.float32)
    bv = np.asarray(inputs["L0_attn_bv"])
    for h in range(H):
        wvT[:, 65 * h:65 * h + 64] = wv[64 * h:64 * h + 64, :].T
        bvA[0, 65 * h:65 * h + 64] = bv[64 * h:64 * h + 64]
        bvA[0, 65 * h + 64] = 1.0
    d["wvTa"] = wvT
    d["bva"] = bvA
    pv = np.full((128, DA), -1.0, np.float32)
    pv[:, 64::65] = 1.0
    d["padvd"] = pv.astype(BF)
    for nm, src in (("bq2", "L0_attn_bq"), ("bk2", "L0_attn_bk"),
                    ("bo2", "L0_attn_bo")):
        d[nm] = np.ascontiguousarray(
            np.asarray(inputs[src]).reshape(NDC, 128).T)
    wg = np.asarray(inputs["L0_ffn_w_gate"])
    wu = np.asarray(inputs["L0_ffn_w_up"])
    wd = np.asarray(inputs["L0_ffn_w_down"])
    d["wgN"] = np.ascontiguousarray(wg[ICORE * core:ICORE * (core + 1)])
    d["wuN"] = np.ascontiguousarray(wu[ICORE * core:ICORE * (core + 1)])
    d["wgTs"] = np.ascontiguousarray(wg.T[:, ICORE * core:ICORE * (core + 1)])
    d["wuTs"] = np.ascontiguousarray(wu.T[:, ICORE * core:ICORE * (core + 1)])
    d["wdN"] = np.ascontiguousarray(wd[128 * core:128 * (core + 1)])
    pk = 0 if s == 0 else 2560
    ptc = 0 if s == 0 else 20
    woff = [1536, 1536, 1536, 1536]
    if s == 0:
        woff[0] = 0
    else:
        woff[3] = 1024
    d["padsi"] = np.array([[pk, ptc * DA, b, ptc] + woff], np.uint32)
    p = np.arange(128)[:, None]
    i_ = np.arange(512)[None, :]
    m0 = np.ones((128, 2048), np.float32)
    m3 = np.ones((128, 2048), np.float32)
    if s == 1:
        for jc in range(4):
            m0[:, 512 * jc:512 * jc + 512] = (i_ <= 128 * jc + p)
    if s == 0:
        for k in range(4):
            m3[:, 512 * k:512 * k + 512] = (i_ >= 128 * k + p)
    d["m0d"] = m0.astype(BF)
    d["m3d"] = m3.astype(BF)
    q = np.arange(128)[None, :]
    d["t1d"] = (q <= p).astype(BF)
    d["t2d"] = (q >= p).astype(BF)
    d["idnB"] = np.eye(128).astype(BF)
    d["idnF"] = np.eye(128).astype(np.float32)
    d["lnod"] = np.full((128, 128), 1.0 / 1024.0).astype(BF)
    d["onesRFd"] = np.ones((1, 128), np.float32)
    return d


def kernel(**inputs):
    nc = build()
    in_maps = [_stage_host(inputs, c) for c in range(8)]
    res = run_bass_kernel_spmd(nc, in_maps, core_ids=list(range(8)))
    outp = np.zeros((B, S, E), np.float32)
    for c in range(8):
        b, s = c // 2, c % 2
        outp[b, 2048 * s:2048 * (s + 1), :] = res.results[c]["out"].T
    return outp
